# version 12
# speedup vs baseline: 1.0212x; 1.0083x over previous
import numpy as np
import ml_dtypes
from contextlib import ExitStack
import concourse.bass as bass
import concourse.mybir as mybir
from concourse.bass_utils import run_bass_kernel_spmd

F32 = mybir.dt.float32
BF16 = mybir.dt.bfloat16
AF = mybir.ActivationFunctionType
ALU = mybir.AluOpType

D = 1024
H = 16
HD = 64
WIN = 512
TC = 2048
NT = TC // 128
HALO = 512
FR = HALO + TC
ROPE_BASE = 10000.0
BF = ml_dtypes.bfloat16


def host_consts(qt):
    c = {}
    c["ident"] = np.eye(128, dtype=BF)
    R = np.zeros((128, 128), np.float32)
    for a in range(2):
        for f in range(0, HD, 2):
            R[a * HD + f, a * HD + f + 1] = -1.0
            R[a * HD + f + 1, a * HD + f] = 1.0
    c["r2ht"] = np.ascontiguousarray(R.T).astype(BF)
    pos0 = qt * TC - HALO
    pos = np.arange(pos0, pos0 + FR, dtype=np.float64)
    pos = np.maximum(pos, 0.0)
    inv = 1.0 / ROPE_BASE ** (np.arange(0, HD, 2, dtype=np.float64) / HD)
    freqs = pos[None, :] * inv[:, None]
    cos = np.repeat(np.cos(freqs), 2, axis=0)
    sin = np.repeat(np.sin(freqs), 2, axis=0)
    c["cosfm"] = np.concatenate([cos, cos], axis=0).astype(BF)
    c["sinfm"] = np.concatenate([sin, sin], axis=0).astype(BF)

    def band(w):
        ccv = np.arange(128)[:, None]
        rv = np.arange(128)[None, :]
        dj = 128 * w + ccv - rv
        return ((dj > 0) & (dj <= WIN)).astype(np.float32)
    c["maskB"] = np.concatenate([band(0), band(4)], axis=1).astype(BF)
    mA = np.zeros((128, 4, 5, 128), np.float32)
    for t in range(4):
        for w in range(5):
            m = band(w)
            if qt == 0 and (t + w) <= 3:
                m = np.zeros_like(m)
            mA[:, t, w, :] = m
    c["maskA"] = np.ascontiguousarray(mA.reshape(128, 4 * 5 * 128)).astype(BF)
    bo = np.zeros((128, 16, 32), np.float32)
    for t in range(16):
        bo[0:64, t, 2 * t] = 1.0
        bo[64:128, t, 2 * t + 1] = 1.0
    c["bucket_ones"] = np.ascontiguousarray(bo.reshape(128, 512)).astype(BF)
    c["ltriT"] = np.triu(np.ones((32, 32), np.float32), 1).astype(BF)
    pm = (np.arange(4) < qt).astype(np.float32)
    c["pmT"] = pm.reshape(4, 1).astype(BF)
    c["pmR"] = pm.reshape(1, 4).astype(BF)
    return c


class P:
    def __init__(self):
        self.sem = {}
        self.cnt = {}

    def add(self, name, handle):
        self.sem[name] = handle
        self.cnt[name] = 0

    def inc(self, name, instr, amt=1):
        instr.then_inc(self.sem[name], amt)
        self.cnt[name] += amt
        return self.cnt[name]

    def val(self, name):
        return self.cnt[name]

    def wait(self, eng, name, v=None):
        eng.wait_ge(self.sem[name], self.cnt[name] if v is None else v)


def build_nc(n_sub=4, dbg_dump=False, dbg_half0=False, dbg_ffn=False, b_stop=None):
    nc = bass.Bass("TRN2", target_bir_lowering=False, num_devices=8)

    xh = nc.dram_tensor("xh", [FR, D], F32, kind="ExternalInput").ap()
    w_qkv = nc.dram_tensor("w_qkv", [D, 3 * D], F32, kind="ExternalInput").ap()
    w_out = nc.dram_tensor("w_out", [D, D], F32, kind="ExternalInput").ap()
    cons = {}
    for nm, shape in [("ident", [128, 128]), ("r2ht", [128, 128]),
                      ("cosfm", [128, FR]), ("sinfm", [128, FR]),
                      ("maskB", [128, 256]), ("maskA", [128, 4 * 5 * 128]),
                      ("bucket_ones", [128, 512]), ("ltriT", [32, 32]), ("pmT", [4, 1]),
                      ("pmR", [1, 4])]:
        cons[nm] = nc.dram_tensor(nm, shape, BF16, kind="ExternalInput").ap()
    wqkv_b = nc.dram_tensor("wqkv_b", [3 * D, D], F32, kind="ExternalInput").ap()
    wout_b = nc.dram_tensor("wout_b", [D, D], F32, kind="ExternalInput").ap()
    ffn_in = {}
    for nm, shape in [("slot_init", [64, D]), ("w1", [2 * D, 512]), ("b1", [1, 512]),
                      ("w2", [512, 64]), ("b2", [1, 64]), ("tau", [1, 1]),
                      ("Wih", [3 * D, D]), ("Whh", [3 * D, D]), ("bih", [1, 3 * D]),
                      ("bhh", [1, 3 * D]), ("wq", [D, 256]), ("bq", [1, 256]),
                      ("wk", [D, 256]), ("bk", [1, 256]), ("wv", [D, D]), ("bv", [1, D]),
                      ("ln_g", [1, D]), ("ln_b", [1, D]), ("wvp", [D, D]), ("bvp", [1, D]),
                      ("wo", [D, D]), ("bo", [1, D])]:
        ffn_in[nm] = nc.dram_tensor("f_" + nm, shape, F32, kind="ExternalInput").ap()
    out = nc.dram_tensor("out", [TC, D], F32, kind="ExternalOutput").ap()
    xres = nc.dram_tensor("xres", [TC, D], F32).ap()
    xlnsp = nc.dram_tensor("xlnsp", [TC, D], BF16).ap()
    cc_in = nc.dram_tensor("cc_in", [64, 1028], F32)
    cc_out = nc.dram_tensor("cc_out", [64, 1028], F32)
    qsp = nc.dram_tensor("qsp", [8, 128, 2048], BF16)
    ag2_in = nc.dram_tensor("ag2_in", [132, 512], F32)
    ag2_out = nc.dram_tensor("ag2_out", [528, 512], F32)
    dbg = {}
    if dbg_ffn:
        for nm, shape in [("e_alpha", [128, 16 * 64]), ("e_sitot", [64, 1028]),
                          ("e_gh0", [64, 3072]), ("e_snew", [64, 1024]),
                          ("e_sv", [64, 1024]), ("e_c1", [128, 4]),
                          ("e_hfm", [128, 4 * 2048]), ("e_xlf", [128, 8 * 2048]),
                          ("e_xres", [256, 1024]),
                          ("e_y", [64, 1024]), ("e_alphaT", [64, 2048])]:
            dbg[nm] = nc.dram_tensor(nm, shape, F32, kind="ExternalOutput").ap()
    if dbg_dump:
        for nm, shape in [("d_xlnf", [128, 8 * 1536]), ("d_qfm", [128, 8 * 1024]),
                          ("d_kfm", [128, 8 * 1536]), ("d_vaug", [128, 12 * H * 65]),
                          ("d_probs", [128, 8 * 640]), ("d_aofm", [128, 8 * 1024]),
                          ("d_aotm", [128, 8 * 64])]:
            dbg[nm] = nc.dram_tensor(nm, shape, F32, kind="ExternalOutput").ap()

    xh_t = xh.rearrange("(t p) d -> t p d", p=128)
    xlnsp_t = xlnsp.rearrange("(t p) d -> t p d", p=128)
    xres_t = xres.rearrange("(t p) d -> t p d", p=128)
    out_t = out.rearrange("(t p) d -> t p d", p=128)

    p = P()

    with ExitStack() as _es:
        def SB(name, shape, dt):
            return _es.enter_context(nc.sbuf_tensor(name, shape, dt))
        c_ident = SB("c_ident", [128, 128], BF16)
        c_r2ht = SB("c_r2ht", [128, 128], BF16)
        c_cos = SB("c_cos", [128, FR], BF16)
        c_sin = SB("c_sin", [128, FR], BF16)
        c_maskB = SB("c_maskB", [128, 256], BF16)
        c_maskA = SB("c_maskA", [128, 4 * 5 * 128], BF16)
        cst = SB("cst", [128, 8], F32)
        stats = SB("stats", [128, 96], F32)
        xt = SB("xt", [128, 2 * 1024], F32)
        xlt = SB("xlt", [128, 2 * 1024], BF16)
        rt = SB("rt", [128, 2 * 1024], F32)
        onesr = SB("onesr", [1, 128], BF16)
        c_bones = SB("c_bones", [128, 512], BF16)
        c_ltriT = SB("c_ltriT", [32, 32], BF16)
        c_pmT = SB("c_pmT", [4, 1], BF16)
        c_pmR = SB("c_pmR", [1, 4], BF16)
        es_a = ExitStack()
        def SBA(name, shape, dt):
            return es_a.enter_context(nc.sbuf_tensor(name, shape, dt))
        vw = SBA("vw", [128, 8 * D], BF16)
        wqk = SBA("wqk", [128, 2 * 2048], BF16)
        wo_b = SBA("wo_b", [128, 8 * D], BF16)
        xlnf = SBA("xlnf", [128, 8 * 1536], BF16)
        qfm = SBA("qfm", [128, 8 * 1024], BF16)
        kfm = SBA("kfm", [128, 8 * 1536], BF16)
        vaug = SBA("vaug", [128, 12 * H * 65], BF16)
        probs = SBA("probs", [128, 8 * 640], BF16)
        aofm = SBA("aofm", [128, 8 * 1024], BF16)
        aotm = SBA("aotm", [128, 8 * 64], BF16)
        qkbf = SBA("qkbf", [128, 2560], BF16)
        ubuf = SBA("ubuf", [128, 2560], BF16)
        vbuf = SBA("vbuf", [128, 1536], BF16)
        PP = _es.enter_context(nc.psum_tensor("PP", [128, 4096], F32))
        hLD = _es.enter_context(nc.semaphore("sLD"))
        hW = _es.enter_context(nc.semaphore("sW"))
        hPE = _es.enter_context(nc.semaphore("sPE"))
        hVE = _es.enter_context(nc.semaphore("sVE"))
        hSE = _es.enter_context(nc.semaphore("sSE"))
        hPO = _es.enter_context(nc.semaphore("sPO"))
        hST = _es.enter_context(nc.semaphore("sST"))
        hC1 = _es.enter_context(nc.semaphore("sC1"))
        hC2 = _es.enter_context(nc.semaphore("sC2"))
        hC3 = _es.enter_context(nc.semaphore("sC3"))
        hLS = _es.enter_context(nc.semaphore("sLS"))
        block = _es.enter_context(nc.Block())
        for nm, h in [("LD", hLD), ("W", hW), ("PE", hPE), ("VE", hVE),
                      ("SE", hSE), ("PO", hPO), ("ST", hST),
                      ("C1", hC1), ("C2", hC2), ("C3", hC3), ("LS", hLS)]:
            p.add(nm, h)

        def _pre_sync(s):
            for nm, dst in [("ident", c_ident), ("r2ht", c_r2ht), ("cosfm", c_cos),
                            ("sinfm", c_sin), ("maskB", c_maskB), ("maskA", c_maskA),
                            ("bucket_ones", c_bones), ("ltriT", c_ltriT), ("pmT", c_pmT),
                            ("pmR", c_pmR)]:
                p.inc("LS", s.dma_start(out=dst[:], in_=cons[nm]), 16)
        block.sync(_pre_sync)

        def _pre_g(g):
            g.memset(cst[:, 0:1], 1e-5)
            g.memset(cst[:, 1:2], -20.0)
            g.memset(cst[:, 2:3], 0.0)
            p.inc("PO", g.memset(onesr[:], 1.0))
            p.inc("LD", g.dma_start(
                out=vw.ap().rearrange("p (c j) -> p c j", c=8),
                in_=w_qkv.rearrange("(c p) j -> p c j", p=128)[:, :, 2 * D:3 * D]), 16)
            p.inc("LD", g.dma_start(
                out=wo_b.ap().rearrange("p (c j) -> p c j", c=8),
                in_=w_out.rearrange("(c p) j -> p c j", p=128)), 16)
        block.gpsimd(_pre_g)
        ld_pre = p.val("LD")
        ls_pre = p.val("LS")
        po_pre = p.val("PO")

        for bb in (block.tensor, block.vector, block.scalar, block.gpsimd, block.sync):
            def _init(e, lp=ld_pre, ls=ls_pre, pp_=po_pre):
                p.wait(e, "LD", lp)
                p.wait(e, "LS", ls)
                p.wait(e, "PO", pp_)
            bb(_init)

        def emit_ln(src_ap, dst_ap, sslot, extra_se_wait=None, ld_guard=None,
                    rows=128, extra_ve_wait=None):
            so = 8 * sslot
            bo = 32 + 12 * sslot
            R = rows

            def _v1(v):
                if ld_guard is not None:
                    p.wait(v, "LS", ld_guard)
                if extra_ve_wait is not None:
                    p.wait(v, "VE", extra_ve_wait)
                v.bn_stats(stats[0:R, bo:bo + 6], src_ap[:, 0:512])
                p.inc("VE", v.bn_stats(stats[0:R, bo + 6:bo + 12], src_ap[:, 512:1024]))
                p.wait(v, "VE")
                p.inc("VE", v.bn_aggr(stats[0:R, so:so + 2], stats[0:R, bo:bo + 12]))
            block.vector(_v1)
            ve_mv = p.val("VE")

            def _s1(sc):
                p.wait(sc, "VE", ve_mv)
                p.inc("SE", sc.activation(stats[0:R, so + 4:so + 5], stats[0:R, so + 1:so + 2],
                                          AF.Ln, bias=cst[0:R, 0:1], scale=1.0))
                p.wait(sc, "SE")
                p.inc("SE", sc.activation(stats[0:R, so + 2:so + 3], stats[0:R, so + 4:so + 5],
                                          AF.Exp, bias=cst[0:R, 2:3], scale=-0.5))
            block.scalar(_s1)
            se_rstd = p.val("SE")

            def _v2(v):
                p.wait(v, "SE", se_rstd)
                p.inc("VE", v.scalar_tensor_tensor(
                    out=stats[0:R, so + 3:so + 4], in0=stats[0:R, so:so + 1], scalar=-1.0,
                    in1=stats[0:R, so + 2:so + 3], op0=ALU.mult, op1=ALU.mult))
            block.vector(_v2)
            ve_nm = p.val("VE")

            def _s2(sc):
                p.wait(sc, "VE", ve_nm)
                if extra_se_wait is not None:
                    nm, vv = extra_se_wait
                    p.wait(sc, nm, vv)
                p.inc("SE", sc.activation(dst_ap, src_ap, AF.Identity,
                                          bias=stats[0:R, so + 3:so + 4],
                                          scale=stats[0:R, so + 2:so + 3]))
            block.scalar(_s2)
            return p.val("SE")

        res_ve = {}
        st_cnt = {}
        ln_se_g = {}

        def emit_attn_a():
            for half in range([2, 1][bool(dbg_half0)]):
                base_fr = 8 * half
                ve_half = p.val("VE")

                def _g_vone(g, w=ve_half):
                    p.wait(g, "VE", w)
                    p.inc("PO", g.memset(vaug[:], 1.0))
                block.gpsimd(_g_vone)
                po_vone = p.val("PO")

                evac_a = {}
                evac_b = {}
                t1_pe = {}
                t1b_pe = {}
                for fi in range(12):
                    fr = base_fr + fi
                    buf = fi % 2
                    tb = 0 if buf == 0 else 2048
                    xsrc = xt[:, 1024 * buf:1024 * buf + 1024]
                    xdst = xlt[:, 1024 * buf:1024 * buf + 1024]
                    g_se = ln_se_g.get((half, fi - 2), 0)
                    g_ve = ve_half if fi < 2 else res_ve.get(8 * half + fi - 10, 0)

                    def _ld(s, fr=fr, xsrc=xsrc, g_se=g_se, g_ve=g_ve):
                        p.wait(s, "SE", g_se)
                        p.wait(s, "VE", g_ve)
                        p.inc("LS", s.dma_start(out=xsrc, in_=xh_t[fr]), 16)
                    block.sync(_ld)
                    ln_se_g[(half, fi)] = emit_ln(
                        xsrc, xdst, sslot=buf,
                        extra_se_wait=("PE", t1_pe.get(fi - 2, 0)),
                        ld_guard=p.val("LS"))

                    def _t1(t, xdst=xdst, tb=tb, se=ln_se_g[(half, fi)],
                            g=evac_a.get(fi - 2, 0)):
                        p.wait(t, "SE", se)
                        p.wait(t, "VE", g)
                        for c in range(8):
                            ins = t.matmul(PP[:, tb + 128 * c: tb + 128 * c + 128],
                                           xdst[:, 128 * c:128 * c + 128], c_ident[:],
                                           start=True, stop=True)
                        p.inc("PE", ins)
                    block.tensor(_t1)
                    t1_pe[fi] = p.val("PE")

                    def _v3a(v, tb=tb, fi=fi, pe=t1_pe[fi]):
                        p.wait(v, "PE", pe)
                        dst = xlnf.ap().rearrange("p (c j) -> p c j", c=8)[:, :, 128 * fi:128 * fi + 128]
                        p.inc("VE", v.tensor_copy(
                            dst, PP[:, tb:tb + 1024].rearrange("p (c j) -> p c j", c=8)))
                    block.vector(_v3a)
                    evac_a[fi] = p.val("VE")

                    def _t1b(t, fi=fi, tb=tb, g1=evac_a[fi], g2=evac_b.get(fi - 2, 0)):
                        p.wait(t, "VE", max(g1, g2))
                        for c in range(8):
                            for nn in range(2):
                                ins = t.matmul(
                                    PP[:, tb + 1024 + 512 * nn: tb + 1536 + 512 * nn],
                                    xlnf[:, 1536 * c + 128 * fi: 1536 * c + 128 * fi + 128],
                                    vw[:, 1024 * c + 512 * nn: 1024 * c + 512 * nn + 512],
                                    start=(c == 0), stop=(c == 7))
                        p.inc("PE", ins)
                    block.tensor(_t1b)
                    t1b_pe[fi] = p.val("PE")

                    def _v3b(v, tb=tb, fi=fi, pe=t1b_pe[fi], pv=po_vone):
                        p.wait(v, "PE", pe)
                        p.wait(v, "PO", pv)
                        vdst = vaug.ap().rearrange("p (f h e) -> p f h e", f=12, h=H)[:, fi, :, 0:64]
                        p.inc("VE", v.tensor_copy(
                            vdst, PP[:, tb + 1024:tb + 2048].rearrange("p (h e) -> p h e", h=H)))
                    block.vector(_v3b)
                    evac_b[fi] = p.val("VE")

                ve_a1 = p.val("VE")

                evq_se = {}
                evk_se = {}
                addq_ve = {}
                rot_free_ve = {}
                for pair in range(8):
                    buf = pair % 2
                    wbuf = wqk[:, 2048 * buf:2048 * buf + 2048]

                    def _gw(g, pair=pair, wbuf=wbuf, se_g=evk_se.get(pair - 2, 0)):
                        p.wait(g, "SE", se_g)
                        src_q = w_qkv.rearrange("(c p) j -> p c j", p=128)[:, :, 128 * pair:128 * pair + 128]
                        src_k = w_qkv.rearrange("(c p) j -> p c j", p=128)[:, :, D + 128 * pair:D + 128 * pair + 128]
                        p.inc("W", g.dma_start(out=wbuf.rearrange("p (c j) -> p c j", c=8)[:, :, 0:128], in_=src_q), 16)
                        p.inc("W", g.dma_start(out=wbuf.rearrange("p (c j) -> p c j", c=8)[:, :, 128:256], in_=src_k), 16)
                    block.gpsimd(_gw)
                    w_ld = p.val("W")

                    def _tq(t, wbuf=wbuf, w_ld=w_ld, g1=ve_a1, g2=evq_se.get(pair - 1, 0)):
                        p.wait(t, "W", w_ld)
                        p.wait(t, "VE", g1)
                        p.wait(t, "SE", g2)
                        for c in range(8):
                            for nn in range(2):
                                ins = t.matmul(PP[:, 512 * nn: 512 * nn + 512],
                                               wbuf[:, 256 * c:256 * c + 128],
                                               xlnf[:, 1536 * c + 512 + 512 * nn: 1536 * c + 1024 + 512 * nn],
                                               start=(c == 0), stop=(c == 7))
                        p.inc("PE", ins)
                    block.tensor(_tq)
                    pe_q = p.val("PE")

                    def _sq(sc, pe_q=pe_q):
                        p.wait(sc, "PE", pe_q)
                        p.inc("SE", sc.activation(qkbf[:, 0:1024], PP[:, 0:1024],
                                                  AF.Identity, bias=cst[:, 2:3], scale=1.0))
                    block.scalar(_sq)
                    evq_se[pair] = p.val("SE")

                    def _tk(t, wbuf=wbuf, g=evk_se.get(pair - 1, 0)):
                        p.wait(t, "SE", g)
                        for c in range(8):
                            for nn in range(3):
                                ins = t.matmul(PP[:, 1024 + 512 * nn: 1536 + 512 * nn],
                                               wbuf[:, 256 * c + 128:256 * c + 256],
                                               xlnf[:, 1536 * c + 512 * nn: 1536 * c + 512 * nn + 512],
                                               start=(c == 0), stop=(c == 7))
                        p.inc("PE", ins)
                    block.tensor(_tk)
                    pe_k = p.val("PE")

                    def _sk(sc, pe_k=pe_k):
                        p.wait(sc, "PE", pe_k)
                        p.inc("SE", sc.activation(qkbf[:, 1024:2560], PP[:, 1024:2560],
                                                  AF.Identity, bias=cst[:, 2:3], scale=1.0))
                    block.scalar(_sk)
                    evk_se[pair] = p.val("SE")

                    def _trq(t, g1=evq_se[pair], g2=rot_free_ve.get(pair - 1, 0)):
                        p.wait(t, "SE", g1)
                        p.wait(t, "VE", g2)
                        for nn in range(2):
                            ins = t.matmul(PP[:, 2560 + 512 * nn:3072 + 512 * nn],
                                           c_r2ht[:], qkbf[:, 512 * nn:512 * nn + 512],
                                           start=True, stop=True)
                        p.inc("PE", ins)
                    block.tensor(_trq)
                    pe_rq = p.val("PE")

                    qc0 = HALO + 1024 * half
                    kc0 = 1024 * half

                    def _vq(v, pair=pair, qc0=qc0, g1=evq_se[pair], g2=pe_rq):
                        p.wait(v, "SE", g1)
                        p.inc("VE", v.tensor_tensor(ubuf[:, 0:1024], qkbf[:, 0:1024],
                                                    c_cos[:, qc0:qc0 + 1024], op=ALU.mult))
                        p.wait(v, "PE", g2)
                        p.inc("VE", v.tensor_tensor(vbuf[:, 0:1024], PP[:, 2560:3584],
                                                    c_sin[:, qc0:qc0 + 1024], op=ALU.mult))
                        p.wait(v, "VE")
                        p.inc("VE", v.tensor_tensor(qfm[:, 1024 * pair:1024 * pair + 1024],
                                                    ubuf[:, 0:1024], vbuf[:, 0:1024], op=ALU.add))
                    block.vector(_vq)
                    addq_ve[pair] = p.val("VE")

                    def _trk(t, g1=evk_se[pair], g2=addq_ve[pair]):
                        p.wait(t, "SE", g1)
                        p.wait(t, "VE", g2)
                        for nn in range(3):
                            ins = t.matmul(PP[:, 2560 + 512 * nn:3072 + 512 * nn],
                                           c_r2ht[:], qkbf[:, 1024 + 512 * nn:1536 + 512 * nn],
                                           start=True, stop=True)
                        p.inc("PE", ins)
                    block.tensor(_trk)
                    pe_rk = p.val("PE")

                    def _vk(v, pair=pair, kc0=kc0, g1=pe_rk):
                        p.inc("VE", v.tensor_tensor(ubuf[:, 1024:2560], qkbf[:, 1024:2560],
                                                    c_cos[:, kc0:kc0 + 1536], op=ALU.mult))
                        p.wait(v, "PE", g1)
                        p.inc("VE", v.tensor_tensor(vbuf[:, 0:1536], PP[:, 2560:4096],
                                                    c_sin[:, kc0:kc0 + 1536], op=ALU.mult))
                        p.wait(v, "VE")
                        p.inc("VE", v.tensor_tensor(kfm[:, 1536 * pair:1536 * pair + 1536],
                                                    ubuf[:, 1024:2560], vbuf[:, 0:1536], op=ALU.add))
                    block.vector(_vk)
                    rot_free_ve[pair] = p.val("VE")

                ve_a2 = p.val("VE")

                pending_tail = [None]

                def emit_tail(hh):
                    pair, a = hh // 2, hh % 2
                    rb = 64 * a
                    pe_av = p.val("PE")

                    def _v6(v, pe_av=pe_av):
                        p.wait(v, "PE", pe_av)
                        zv = PP[:, 3072:4096].rearrange("p (t e) -> p t e", e=128)[:, :, 64:65]
                        p.inc("VE", v.reciprocal(stats[:, 16:24], zv.rearrange("p t e -> p (t e)")))
                        p.wait(v, "VE")
                        att = PP[:, 3072:4096].rearrange("p (t e) -> p t e", e=128)[:, :, 0:64]
                        rz = stats[:, 16:24].unsqueeze(2).broadcast_to([128, 8, 64])
                        p.inc("VE", v.tensor_tensor(
                            aotm.ap().rearrange("p (t e) -> p t e", e=64), att, rz, op=ALU.mult))
                    block.vector(_v6)
                    ve_norm = p.val("VE")

                    def _t6(t, rb=rb, g=ve_norm):
                        p.wait(t, "VE", g)
                        for tt in range(8):
                            ins = t.matmul(PP[rb:rb + 64, 3072 + 128 * tt:3072 + 128 * tt + 128],
                                           aotm[:, 64 * tt:64 * tt + 64], c_ident[:],
                                           start=True, stop=True)
                        p.inc("PE", ins)
                    block.tensor(_t6)
                    pe_tr = p.val("PE")

                    def _v7(v, pair=pair, rb=rb, g=pe_tr):
                        p.wait(v, "PE", g)
                        p.inc("VE", v.tensor_copy(aofm[rb:rb + 64, 1024 * pair:1024 * pair + 1024],
                                                  PP[rb:rb + 64, 3072:4096]))
                    block.vector(_v7)
                    return p.val("VE")

                tail_ve = 0
                for hh in range(H):
                    pair, a = hh // 2, hh % 2
                    rb = 64 * a
                    first_grp = True
                    for tlist in ([0, 1, 2], [3, 4, 5], [6, 7]):
                        pe_before = p.val("PE")
                        se_guard = p.val("SE")

                        def _t4(t, pair=pair, rb=rb, tlist=tlist,
                                g1=ve_a2, g2=se_guard):
                            p.wait(t, "VE", g1)
                            p.wait(t, "SE", g2)
                            for kc in range(tlist[0], tlist[-1] + 5):
                                for tt in tlist:
                                    w = kc - tt
                                    if w < 0 or w > 4:
                                        continue
                                    slot = tt % 3
                                    ins = t.matmul(
                                        PP[:, 1024 * slot + 128 * w: 1024 * slot + 128 * w + 128],
                                        kfm[rb:rb + 64, 1536 * pair + 128 * kc: 1536 * pair + 128 * kc + 128],
                                        qfm[rb:rb + 64, 1024 * pair + 128 * tt: 1024 * pair + 128 * tt + 128],
                                        start=True, stop=True)
                                    if w == 4:
                                        p.inc("PE", ins)
                        block.tensor(_t4)

                        if first_grp and pending_tail[0] is not None:
                            tail_ve = emit_tail(pending_tail[0])
                            pending_tail[0] = None
                        first_grp = False

                        def _s4(sc, tlist=tlist, pe_base=pe_before):
                            for i, tt in enumerate(tlist):
                                p.wait(sc, "PE", pe_base + i + 1)
                                slot = tt % 3
                                p.inc("SE", sc.activation(
                                    probs[:, 640 * tt:640 * tt + 640],
                                    PP[:, 1024 * slot:1024 * slot + 640],
                                    AF.Exp, bias=cst[:, 1:2], scale=0.125))
                        block.scalar(_s4)
                        se_base = p.val("SE") - len(tlist)

                        def _v5(v, tlist=tlist, se_base=se_base, half=half):
                            for i, tt in enumerate(tlist):
                                p.wait(v, "SE", se_base + i + 1)
                                if half == 0 and tt < 4:
                                    p.inc("VE", v.tensor_tensor(
                                        probs[:, 640 * tt:640 * tt + 640],
                                        probs[:, 640 * tt:640 * tt + 640],
                                        c_maskA[:, 640 * tt:640 * tt + 640], op=ALU.mult))
                                else:
                                    v0 = probs[:, 640 * tt:640 * tt + 128]
                                    v4 = probs[:, 640 * tt + 512:640 * tt + 640]
                                    v.tensor_tensor(v0, v0, c_maskB[:, 0:128], op=ALU.mult)
                                    p.inc("VE", v.tensor_tensor(v4, v4, c_maskB[:, 128:256], op=ALU.mult))
                        block.vector(_v5)
                        ve_base = p.val("VE")

                        def _t5(t, tlist=tlist, hh=hh, g1=ve_base, g2=tail_ve):
                            p.wait(t, "VE", max(g1, g2))
                            for tt in tlist:
                                for w in range(5):
                                    kc = tt + w
                                    ins = t.matmul(
                                        PP[:, 3072 + 128 * tt:3072 + 128 * tt + 65],
                                        probs[:, 640 * tt + 128 * w:640 * tt + 128 * w + 128],
                                        vaug[:, 1040 * kc + 65 * hh:1040 * kc + 65 * hh + 65],
                                        start=(w == 0), stop=(w == 4))
                            p.inc("PE", ins)
                        block.tensor(_t5)
                    pending_tail[0] = hh
                tail_ve = emit_tail(pending_tail[0])
                pending_tail[0] = None

                ve_a3 = p.val("VE")
                se_a3 = p.val("SE")

                if dbg_dump and half == 0:
                    def _dump(g, g1=ve_a3):
                        p.wait(g, "VE", g1)
                        for nm, src in [("d_xlnf", xlnf), ("d_qfm", qfm), ("d_kfm", kfm),
                                        ("d_vaug", vaug), ("d_probs", probs),
                                        ("d_aofm", aofm), ("d_aotm", aotm)]:
                            p.inc("ST", g.dma_start(out=dbg[nm], in_=src[:]), 16)
                        p.wait(g, "ST")
                    block.gpsimd(_dump)

                for tt in range(8):
                    buf = tt % 2
                    obase = 0 if buf == 0 else 1024
                    gt = 8 * half + tt
                    g_se = max(se_a3, ln_se_g.get((half, 10 + buf), 0))
                    g_ve = res_ve.get(gt - 2, 0)

                    def _ldx(s, gt=gt, buf=buf, g_se=g_se, g_ve=g_ve):
                        p.wait(s, "SE", g_se)
                        p.wait(s, "VE", g_ve)
                        p.inc("LS", s.dma_start(out=xt[:, 1024 * buf:1024 * buf + 1024],
                                                in_=xh_t[4 + gt]), 16)
                    block.sync(_ldx)
                    ld_x = p.val("LS")

                    def _t7(t, tt=tt, obase=obase, g1=ve_a3, g2=res_ve.get(gt - 2, 0)):
                        p.wait(t, "VE", max(g1, g2))
                        for c in range(8):
                            for nn in range(2):
                                ins = t.matmul(PP[:, obase + 512 * nn:obase + 512 * nn + 512],
                                               aofm[:, 1024 * c + 128 * tt:1024 * c + 128 * tt + 128],
                                               wo_b[:, 1024 * c + 512 * nn:1024 * c + 512 * nn + 512],
                                               start=(c == 0), stop=(c == 7))
                        p.inc("PE", ins)
                    block.tensor(_t7)
                    pe_o = p.val("PE")

                    def _v8(v, buf=buf, obase=obase, pe_o=pe_o, ld_x=ld_x,
                            g_st=st_cnt.get(gt - 2, 0)):
                        p.wait(v, "PE", pe_o)
                        p.wait(v, "LS", ld_x)
                        p.wait(v, "ST", g_st)
                        p.inc("VE", v.tensor_tensor(rt[:, 1024 * buf:1024 * buf + 1024],
                                                    PP[:, obase:obase + 1024],
                                                    xt[:, 1024 * buf:1024 * buf + 1024], op=ALU.add))
                    block.vector(_v8)
                    res_ve[gt] = p.val("VE")

                    def _st(g, gt=gt, buf=buf, ve_r=res_ve[gt]):
                        p.wait(g, "VE", ve_r)
                        dst = out_t[gt] if n_sub == 1 else xres_t[gt]
                        p.inc("ST", g.dma_start(out=dst, in_=rt[:, 1024 * buf:1024 * buf + 1024]), 16)
                    block.gpsimd(_st)
                    st_cnt[gt] = p.val("ST")

        emit_attn_a()
        es_a.close()

        def emit_join():
            vals = {nm: p.val(nm) for nm in ["LD", "LS", "W", "PE", "VE", "SE", "PO", "ST"]}
            for bb in (block.tensor, block.vector, block.scalar, block.gpsimd, block.sync):
                def _j(e, vals=vals):
                    for nm, v in vals.items():
                        p.wait(e, nm, v)
                bb(_j)

        def emit_ffn(src_t, dst_t, first, write_out, ccs="C1"):
            emit_join()
            with ExitStack() as ef:
                def SBF(name, shape, dt):
                    return ef.enter_context(nc.sbuf_tensor(name + "_" + ccs, shape, dt))
                xlf = SBF("xlf", [128, 8 * 2048], BF16)
                hfm = SBF("hfm", [128, 4 * 2048], BF16)
                w1jc = SBF("w1jc", [128, 2 * 1024], BF16)
                w2b = SBF("w2b", [128, 4 * 64], BF16)
                alpha = SBF("alpha", [128, 16 * 64], BF16)
                alphaT = SBF("alphaT", [64, 2048], BF16)
                wihT = SBF("wihT", [128, 3072], BF16)
                wraw = SBF("wraw", [128, 2 * 3072], BF16)
                sa = SBF("sa", [64, 6 * 1024], F32)
                sab = SBF("sab", [64, 3712], BF16)
                sfmb = SBF("sfmb", [128, 1536], BF16)
                wsl = SBF("wsl", [128, 8 * 1024], BF16)
                xlnt = SBF("xlnt", [128, 2 * 1024], BF16)
                ffsc = SBF("ffsc", [128, 256], F32)
                sipart = SBF("sipart", [64, 1028], F32)
                sitot = SBF("sitot", [64, 1028], F32)
                brow = SBF("brow", [1, 4096], BF16)
                gh0_sb = SBF("gh0_sb", [64, 3 * D], F32)

                s0f = SBF("s0f", [64, D], F32)
                s0b = SBF("s0b", [64, D], BF16)
                c1sb = SBF("c1sb", [128, 4], F32)
                Gt = SBF("Gt", [64, D], BF16)
                Bt = SBF("Bt", [64, D], BF16)

                SA = dict(t1=0, t2=1024, r=2048, z=3072, n=4096, snew=5120, y=2048)

                def sac(nm):
                    return sa[:, SA[nm]:SA[nm] + 1024]

                if True:
                    def _g0(g):
                        p.inc("LD", g.dma_start(out=s0f[:], in_=ffn_in["slot_init"]), 16)
                        p.inc("LD", g.dma_start(out=s0b[:], in_=ffn_in["slot_init"]), 16)
                        p.inc("LD", g.dma_start(out=brow[0:1, 0:1024], in_=ffn_in["ln_g"]), 16)
                        p.inc("LD", g.dma_start(out=brow[0:1, 1024:2048], in_=ffn_in["ln_b"]), 16)
                        p.inc("LD", g.dma_start(out=ffsc[0:1, 0:1], in_=ffn_in["tau"]), 16)
                        p.inc("LD", g.dma_start(
                            out=wsl.ap().rearrange("p (c j) -> p c j", c=8)[:, :, 0:512],
                            in_=ffn_in["w1"].rearrange("(c p) j -> p c j", p=128)[:, 8:16, :]), 16)
                        p.inc("LD", g.dma_start(out=brow[0:1, 2048:2560], in_=ffn_in["b1"]), 16)
                    block.gpsimd(_g0)
                    ld0 = p.val("LD")

                    def _s0(sc):
                        p.wait(sc, "LD", ld0)
                        p.inc("SE", sc.activation(ffsc[0:1, 1:2], ffsc[0:1, 0:1], AF.Abs,
                                                  bias=cst[0:1, 2:3], scale=1.0))
                    block.scalar(_s0)
                    se0 = p.val("SE")

                    def _v0(v):
                        p.wait(v, "SE", se0)
                        p.inc("VE", v.tensor_scalar(ffsc[0:1, 2:3], ffsc[0:1, 1:2], 0.1, None, op0=ALU.add))
                        p.wait(v, "VE")
                        p.inc("VE", v.reciprocal(ffsc[0:1, 3:4], ffsc[0:1, 2:3]))
                        p.wait(v, "VE")
                        p.inc("VE", v.tensor_copy(brow[0:1, 2560:2561], ffsc[0:1, 3:4]))
                    block.vector(_v0)
                    ve0 = p.val("VE")

                    def _t0(t):
                        p.wait(t, "VE", ve0)
                        t.matmul(PP[:, 0:1], onesr[:], brow[0:1, 2560:2561], start=True, stop=True)
                        for c in range(8):
                            ins = t.matmul(PP[:, 128 + 64 * c:128 + 64 * c + 64],
                                           s0b[:, 128 * c:128 * c + 128],
                                           c_ident[0:64, 0:64], start=True, stop=True)
                        p.inc("PE", ins)
                    block.tensor(_t0)
                    pe0 = p.val("PE")

                    def _v01(v):
                        p.wait(v, "PE", pe0)
                        v.tensor_copy(cst[:, 3:4], PP[:, 0:1])
                        v.tensor_copy(sfmb[:, 512:1024], PP[:, 128:640])
                        p.inc("VE", v.tensor_reduce(
                            stats[:, 24:32], PP[:, 128:640].rearrange("p (c s) -> p c s", s=64),
                            axis=mybir.AxisListType.X, op=ALU.add))
                        p.wait(v, "VE")
                        p.inc("VE", v.tensor_scalar(sfmb[:, 1024:1032], stats[:, 24:32],
                                                    1.0 / 64, None, op0=ALU.mult))
                    block.vector(_v01)
                    ve01 = p.val("VE")

                    def _t01(t):
                        p.wait(t, "VE", ve01)
                        for piece in range(4):
                            for c in range(8):
                                t.matmul(PP[:, 1024 + piece:1024 + piece + 1],
                                         wsl[:, 1024 * c + 128 * piece:1024 * c + 128 * piece + 128],
                                         sfmb[:, 1024 + c:1024 + c + 1],
                                         start=(c == 0), stop=False)
                            t.matmul(PP[:, 1024 + piece:1024 + piece + 1],
                                     brow[0:1, 2048 + 128 * piece:2048 + 128 * piece + 128],
                                     onesr[0:1, 0:1], start=False, stop=True)
                        for piece in range(2):
                            t.matmul(PP[0:64, 2048 + 512 * piece:2560 + 512 * piece],
                                     onesr[0:1, 0:64],
                                     brow[0:1, 512 * piece:512 * piece + 512],
                                     start=True, stop=True)
                            ins = t.matmul(PP[0:64, 3072 + 512 * piece:3584 + 512 * piece],
                                           onesr[0:1, 0:64],
                                           brow[0:1, 1024 + 512 * piece:1536 + 512 * piece],
                                           start=True, stop=True)
                        p.inc("PE", ins)
                    block.tensor(_t01)
                    pe01 = p.val("PE")

                    def _v02(v):
                        p.wait(v, "PE", pe01)
                        v.tensor_copy(c1sb[:], PP[:, 1024:1028])
                        v.tensor_copy(Gt[:], PP[0:64, 2048:3072])
                        p.inc("VE", v.tensor_copy(Bt[:], PP[0:64, 3072:4096]))
                    block.vector(_v02)
                ve_f0 = p.val("VE")

                ln_f = {}
                ev_f = {}
                t1p = {}
                sp_st = {}
                for tt in range(NT):
                    buf = tt % 2
                    tb = 0 if buf == 0 else 1024
                    xsrc = xt[:, 1024 * buf:1024 * buf + 1024]
                    xdst = xlt[:, 1024 * buf:1024 * buf + 1024]

                    def _ld(sy, tt=tt, xsrc=xsrc, g=ln_f.get(tt - 2, 0)):
                        p.wait(sy, "SE", g)
                        p.inc("LS", sy.dma_start(out=xsrc, in_=src_t[tt]), 16)
                    block.sync(_ld)
                    ln_f[tt] = emit_ln(xsrc, xdst, sslot=buf,
                                       extra_se_wait=("PE", t1p.get(tt - 2, 0)),
                                       ld_guard=p.val("LS"))

                    def _sp(g, tt=tt, xdst=xdst, se=ln_f[tt]):
                        p.wait(g, "SE", se)
                        p.inc("ST", g.dma_start(out=xlnsp_t[tt], in_=xdst), 16)
                    block.gpsimd(_sp)
                    sp_st[tt] = p.val("ST")

                    def _t1(t, xdst=xdst, tb=tb, se=ln_f[tt], g=ev_f.get(tt - 2, 0),
                            g2=sp_st[tt]):
                        p.wait(t, "SE", se)
                        p.wait(t, "VE", g)
                        p.wait(t, "ST", g2)
                        for c in range(8):
                            ins = t.matmul(PP[:, tb + 128 * c:tb + 128 * c + 128],
                                           xdst[:, 128 * c:128 * c + 128], c_ident[:],
                                           start=True, stop=True)
                        p.inc("PE", ins)
                    block.tensor(_t1)
                    t1p[tt] = p.val("PE")

                    def _v3(v, tb=tb, tt=tt, pe=t1p[tt]):
                        p.wait(v, "PE", pe)
                        dst = xlf.ap().rearrange("p (c j) -> p c j", c=8)[:, :, 128 * tt:128 * tt + 128]
                        p.inc("VE", v.tensor_copy(
                            dst, PP[:, tb:tb + 1024].rearrange("p (c j) -> p c j", c=8)))
                    block.vector(_v3)
                    ev_f[tt] = p.val("VE")
                ve_f1 = p.val("VE")

                if dbg_ffn:
                    def _dxl(g, g1=ve_f1):
                        p.wait(g, "VE", g1)
                        p.inc("ST", g.dma_start(out=dbg["e_xlf"], in_=xlf[:]), 16)
                        p.inc("ST", g.dma_start(out=dbg["e_xres"].rearrange("(a p) d -> p a d", p=128),
                                                in_=xt.ap().rearrange("p (a d) -> p a d", a=2)), 16)
                        p.wait(g, "ST")
                    block.gpsimd(_dxl)

                def _gw2(g):
                    p.inc("W", g.dma_start(
                        out=w2b.ap().rearrange("p (c j) -> p c j", c=4),
                        in_=ffn_in["w2"].rearrange("(c p) j -> p c j", p=128)), 16)
                    p.inc("W", g.dma_start(out=brow[0:1, 3072:3136], in_=ffn_in["b2"]), 16)
                block.gpsimd(_gw2)
                w2_ld = p.val("W")

                h_se = {}
                for jc in range(4):
                    buf = jc % 2
                    pb = 2048 * buf
                    wb = w1jc[:, 1024 * buf:1024 * buf + 1024]

                    def _gw1(g, jc=jc, wb=wb, guard=h_se.get(jc - 2, 0)):
                        p.wait(g, "SE", guard)
                        p.inc("W", g.dma_start(
                            out=wb.rearrange("p (c j) -> p c j", c=8),
                            in_=ffn_in["w1"].rearrange("(c p) j -> p c j", p=128)[:, 0:8, 128 * jc:128 * jc + 128]), 16)
                    block.gpsimd(_gw1)
                    wld = p.val("W")

                    def _t2(t, wb=wb, wld=wld, pb=pb, g1=ve_f1, g2=h_se.get(jc - 2, 0)):
                        p.wait(t, "W", wld)
                        p.wait(t, "VE", g1)
                        p.wait(t, "SE", g2)
                        for piece in range(4):
                            for c in range(8):
                                ins = t.matmul(
                                    PP[:, pb + 512 * piece:pb + 512 * piece + 512],
                                    wb[:, 128 * c:128 * c + 128],
                                    xlf[:, 2048 * c + 512 * piece:2048 * c + 512 * piece + 512],
                                    start=(c == 0), stop=(c == 7))
                        p.inc("PE", ins)
                    block.tensor(_t2)
                    pe2 = p.val("PE")

                    def _s2(sc, jc=jc, pe2=pe2, pb=pb, g0=ve_f0):
                        p.wait(sc, "PE", pe2)
                        p.wait(sc, "VE", g0)
                        p.inc("SE", sc.activation(hfm[:, 2048 * jc:2048 * jc + 2048],
                                                  PP[:, pb:pb + 2048], AF.Gelu,
                                                  bias=c1sb[:, jc:jc + 1], scale=1.0))
                    block.scalar(_s2)
                    h_se[jc] = p.val("SE")
                se_f2 = p.val("SE")

                al_ve = {}
                sel_ve = {}
                for tt in range(NT):
                    buf = tt % 2
                    rb4 = 2048 if buf == 0 else 2560
                    so2 = 16 + 32 * buf
                    fo = 128 * buf
                    xb = xlnt[:, 1024 * buf:1024 * buf + 1024]

                    def _lx(g, xb=xb, guard=al_ve.get(tt - 2, 0), g2=sp_st[tt]):
                        p.wait(g, "VE", guard)
                        p.wait(g, "ST", g2)
                        p.inc("LD", g.dma_start(out=xb, in_=xlnsp_t[tt]), 16)
                    block.gpsimd(_lx)
                    lx = p.val("LD")

                    def _t3(t, tt=tt, rb4=rb4, g1=se_f2, g2=w2_ld, g3=sel_ve.get(tt - 2, 0)):
                        p.wait(t, "SE", g1)
                        p.wait(t, "W", g2)
                        p.wait(t, "VE", g3)
                        for c in range(4):
                            t.matmul(PP[:, rb4:rb4 + 64],
                                     hfm[:, 2048 * c + 128 * tt:2048 * c + 128 * tt + 128],
                                     w2b[:, 64 * c:64 * c + 64], start=(c == 0), stop=False)
                        ins = t.matmul(PP[:, rb4:rb4 + 64], onesr[:], brow[0:1, 3072:3136],
                                       start=False, stop=True)
                        p.inc("PE", ins)
                    block.tensor(_t3)
                    pe3 = p.val("PE")

                    def _v4(v, rb4=rb4, so2=so2, fo=fo, pe3=pe3):
                        p.wait(v, "PE", pe3)
                        p.inc("VE", v.max(stats[:, so2:so2 + 8], PP[:, rb4:rb4 + 64]))
                        p.wait(v, "VE")
                        v.scalar_tensor_tensor(out=stats[:, so2 + 8:so2 + 9],
                                               in0=stats[:, so2:so2 + 1],
                                               scalar=-1.0, in1=cst[:, 3:4],
                                               op0=ALU.mult, op1=ALU.mult)
                        p.inc("VE", v.tensor_scalar(ffsc[:, fo + 64:fo + 128], PP[:, rb4:rb4 + 64],
                                                    stats[:, so2 + 7:so2 + 8], None, op0=ALU.is_ge))
                    block.vector(_v4)
                    sel_ve[tt] = p.val("VE")
                    ve4 = p.val("VE")

                    def _s3(sc, rb4=rb4, so2=so2, fo=fo, ve4=ve4):
                        p.wait(sc, "VE", ve4)
                        p.inc("SE", sc.activation(ffsc[:, fo:fo + 64], PP[:, rb4:rb4 + 64], AF.Exp,
                                                  bias=stats[:, so2 + 8:so2 + 9], scale=cst[:, 3:4]))
                    block.scalar(_s3)
                    se3 = p.val("SE")

                    def _v5(v, tt=tt, so2=so2, fo=fo, se3=se3):
                        p.wait(v, "SE", se3)
                        p.inc("VE", v.scalar_tensor_tensor(
                            out=ffsc[:, fo:fo + 64], in0=ffsc[:, fo:fo + 64], scalar=1.0,
                            in1=ffsc[:, fo + 64:fo + 128], op0=ALU.mult, op1=ALU.mult,
                            accum_out=stats[:, so2 + 9:so2 + 10]))
                        p.wait(v, "VE")
                        p.inc("VE", v.reciprocal(stats[:, so2 + 10:so2 + 11], stats[:, so2 + 9:so2 + 10]))
                        p.wait(v, "VE")
                        p.inc("VE", v.tensor_scalar(alpha[:, 64 * tt:64 * tt + 64],
                                                    ffsc[:, fo:fo + 64], stats[:, so2 + 10:so2 + 11],
                                                    None, op0=ALU.mult))
                    block.vector(_v5)
                    ve5 = p.val("VE")

                    def _t4(t, tt=tt, ve5=ve5, lx=lx, g3=al_ve.get(tt - 1, 0)):
                        p.wait(t, "VE", max(ve5, g3))
                        p.wait(t, "LD", lx)
                        t.matmul(PP[0:64, 0:128], alpha[:, 64 * tt:64 * tt + 64],
                                 c_ident[:], start=True, stop=True)
                        xb2 = xlnt[:, 1024 * (tt % 2):1024 * (tt % 2) + 1024]
                        for piece in range(2):
                            ins = t.matmul(PP[0:64, 3072 + 512 * piece:3584 + 512 * piece],
                                           alpha[:, 64 * tt:64 * tt + 64],
                                           xb2[:, 512 * piece:512 * piece + 512],
                                           start=(tt == 0), stop=(tt == NT - 1),
                                           skip_group_check=True)
                        p.inc("PE", ins)
                    block.tensor(_t4)
                    pe4 = p.val("PE")

                    def _v6(v, tt=tt, pe4=pe4):
                        p.wait(v, "PE", pe4)
                        p.inc("VE", v.tensor_copy(alphaT[:, 128 * tt:128 * tt + 128],
                                                  PP[0:64, 0:128]))
                    block.vector(_v6)
                    al_ve[tt] = p.val("VE")
                pe_si = p.val("PE")

                def _v7(v, g=pe_si):
                    p.wait(v, "PE", g)
                    v.memset(sipart[:, 1025:1028], 0.0)
                    v.tensor_copy(sipart[:, 0:1024], PP[0:64, 3072:4096])
                    p.inc("VE", v.tensor_reduce(sipart[:, 1024:1025], alphaT[:],
                                                axis=mybir.AxisListType.X, op=ALU.add))
                block.vector(_v7)
                ve7 = p.val("VE")

                def _gcc(g, ve7=ve7):
                    p.wait(g, "VE", ve7)
                    p.inc("ST", g.dma_start(out=cc_in[:, :], in_=sipart[:]), 16)
                    p.wait(g, "ST")
                    g.collective_compute(
                        "AllReduce", ALU.add, replica_groups=[[0, 1, 2, 3], [4, 5, 6, 7]],
                        ins=[cc_in.ap().opt()], outs=[cc_out.ap().opt()]).then_inc(p.sem[ccs], 1)
                    g.wait_ge(p.sem[ccs], 1)
                    p.inc("LD", g.dma_start(out=sitot[:], in_=cc_out[:, :]), 16)
                block.gpsimd(_gcc)
                ld_cc = p.val("LD")

                def _v8(v, g=ld_cc):
                    p.wait(v, "LD", g)
                    p.inc("VE", v.tensor_scalar(stats[0:64, 25:26], sitot[:, 1024:1025],
                                                1e-8, None, op0=ALU.add))
                    p.wait(v, "VE")
                    p.inc("VE", v.reciprocal(stats[0:64, 26:27], stats[0:64, 25:26]))
                    p.wait(v, "VE")
                    p.inc("VE", v.tensor_scalar(sab[:, 0:1024], sitot[:, 0:1024],
                                                stats[0:64, 26:27], None, op0=ALU.mult))
                block.vector(_v8)
                ve8 = p.val("VE")

                def _t5(t, g=ve8):
                    p.wait(t, "VE", g)
                    for c in range(8):
                        ins = t.matmul(PP[:, 64 * c:64 * c + 64], sab[:, 128 * c:128 * c + 128],
                                       c_ident[0:64, 0:64], start=True, stop=True)
                    p.inc("PE", ins)
                block.tensor(_t5)
                pe5 = p.val("PE")

                def _v9(v, g=pe5):
                    p.wait(v, "PE", g)
                    p.inc("VE", v.tensor_copy(sfmb[:, 0:512], PP[:, 0:512]))
                block.vector(_v9)
                ve9 = p.val("VE")

                def emit_gru_mm(weight_ap, bias_ap, lhs_col0, psum_free_guard):
                    tr_ve = {}
                    entry_ve = p.val("VE")
                    for c in range(8):
                        wbuf2 = c % 2
                        wrb = wraw[:, 3072 * wbuf2:3072 * wbuf2 + 3072]

                        def _gw(g, c=c, wrb=wrb, guard=max(tr_ve.get(c - 2, 0), entry_ve)):
                            p.wait(g, "VE", guard)
                            p.inc("W", g.dma_start(
                                out=wrb.rearrange("p (jc j) -> p jc j", jc=24),
                                in_=weight_ap.rearrange("(jc p) d -> p jc d", p=128)[:, :, 128 * c:128 * c + 128]), 16)
                        block.gpsimd(_gw)
                        wld = p.val("W")

                        prev_ev = p.val("VE")
                        for b3 in range(3):
                            def _tt(t, wrb=wrb, b3=b3, wld=wld, g=prev_ev):
                                p.wait(t, "W", wld)
                                p.wait(t, "VE", g)
                                for j in range(8):
                                    ins = t.matmul(PP[:, 3072 + 128 * j:3072 + 128 * j + 128],
                                                   wrb[:, 128 * (8 * b3 + j):128 * (8 * b3 + j) + 128],
                                                   c_ident[:], start=True, stop=True)
                                p.inc("PE", ins)
                            block.tensor(_tt)
                            peb = p.val("PE")

                            def _ve(v, b3=b3, peb=peb):
                                p.wait(v, "PE", peb)
                                p.inc("VE", v.tensor_copy(wihT[:, 1024 * b3:1024 * b3 + 1024],
                                                          PP[:, 3072:4096]))
                            block.vector(_ve)
                            prev_ev = p.val("VE")
                        tr_ve[c] = p.val("VE")

                        def _tg(t, c=c, g=tr_ve[c], g2=psum_free_guard):
                            p.wait(t, "VE", max(g, g2))
                            for piece in range(6):
                                ins = t.matmul(PP[0:64, 512 * piece:512 * piece + 512],
                                               sfmb[:, lhs_col0 + 64 * c:lhs_col0 + 64 * c + 64],
                                               wihT[:, 512 * piece:512 * piece + 512],
                                               start=(c == 0), stop=False,
                                               skip_group_check=True)
                            p.inc("PE", ins)
                        block.tensor(_tg)

                    def _gb(g):
                        p.inc("W", g.dma_start(out=brow[0:1, 0:3072], in_=bias_ap), 16)
                    block.gpsimd(_gb)
                    bld = p.val("W")

                    def _tb(t, g=bld):
                        p.wait(t, "W", g)
                        for piece in range(6):
                            ins = t.matmul(PP[0:64, 512 * piece:512 * piece + 512],
                                           onesr[0:1, 0:64],
                                           brow[0:1, 512 * piece:512 * piece + 512],
                                           start=False, stop=True, skip_group_check=True)
                        p.inc("PE", ins)
                    block.tensor(_tb)
                    return p.val("PE")

                if True:
                    pe_gh = emit_gru_mm(ffn_in["Whh"], ffn_in["bhh"], 512, ve9)

                    def _vgh(v, g=pe_gh):
                        p.wait(v, "PE", g)
                        p.inc("VE", v.tensor_copy(gh0_sb[:], PP[0:64, 0:3072]))
                    block.vector(_vgh)
                gh_done = p.val("VE")

                pe_gi = emit_gru_mm(ffn_in["Wih"], ffn_in["bih"], 0, gh_done)

                def _vg1(v, g=pe_gi):
                    p.wait(v, "PE", g)
                    v.tensor_tensor(sac("t1"), PP[0:64, 0:1024], gh0_sb[:, 0:1024], op=ALU.add)
                    p.inc("VE", v.tensor_tensor(sac("t2"), PP[0:64, 1024:2048],
                                                gh0_sb[:, 1024:2048], op=ALU.add))
                block.vector(_vg1)
                vg1 = p.val("VE")

                def _sg1(sc, g=vg1):
                    p.wait(sc, "VE", g)
                    sc.activation(sac("r"), sac("t1"), AF.Sigmoid, bias=cst[0:64, 2:3], scale=1.0)
                    p.inc("SE", sc.activation(sac("z"), sac("t2"), AF.Sigmoid,
                                              bias=cst[0:64, 2:3], scale=1.0))
                block.scalar(_sg1)
                sg1 = p.val("SE")

                def _vg2(v, g=sg1):
                    p.wait(v, "SE", g)
                    p.inc("VE", v.tensor_tensor(sac("t1"), sac("r"), gh0_sb[:, 2048:3072], op=ALU.mult))
                    p.wait(v, "VE")
                    p.inc("VE", v.tensor_tensor(sac("t2"), sac("t1"), PP[0:64, 2048:3072], op=ALU.add))
                block.vector(_vg2)
                vg2 = p.val("VE")

                def _sg2(sc, g=vg2):
                    p.wait(sc, "VE", g)
                    p.inc("SE", sc.activation(sac("n"), sac("t2"), AF.Tanh,
                                              bias=cst[0:64, 2:3], scale=1.0))
                block.scalar(_sg2)
                sg2 = p.val("SE")

                def _vg3(v, g=sg2):
                    p.wait(v, "SE", g)
                    p.inc("VE", v.tensor_tensor(sac("t1"), s0f[:], sac("n"), op=ALU.subtract))
                    p.wait(v, "VE")
                    p.inc("VE", v.tensor_tensor(sac("t2"), sac("z"), sac("t1"), op=ALU.mult))
                    p.wait(v, "VE")
                    p.inc("VE", v.tensor_tensor(sac("snew"), sac("n"), sac("t2"), op=ALU.add))
                    p.wait(v, "VE")
                    p.inc("VE", v.tensor_copy(sab[:, 1024:2048], sac("snew")))
                block.vector(_vg3)
                vg3 = p.val("VE")

                def _t6(t, g=vg3):
                    p.wait(t, "VE", g)
                    for c in range(8):
                        ins = t.matmul(PP[:, 3072 + 64 * c:3072 + 64 * c + 64],
                                       sab[:, 1024 + 128 * c:1024 + 128 * c + 128],
                                       c_ident[0:64, 0:64], start=True, stop=True)
                    p.inc("PE", ins)
                block.tensor(_t6)
                pe6 = p.val("PE")

                def _v10(v, g=pe6):
                    p.wait(v, "PE", g)
                    p.inc("VE", v.tensor_copy(sfmb[:, 0:512], PP[:, 3072:3584]))
                block.vector(_v10)
                ve10 = p.val("VE")

                def _gqk(g):
                    p.inc("W", g.dma_start(
                        out=wsl.ap().rearrange("p (c j) -> p c j", c=8)[:, :, 0:256],
                        in_=ffn_in["wq"].rearrange("(c p) j -> p c j", p=128)), 16)
                    p.inc("W", g.dma_start(
                        out=wsl.ap().rearrange("p (c j) -> p c j", c=8)[:, :, 256:512],
                        in_=ffn_in["wk"].rearrange("(c p) j -> p c j", p=128)), 16)
                    p.inc("W", g.dma_start(out=brow[0:1, 3136:3392], in_=ffn_in["bq"]), 16)
                    p.inc("W", g.dma_start(out=brow[0:1, 3392:3648], in_=ffn_in["bk"]), 16)
                block.gpsimd(_gqk)
                qk_ld = p.val("W")

                def _t7(t, g1=ve10, g2=qk_ld):
                    p.wait(t, "VE", g1)
                    p.wait(t, "W", g2)
                    for c in range(8):
                        t.matmul(PP[0:64, 0:256], sfmb[:, 64 * c:64 * c + 64],
                                 wsl[:, 1024 * c:1024 * c + 256], start=(c == 0), stop=False)
                    t.matmul(PP[0:64, 0:256], onesr[0:1, 0:64], brow[0:1, 3136:3392],
                             start=False, stop=True)
                    for c in range(8):
                        t.matmul(PP[0:64, 512:768], sfmb[:, 64 * c:64 * c + 64],
                                 wsl[:, 1024 * c + 256:1024 * c + 512], start=(c == 0), stop=False)
                    ins = t.matmul(PP[0:64, 512:768], onesr[0:1, 0:64], brow[0:1, 3392:3648],
                                   start=False, stop=True)
                    p.inc("PE", ins)
                block.tensor(_t7)
                pe7 = p.val("PE")

                def _v11(v, g=pe7):
                    p.wait(v, "PE", g)
                    v.tensor_copy(sab[:, 2048:2304], PP[0:64, 0:256])
                    p.inc("VE", v.tensor_copy(sab[:, 2304:2560], PP[0:64, 512:768]))
                block.vector(_v11)
                ve11 = p.val("VE")

                def _gv(g, g1=pe7):
                    p.wait(g, "PE", g1)
                    p.inc("W", g.dma_start(
                        out=wsl.ap().rearrange("p (c j) -> p c j", c=8)[:, :, 0:1024],
                        in_=ffn_in["wv"].rearrange("(c p) j -> p c j", p=128)), 16)
                    p.inc("W", g.dma_start(out=brow[0:1, 0:1024], in_=ffn_in["bv"]), 16)
                block.gpsimd(_gv)
                v_ld = p.val("W")

                def _t8(t, g1=ve11, g2=v_ld, g3=vg3):
                    p.wait(t, "W", g2)
                    p.wait(t, "VE", max(g1, g3))
                    for piece in range(2):
                        for c in range(8):
                            t.matmul(PP[0:64, 1024 + 512 * piece:1536 + 512 * piece],
                                     sfmb[:, 64 * c:64 * c + 64],
                                     wsl[:, 1024 * c + 512 * piece:1024 * c + 512 * piece + 512],
                                     start=(c == 0), stop=False)
                        t.matmul(PP[0:64, 1024 + 512 * piece:1536 + 512 * piece],
                                 onesr[0:1, 0:64], brow[0:1, 512 * piece:512 * piece + 512],
                                 start=False, stop=True)
                    for c in range(2):
                        t.matmul(PP[:, 3072 + 64 * c:3072 + 64 * c + 64],
                                 sab[:, 2048 + 128 * c:2048 + 128 * c + 128],
                                 c_ident[0:64, 0:64], start=True, stop=True)
                        ins = t.matmul(PP[:, 3200 + 64 * c:3200 + 64 * c + 64],
                                       sab[:, 2304 + 128 * c:2304 + 128 * c + 128],
                                       c_ident[0:64, 0:64], start=True, stop=True)
                    p.inc("PE", ins)
                block.tensor(_t8)
                pe8 = p.val("PE")

                def _v12(v, g=pe8):
                    p.wait(v, "PE", g)
                    v.tensor_copy(sfmb[:, 1040:1168], PP[:, 3072:3200])
                    v.tensor_copy(sfmb[:, 1168:1296], PP[:, 3200:3328])
                    p.inc("VE", v.tensor_copy(sab[:, 2560:3584], PP[0:64, 1024:2048]))
                block.vector(_v12)
                ve12 = p.val("VE")

                def _t9(t, g=ve12):
                    p.wait(t, "VE", g)
                    for c in range(2):
                        ins = t.matmul(PP[0:64, 2048:2112], sfmb[:, 1040 + 64 * c:1040 + 64 * c + 64],
                                       sfmb[:, 1168 + 64 * c:1168 + 64 * c + 64],
                                       start=(c == 0), stop=(c == 1))
                    p.inc("PE", ins)
                block.tensor(_t9)
                pe9 = p.val("PE")

                def _v13(v, g=pe9):
                    p.wait(v, "PE", g)
                    p.inc("VE", v.reduce_max(stats[0:64, 16:17], PP[0:64, 2048:2112],
                                             axis=mybir.AxisListType.X, negate=True))
                    p.wait(v, "VE")
                    p.inc("VE", v.tensor_scalar(stats[0:64, 17:18], stats[0:64, 16:17],
                                                0.0625, None, op0=ALU.mult))
                block.vector(_v13)
                ve13 = p.val("VE")

                def _s4(sc, g=ve13):
                    p.wait(sc, "VE", g)
                    p.inc("SE", sc.activation(ffsc[0:64, 0:64], PP[0:64, 2048:2112], AF.Exp,
                                              bias=stats[0:64, 17:18], scale=0.0625,
                                              accum_out=stats[0:64, 18:19]))
                block.scalar(_s4)
                se4 = p.val("SE")

                def _v14(v, g=se4):
                    p.wait(v, "SE", g)
                    p.inc("VE", v.reciprocal(stats[0:64, 19:20], stats[0:64, 18:19]))
                    p.wait(v, "VE")
                    p.inc("VE", v.tensor_scalar(sab[:, 3584:3648], ffsc[0:64, 0:64],
                                                stats[0:64, 19:20], None, op0=ALU.mult))
                block.vector(_v14)
                ve14 = p.val("VE")

                def _t10(t, g=ve14):
                    p.wait(t, "VE", g)
                    p.inc("PE", t.matmul(PP[0:64, 3072:3136], sab[:, 3584:3648],
                                         c_ident[0:64, 0:64], start=True, stop=True))
                block.tensor(_t10)
                pe10 = p.val("PE")

                def _v15(v, g=pe10):
                    p.wait(v, "PE", g)
                    p.inc("VE", v.tensor_copy(sab[:, 3648:3712], PP[0:64, 3072:3136]))
                block.vector(_v15)
                ve15 = p.val("VE")

                def _t11(t, g=ve15):
                    p.wait(t, "VE", g)
                    for piece in range(2):
                        ins = t.matmul(PP[0:64, 2048 + 512 * piece:2560 + 512 * piece],
                                       sab[:, 3648:3712],
                                       sab[:, 2560 + 512 * piece:3072 + 512 * piece],
                                       start=True, stop=True)
                    p.inc("PE", ins)
                block.tensor(_t11)
                pe11 = p.val("PE")

                def _v16(v, g=pe11):
                    p.wait(v, "PE", g)
                    p.inc("VE", v.tensor_tensor(sac("y"), sac("snew"), PP[0:64, 2048:3072], op=ALU.add))
                block.vector(_v16)
                ve16 = p.val("VE")
                ln_y = emit_ln(sac("y"), sac("t2"), sslot=2, rows=64,
                               extra_ve_wait=ve16)

                def _v17(v, g=ln_y):
                    p.wait(v, "SE", g)
                    p.inc("VE", v.tensor_tensor(sac("t1"), sac("t2"), Gt[:], op=ALU.mult))
                    p.wait(v, "VE")
                    p.inc("VE", v.tensor_tensor(sac("t2"), sac("t1"), Bt[:], op=ALU.add))
                    p.wait(v, "VE")
                    p.inc("VE", v.tensor_copy(sab[:, 2048:3072], sac("t2")))
                block.vector(_v17)
                ve17 = p.val("VE")

                def _t12(t, g=ve17):
                    p.wait(t, "VE", g)
                    for c in range(8):
                        ins = t.matmul(PP[:, 3072 + 64 * c:3072 + 64 * c + 64],
                                       sab[:, 2048 + 128 * c:2048 + 128 * c + 128],
                                       c_ident[0:64, 0:64], start=True, stop=True)
                    p.inc("PE", ins)
                block.tensor(_t12)
                pe12 = p.val("PE")

                def _v18(v, g=pe12):
                    p.wait(v, "PE", g)
                    p.inc("VE", v.tensor_copy(sfmb[:, 0:512], PP[:, 3072:3584]))
                block.vector(_v18)
                ve18 = p.val("VE")

                def _gvp(g, g1=pe8):
                    p.wait(g, "PE", g1)
                    p.inc("W", g.dma_start(
                        out=wsl.ap().rearrange("p (c j) -> p c j", c=8)[:, :, 0:1024],
                        in_=ffn_in["wvp"].rearrange("(c p) j -> p c j", p=128)), 16)
                    p.inc("W", g.dma_start(out=brow[0:1, 0:1024], in_=ffn_in["bvp"]), 16)
                block.gpsimd(_gvp)
                vp_ld = p.val("W")

                def _t13(t, g1=ve18, g2=vp_ld):
                    p.wait(t, "VE", g1)
                    p.wait(t, "W", g2)
                    for piece in range(2):
                        for c in range(8):
                            t.matmul(PP[0:64, 512 * piece:512 * piece + 512],
                                     sfmb[:, 64 * c:64 * c + 64],
                                     wsl[:, 1024 * c + 512 * piece:1024 * c + 512 * piece + 512],
                                     start=(c == 0), stop=False)
                        ins = t.matmul(PP[0:64, 512 * piece:512 * piece + 512],
                                       onesr[0:1, 0:64], brow[0:1, 512 * piece:512 * piece + 512],
                                       start=False, stop=True)
                    p.inc("PE", ins)
                block.tensor(_t13)
                pe13 = p.val("PE")

                def _v19(v, g=pe13):
                    p.wait(v, "PE", g)
                    p.inc("VE", v.tensor_copy(sab[:, 0:1024], PP[0:64, 0:1024]))
                block.vector(_v19)
                ve19 = p.val("VE")

                if dbg_ffn:
                    def _dmp(g, g1=ve19):
                        p.wait(g, "VE", g1)
                        for nm, src in [("e_alpha", alpha[:]), ("e_sitot", sitot[:]),
                                        ("e_gh0", gh0_sb[:]), ("e_snew", sab[:, 1024:2048]),
                                        ("e_sv", sab[:, 0:1024]), ("e_c1", c1sb[:]),
                                        ("e_hfm", hfm[:]),
                                        ("e_y", sac("y")), ("e_alphaT", alphaT[:])]:
                            p.inc("ST", g.dma_start(out=dbg[nm], in_=src), 16)
                        p.wait(g, "ST")
                    block.gpsimd(_dmp)

                of_ve = {}
                for c in range(8):
                    def _t14(t, c=c, g1=ve19, g2=of_ve.get(c - 1, 0)):
                        p.wait(t, "VE", max(g1, g2))
                        for piece in range(4):
                            ins = t.matmul(PP[:, 512 * piece:512 * piece + 512],
                                           sab[:, 128 * c:128 * c + 128],
                                           alphaT[:, 512 * piece:512 * piece + 512],
                                           start=True, stop=True)
                        p.inc("PE", ins)
                    block.tensor(_t14)
                    pe14 = p.val("PE")

                    def _v20(v, c=c, g=pe14):
                        p.wait(v, "PE", g)
                        p.inc("VE", v.tensor_copy(xlf[:, 2048 * c:2048 * c + 2048], PP[:, 0:2048]))
                    block.vector(_v20)
                    of_ve[c] = p.val("VE")
                ve_f6 = p.val("VE")

                def _gwo(g, g1=pe13):
                    p.wait(g, "PE", g1)
                    p.inc("W", g.dma_start(
                        out=wsl.ap().rearrange("p (c j) -> p c j", c=8)[:, :, 0:1024],
                        in_=ffn_in["wo"].rearrange("(c p) j -> p c j", p=128)), 16)
                    p.inc("W", g.dma_start(out=brow[0:1, 0:1024], in_=ffn_in["bo"]), 16)
                block.gpsimd(_gwo)
                wo_ld = p.val("W")

                res_f = {}
                st_f = {}
                se_f1_last = max(ln_f.values())
                for tt in range(NT):
                    buf = tt % 2
                    obase = 2048 if buf == 0 else 3072

                    def _ldx(sy, tt=tt, buf=buf, g=res_f.get(tt - 2, 0), g2=se_f1_last):
                        p.wait(sy, "VE", g)
                        p.wait(sy, "SE", g2)
                        p.inc("LS", sy.dma_start(out=xt[:, 1024 * buf:1024 * buf + 1024],
                                                 in_=src_t[tt]), 16)
                    block.sync(_ldx)
                    ld_x = p.val("LS")

                    def _t15(t, tt=tt, obase=obase, g1=ve_f6, g2=wo_ld, g3=res_f.get(tt - 2, 0)):
                        p.wait(t, "VE", max(g1, g3))
                        p.wait(t, "W", g2)
                        for piece in range(2):
                            for c in range(8):
                                t.matmul(PP[:, obase + 512 * piece:obase + 512 * piece + 512],
                                         xlf[:, 2048 * c + 128 * tt:2048 * c + 128 * tt + 128],
                                         wsl[:, 1024 * c + 512 * piece:1024 * c + 512 * piece + 512],
                                         start=(c == 0), stop=False)
                            ins = t.matmul(PP[:, obase + 512 * piece:obase + 512 * piece + 512],
                                           onesr[:], brow[0:1, 512 * piece:512 * piece + 512],
                                           start=False, stop=True)
                        p.inc("PE", ins)
                    block.tensor(_t15)
                    pe15 = p.val("PE")

                    def _v21(v, tt=tt, buf=buf, obase=obase, g1=pe15, g2=ld_x,
                             g3=st_f.get(tt - 2, 0)):
                        p.wait(v, "PE", g1)
                        p.wait(v, "LS", g2)
                        p.wait(v, "ST", g3)
                        p.inc("VE", v.tensor_tensor(rt[:, 1024 * buf:1024 * buf + 1024],
                                                    PP[:, obase:obase + 1024],
                                                    xt[:, 1024 * buf:1024 * buf + 1024],
                                                    op=ALU.add))
                    block.vector(_v21)
                    res_f[tt] = p.val("VE")

                    def _st(g, tt=tt, buf=buf, ve_r=res_f[tt]):
                        p.wait(g, "VE", ve_r)
                        dst = out_t[tt] if write_out else dst_t[tt]
                        p.inc("ST", g.dma_start(out=dst, in_=rt[:, 1024 * buf:1024 * buf + 1024]), 16)
                    block.gpsimd(_st)
                    st_f[tt] = p.val("ST")

        def emit_attn_b(src_t, dst_t, write_out):
            emit_join()
            with ExitStack() as eb:
                def SBB(name, shape, dt):
                    return eb.enter_context(nc.sbuf_tensor(name, shape, dt))
                xlf = SBB("bxlf", [128, 8 * 2048], BF16)
                wqg = SBB("bwqg", [128, 8 * 1024], BF16)
                wqg2 = SBB("bwqg2", [128, 8 * 1024], BF16)
                wraw2 = SBB("bwraw", [128, 8 * 1024], BF16)
                ctxs = SBB("bctx", [128, 8 * 2048], BF16)
                qbuf = SBB("bqbuf", [128, 2 * 2048], BF16)
                qtb = SBB("bqtb", [128, 2 * 1024], BF16)
                ktile = SBB("bkt", [128, 1024], BF16)
                vte = SBB("bvte", [128, 1024], BF16)
                vto = SBB("bvto", [128, 1024], BF16)
                ffq32 = SBB("bffq32", [128, 1024], F32)
                ffq2 = SBB("bffq2", [128, 1024], F32)
                ffqb = SBB("bffqb", [128, 1024], BF16)
                Ctot = SBB("bCtot", [128, 512], F32)
                cof4 = SBB("bcof4", [128, 4 * 512], F32)
                Coff = SBB("bCoff", [128, 512], F32)
                Crun = SBB("bCrun", [128, 512], F32)
                Caug = SBB("bCaug", [128, 2 * 260], BF16)
                ksb = SBB("bksb", [32, 1024], BF16)
                kone = SBB("bkone", [32, 1], BF16)
                ktall = SBB("bktall", [4, 1024], BF16)
                ktrow = SBB("bktrow", [1, 1024], F32)
                koffb = SBB("bkoff", [1, 1024], BF16)
                kcb = SBB("bkcb", [32, 1024], BF16)
                kcT = SBB("bkcT", [128, 256], F32)
                mrb = SBB("bmrb", [128, 4], F32)
                xot = SBB("bxot", [128, 2 * 128], BF16)

                def wq_group(g_idx, dstw, src_ap):
                    pe_g = p.val("PE")

                    def _gl(g, g_idx=g_idx, pg=pe_g):
                        p.wait(g, "PE", pg)
                        p.inc("W", g.dma_start(
                            out=wraw2.ap().rearrange("p (c j) -> p c j", c=8),
                            in_=src_ap.rearrange("(c p) j -> p c j", p=128)[:, 8 * g_idx:8 * g_idx + 8, :]), 16)
                    block.gpsimd(_gl)
                    wld = p.val("W")
                    prev = p.val("VE")
                    for b8 in range(8):
                        def _tt(t, b8=b8, wld=wld, g=prev):
                            p.wait(t, "W", wld)
                            p.wait(t, "VE", g)
                            for j in range(8):
                                ins = t.matmul(PP[:, 3072 + 128 * j:3072 + 128 * j + 128],
                                               wraw2[:, 1024 * b8 + 128 * j:1024 * b8 + 128 * j + 128],
                                               c_ident[:], start=True, stop=True)
                            p.inc("PE", ins)
                        block.tensor(_tt)
                        peb = p.val("PE")

                        def _ve(v, b8=b8, peb=peb, dstw=dstw):
                            p.wait(v, "PE", peb)
                            dst = dstw.ap().rearrange("p (c j) -> p c j", c=8)[:, :, 128 * b8:128 * b8 + 128]
                            p.inc("VE", v.tensor_copy(
                                dst, PP[:, 3072:4096].rearrange("p (c j) -> p c j", c=8)))
                        block.vector(_ve)
                        prev = p.val("VE")
                    return p.val("VE")

                def _gm1(g):
                    g.memset(vte[:], 0.0)
                    g.memset(vto[:], 0.0)
                    g.memset(Caug[:], 0.0)
                    p.inc("PO", g.memset(kone[:], 1.0))
                block.gpsimd(_gm1)
                po_m1 = p.val("PO")

                ln_b = {}
                ev_b = {}
                t1pb = {}
                for tt in range(NT):
                    buf = tt % 2
                    tb = 0 if buf == 0 else 1024
                    xsrc = xt[:, 1024 * buf:1024 * buf + 1024]
                    xdst = xlt[:, 1024 * buf:1024 * buf + 1024]

                    def _ld(sy, tt=tt, xsrc=xsrc, g=ln_b.get(tt - 2, 0)):
                        p.wait(sy, "SE", g)
                        p.inc("LS", sy.dma_start(out=xsrc, in_=src_t[tt]), 16)
                    block.sync(_ld)
                    ln_b[tt] = emit_ln(xsrc, xdst, sslot=buf,
                                       extra_se_wait=("PE", t1pb.get(tt - 2, 0)),
                                       ld_guard=p.val("LS"))

                    def _t1(t, xdst=xdst, tb=tb, se=ln_b[tt], g=ev_b.get(tt - 2, 0)):
                        p.wait(t, "SE", se)
                        p.wait(t, "VE", g)
                        for c in range(8):
                            ins = t.matmul(PP[:, tb + 128 * c:tb + 128 * c + 128],
                                           xdst[:, 128 * c:128 * c + 128], c_ident[:],
                                           start=True, stop=True)
                        p.inc("PE", ins)
                    block.tensor(_t1)
                    t1pb[tt] = p.val("PE")

                    def _v3(v, tb=tb, tt=tt, pe=t1pb[tt]):
                        p.wait(v, "PE", pe)
                        dst = xlf.ap().rearrange("p (c j) -> p c j", c=8)[:, :, 128 * tt:128 * tt + 128]
                        p.inc("VE", v.tensor_copy(
                            dst, PP[:, tb:tb + 1024].rearrange("p (c j) -> p c j", c=8)))
                    block.vector(_v3)
                    ev_b[tt] = p.val("VE")
                ve_b1 = p.val("VE")
                if b_stop == "B1":
                    return

                ve_wq = wq_group(0, wqg, wqkv_b)

                q_se = {}
                q_st = {}
                for tt in range(NT):
                    def _tq(t, tt=tt, g1=ve_b1, g2=ve_wq, g3=q_se.get(tt - 1, 0)):
                        p.wait(t, "VE", max(g1, g2))
                        p.wait(t, "SE", g3)
                        for c in range(8):
                            for nn in range(2):
                                ins = t.matmul(PP[:, 2048 + 512 * nn:2560 + 512 * nn],
                                               xlf[:, 2048 * c + 128 * tt:2048 * c + 128 * tt + 128],
                                               wqg[:, 1024 * c + 512 * nn:1024 * c + 512 * nn + 512],
                                               start=(c == 0), stop=(c == 7))
                        p.inc("PE", ins)
                    block.tensor(_tq)
                    peq = p.val("PE")

                    def _vq(v, peq=peq):
                        p.wait(v, "PE", peq)
                        qv = PP[:, 2048:3072].rearrange("p (h e) -> p h e", h=H)
                        p.inc("VE", v.tensor_reduce(stats[:, 16:32], qv,
                                                    axis=mybir.AxisListType.X, op=ALU.max))
                        p.wait(v, "VE")
                        mb = stats[:, 16:32].unsqueeze(2).broadcast_to([128, 16, 64])
                        p.inc("VE", v.tensor_tensor(
                            ffq32.ap().rearrange("p (h e) -> p h e", h=H), qv, mb, op=ALU.subtract))
                    block.vector(_vq)
                    veq = p.val("VE")

                    def _sq(sc, veq=veq):
                        p.wait(sc, "VE", veq)
                        p.inc("SE", sc.activation(ffq2[:], ffq32[:], AF.Exp,
                                                  bias=cst[:, 2:3], scale=1.0))
                    block.scalar(_sq)
                    q_se[tt] = p.val("SE")

                    def _vq2(v, g=q_se[tt]):
                        p.wait(v, "SE", g)
                        ev = ffq2.ap().rearrange("p (h e) -> p h e", h=H)
                        p.inc("VE", v.tensor_reduce(stats[:, 16:32], ev,
                                                    axis=mybir.AxisListType.X, op=ALU.add))
                        p.wait(v, "VE")
                        p.inc("VE", v.reciprocal(stats[:, 32:48], stats[:, 16:32]))
                        p.wait(v, "VE")
                        p.inc("VE", v.tensor_scalar(stats[:, 32:48], stats[:, 32:48],
                                                    0.125, None, op0=ALU.mult))
                        p.wait(v, "VE")
                        rb2 = stats[:, 32:48].unsqueeze(2).broadcast_to([128, 16, 64])
                        p.inc("VE", v.tensor_tensor(ffqb.ap().rearrange("p (h e) -> p h e", h=H),
                                                    ev, rb2, op=ALU.mult))
                    block.vector(_vq2)
                    veq2 = p.val("VE")

                    def _tq2(t, g=veq2):
                        p.wait(t, "VE", g)
                        for c in range(8):
                            ins = t.matmul(PP[:, 3072 + 128 * c:3072 + 128 * c + 128],
                                           ffqb[:, 128 * c:128 * c + 128], c_ident[:],
                                           start=True, stop=True)
                        p.inc("PE", ins)
                    block.tensor(_tq2)
                    peq2 = p.val("PE")

                    def _vq3(v, tt=tt, peq2=peq2, g=q_st.get(tt - 2, 0)):
                        p.wait(v, "PE", peq2)
                        p.wait(v, "ST", g)
                        p.inc("VE", v.tensor_copy(qtb[:, 1024 * (tt % 2):1024 * (tt % 2) + 1024],
                                                  PP[:, 3072:4096]))
                    block.vector(_vq3)
                    veq3 = p.val("VE")

                    def _gsp(g, tt=tt, veq3=veq3):
                        p.wait(g, "VE", veq3)
                        src = qtb[:, 1024 * (tt % 2):1024 * (tt % 2) + 1024].rearrange(
                            "p (c j) -> p c j", c=8)
                        dst = qsp.ap().rearrange("a p (t j) -> p a t j", t=16)[:, :, tt, :]
                        p.inc("ST", g.dma_start(out=dst, in_=src), 16)
                    block.gpsimd(_gsp)
                    q_st[tt] = p.val("ST")
                se_bq = p.val("SE")
                st_bq = p.val("ST")
                if b_stop == "B1b":
                    return

                ve_wk = wq_group(1, wqg, wqkv_b)
                ve_wv = wq_group(2, wqg2, wqkv_b)
                if b_stop == "B2w":
                    return
                kv_se = {}
                kv_ve = {}
                vto_ve = {}
                pe_k_cell = [0]
                for tt in range(NT):
                    def _tkv(t, tt=tt, g1=ve_wk, g2=ve_wv, g3=kv_se.get(tt - 1, 0)):
                        p.wait(t, "VE", max(g1, g2))
                        p.wait(t, "SE", g3)
                        for c in range(8):
                            for nn in range(2):
                                ins = t.matmul(PP[:, 2048 + 512 * nn:2560 + 512 * nn],
                                               xlf[:, 2048 * c + 128 * tt:2048 * c + 128 * tt + 128],
                                               wqg[:, 1024 * c + 512 * nn:1024 * c + 512 * nn + 512],
                                               start=(c == 0), stop=(c == 7))
                        p.inc("PE", ins)
                        pe_k_cell[0] = p.val("PE")
                        for c in range(8):
                            for nn in range(2):
                                ins = t.matmul(PP[:, 3072 + 512 * nn:3584 + 512 * nn],
                                               xlf[:, 2048 * c + 128 * tt:2048 * c + 128 * tt + 128],
                                               wqg2[:, 1024 * c + 512 * nn:1024 * c + 512 * nn + 512],
                                               start=(c == 0), stop=(c == 7))
                        p.inc("PE", ins)
                    block.tensor(_tkv)
                    pekv = p.val("PE")
                    if b_stop == "B2y":
                        kv_se[tt] = p.val("SE")
                        kv_ve[tt] = p.val("VE")
                        continue

                    def _skv(sc, pekv=pekv, pek=pe_k_cell[0], g=kv_ve.get(tt - 1, 0), g2=po_m1):
                        p.wait(sc, "PE", pek)
                        p.wait(sc, "VE", g)
                        p.wait(sc, "PO", g2)
                        sc.activation(ktile[:], PP[:, 2048:3072], AF.Exp,
                                      bias=cst[:, 2:3], scale=1.0)
                        p.wait(sc, "PE", pekv)
                        p.inc("SE", sc.activation(vte[0:64, :], PP[0:64, 3072:4096],
                                                  AF.Identity, bias=cst[0:64, 2:3], scale=1.0))
                    block.scalar(_skv)
                    kv_se[tt] = p.val("SE")

                    def _vkv(v, pekv=pekv):
                        p.wait(v, "PE", pekv)
                        p.inc("VE", v.tensor_copy(vto[64:128, :], PP[64:128, 3072:4096]))
                    block.vector(_vkv)
                    vto_ve[tt] = p.val("VE")
                    if b_stop == "B2x":
                        kv_ve[tt] = p.val("VE")
                        continue

                    def _tks(t, tt=tt, g=kv_se[tt], gv=vto_ve[tt]):
                        p.wait(t, "SE", g)
                        p.wait(t, "VE", gv)
                        if b_stop != "B2t2":
                            for nn in range(2):
                                t.matmul(PP[0:32, 512 * nn:512 * nn + 512],
                                         c_bones[:, 32 * tt:32 * tt + 32],
                                         ktile[:, 512 * nn:512 * nn + 512],
                                         start=(tt == 0), stop=(tt == NT - 1),
                                         skip_group_check=True)
                        if b_stop == "B2t1":
                            p.inc("PE", t.matmul(PP[0:64, 1024:1088], ktile[0:64, 0:64],
                                                 vtile[0:64, 0:64], start=True, stop=True))
                            return
                        for hh in range(H):
                            hp, a = hh // 2, hh % 2
                            for par in range(2):
                                vsrc = vte if par == 0 else vto
                                ins = t.matmul(
                                    PP[64 * a:64 * a + 64,
                                       1024 + 128 * hp + 64 * par:1024 + 128 * hp + 64 * par + 64],
                                    ktile[:, 64 * hh:64 * hh + 64],
                                    vsrc[:, 64 * hh:64 * hh + 64],
                                    start=True, stop=True)
                        p.inc("PE", ins)
                    block.tensor(_tks)
                    peks = p.val("PE")
                    if b_stop in ("B2t1", "B2t2"):
                        kv_ve[tt] = p.val("VE")
                        continue

                    def _vks(v, tt=tt, peks=peks):
                        p.wait(v, "PE", peks)
                        src = PP[:, 1024:2048].rearrange("p (hp x e) -> p hp x e", hp=8, x=2)
                        dst = ctxs.ap().rearrange("p (hp b e) -> p hp b e", hp=8, b=32)[:, :, 2 * tt:2 * tt + 2, :]
                        p.inc("VE", v.tensor_copy(dst, src))
                        for par in range(2):
                            p.wait(v, "VE")
                            if tt == 0 and par == 0:
                                p.inc("VE", v.tensor_copy(
                                    Ctot.ap().rearrange("p (hp e) -> p hp e", hp=8),
                                    src[:, :, 0, :]))
                            else:
                                p.inc("VE", v.scalar_tensor_tensor(
                                    out=Ctot.ap().rearrange("p (hp e) -> p hp e", hp=8),
                                    in0=src[:, :, par, :],
                                    scalar=1.0,
                                    in1=Ctot.ap().rearrange("p (hp e) -> p hp e", hp=8),
                                    op0=ALU.mult, op1=ALU.add))
                    block.vector(_vks)
                    kv_ve[tt] = p.val("VE")
                pe_b2 = p.val("PE")
                ve_b2 = p.val("VE")
                if b_stop in ("B2", "B2x", "B2y", "B2t1", "B2t2"):
                    return

                def _vkt(v, g=pe_b2):
                    p.wait(v, "PE", g)
                    p.inc("VE", v.tensor_copy(ksb[:], PP[0:32, 0:1024]))
                block.vector(_vkt)
                ve_ksb = p.val("VE")

                def _tkt(t, g1=ve_ksb, g2=po_m1):
                    p.wait(t, "VE", g1)
                    p.wait(t, "PO", g2)
                    for nn in range(2):
                        ins = t.matmul(PP[0:1, 2048 + 512 * nn:2560 + 512 * nn],
                                       kone[:], ksb[:, 512 * nn:512 * nn + 512],
                                       start=True, stop=True)
                    p.inc("PE", ins)
                block.tensor(_tkt)
                pe_kt = p.val("PE")

                def _vkt2(v, g1=pe_kt, g2=ve_b2):
                    p.wait(v, "PE", g1)
                    p.wait(v, "VE", g2)
                    p.inc("VE", v.tensor_copy(ktrow[:], PP[0:1, 2048:3072]))
                block.vector(_vkt2)
                ve_ag = p.val("VE")

                def _gag(g, g1=ve_ag):
                    p.wait(g, "VE", g1)
                    p.inc("ST", g.dma_start(out=ag2_in[0:128, :], in_=Ctot[:]), 16)
                    p.inc("ST", g.dma_start(
                        out=ag2_in.rearrange("r c -> (r c)").unsqueeze(0)[:, 128 * 512:130 * 512],
                        in_=ktrow[:]), 16)
                    p.wait(g, "ST")
                    g.collective_compute(
                        "AllGather", ALU.bypass, replica_groups=[[0, 1, 2, 3], [4, 5, 6, 7]],
                        ins=[ag2_in.ap().opt()], outs=[ag2_out.ap().opt()]).then_inc(p.sem["C2"], 1)
                    g.wait_ge(p.sem["C2"], 1)
                    for r in range(4):
                        p.inc("LD", g.dma_start(
                            out=cof4[:, 512 * r:512 * r + 512],
                            in_=ag2_out.rearrange("(r x) c -> r x c", r=4)[r, 0:128, :]), 16)
                    p.inc("LD", g.dma_start(
                        out=ktall[:],
                        in_=ag2_out.rearrange("(r x) c -> r (x c)", r=4)[:, 128 * 512:130 * 512]), 16)
                block.gpsimd(_gag)
                ld_ag = p.val("LD")

                def _tmr(t, g=ld_ag):
                    p.wait(t, "LD", g)
                    p.inc("PE", t.matmul(PP[:, 2048:2052], onesr[:], c_pmR[:],
                                         start=True, stop=True))
                block.tensor(_tmr)
                pe_mr = p.val("PE")

                def _vmr(v, g=pe_mr):
                    p.wait(v, "PE", g)
                    v.memset(Coff[:], 0.0)
                    p.inc("VE", v.tensor_copy(mrb[:], PP[:, 2048:2052]))
                    for r in range(4):
                        p.wait(v, "VE")
                        p.inc("VE", v.scalar_tensor_tensor(
                            out=Coff[:], in0=cof4[:, 512 * r:512 * r + 512],
                            scalar=mrb[:, r:r + 1], in1=Coff[:],
                            op0=ALU.mult, op1=ALU.add))
                    p.wait(v, "VE")
                    p.inc("VE", v.tensor_copy(Crun[:], Coff[:]))
                block.vector(_vmr)
                ve_co = p.val("VE")

                def _tko(t, g=ve_co):
                    p.wait(t, "VE", g)
                    for nn in range(2):
                        ins = t.matmul(PP[0:1, 2048 + 512 * nn:2560 + 512 * nn],
                                       c_pmT[:], ktall[:, 512 * nn:512 * nn + 512],
                                       start=True, stop=True)
                    p.inc("PE", ins)
                block.tensor(_tko)
                pe_ko = p.val("PE")

                def _vko(v, g=pe_ko):
                    p.wait(v, "PE", g)
                    p.inc("VE", v.tensor_copy(koffb[:], PP[0:1, 2048:3072]))
                block.vector(_vko)
                ve_ko = p.val("VE")

                def _tkc(t, g=ve_ko):
                    p.wait(t, "VE", g)
                    for nn in range(2):
                        t.matmul(PP[0:32, 512 * nn:512 * nn + 512], c_ltriT[:],
                                 ksb[:, 512 * nn:512 * nn + 512], start=True, stop=False)
                        ins = t.matmul(PP[0:32, 512 * nn:512 * nn + 512], onesr[0:1, 0:32],
                                       koffb[0:1, 512 * nn:512 * nn + 512],
                                       start=False, stop=True)
                    p.inc("PE", ins)
                block.tensor(_tkc)
                pe_kc = p.val("PE")

                def _vkc(v, g=pe_kc):
                    p.wait(v, "PE", g)
                    p.inc("VE", v.tensor_copy(kcb[:], PP[0:32, 0:1024]))
                block.vector(_vkc)
                ve_kcb = p.val("VE")

                def _tkc2(t, g=ve_kcb):
                    p.wait(t, "VE", g)
                    for c in range(8):
                        ins = t.matmul(PP[:, 3072 + 32 * c:3072 + 32 * c + 32],
                                       kcb[:, 128 * c:128 * c + 128], c_ident[0:32, 0:32],
                                       start=True, stop=True)
                    p.inc("PE", ins)
                block.tensor(_tkc2)
                pe_kc2 = p.val("PE")

                def _vkc2(v, g=pe_kc2):
                    p.wait(v, "PE", g)
                    p.inc("VE", v.tensor_copy(kcT[:], PP[:, 3072:3328]))
                block.vector(_vkc2)
                ve_kcT = p.val("VE")
                if b_stop == "B2b":
                    return

                qld = {}
                xo_ve = {}
                pe_mm = {}
                for hp in range(8):
                    bufh = hp % 2

                    def _gq(g, hp=hp, bufh=bufh, g1=xo_ve.get((hp - 2, NT - 1), 0),
                            g2=st_bq):
                        p.wait(g, "VE", g1)
                        p.wait(g, "ST", g2)
                        p.inc("LD", g.dma_start(out=qbuf[:, 2048 * bufh:2048 * bufh + 2048],
                                                in_=qsp.ap()[hp]), 16)
                    block.gpsimd(_gq)
                    qld[hp] = p.val("LD")

                    for tt in range(NT):
                        sl = 260 * (tt % 2)
                        abase = 2048 if tt % 2 == 0 else 2560
                        xbase = 3072 if tt % 2 == 0 else 3584

                        def _vca(v, hp=hp, tt=tt, sl=sl, g1=max(ve_kcT, ve_co),
                                 gpe=pe_mm.get((hp, tt - 2), pe_mm.get((hp - 1, NT + tt - 2), 0)),
                                 gpo=po_m1):
                            p.wait(v, "VE", g1)
                            p.wait(v, "PE", gpe)
                            p.wait(v, "PO", gpo)
                            hc = slice(64 * hp, 64 * hp + 64)
                            v.tensor_copy(Caug[0:64, sl + 0:sl + 64], Crun[0:64, hc])
                            v.tensor_copy(Caug[64:128, sl + 65:sl + 129], Crun[64:128, hc])
                            v.tensor_copy(Caug[0:64, sl + 64:sl + 65],
                                          kcT[0:64, 32 * hp + 2 * tt:32 * hp + 2 * tt + 1])
                            p.inc("PO", v.tensor_copy(
                                Caug[64:128, sl + 129:sl + 130],
                                kcT[64:128, 32 * hp + 2 * tt:32 * hp + 2 * tt + 1]))
                            p.wait(v, "PO")
                            p.inc("PO", v.tensor_tensor(
                                Crun[:, hc],
                                ctxs[:, 2048 * hp + 128 * tt:2048 * hp + 128 * tt + 64],
                                Crun[:, hc], op=ALU.add))
                            p.wait(v, "PO")
                            v.tensor_copy(Caug[0:64, sl + 130:sl + 194], Crun[0:64, hc])
                            v.tensor_copy(Caug[64:128, sl + 195:sl + 259], Crun[64:128, hc])
                            v.tensor_copy(Caug[0:64, sl + 194:sl + 195],
                                          kcT[0:64, 32 * hp + 2 * tt + 1:32 * hp + 2 * tt + 2])
                            p.inc("PO", v.tensor_copy(
                                Caug[64:128, sl + 259:sl + 260],
                                kcT[64:128, 32 * hp + 2 * tt + 1:32 * hp + 2 * tt + 2]))
                            p.wait(v, "PO")
                            p.inc("PO", v.tensor_tensor(
                                Crun[:, hc],
                                ctxs[:, 2048 * hp + 128 * tt + 64:2048 * hp + 128 * tt + 128],
                                Crun[:, hc], op=ALU.add))
                        block.gpsimd(_vca)
                        vca = p.val("PO")

                        def _tmm(t, hp=hp, tt=tt, sl=sl, abase=abase, g1=vca,
                                 g2=qld[hp], bufh=bufh, g3=xo_ve.get((hp, tt - 2), 0)):
                            p.wait(t, "PO", g1)
                            p.wait(t, "VE", g3)
                            p.wait(t, "LD", g2)
                            qs = qbuf[:, 2048 * bufh + 128 * tt:2048 * bufh + 128 * tt + 64]
                            qs2 = qbuf[:, 2048 * bufh + 128 * tt + 64:2048 * bufh + 128 * tt + 128]
                            for a in range(2):
                                t.matmul(PP[0:64, abase + 65 * a:abase + 65 * a + 65],
                                         qs, Caug[:, sl + 65 * a:sl + 65 * a + 65],
                                         start=True, stop=True)
                            for a in range(2):
                                ins = t.matmul(PP[64:128, abase + 65 * a:abase + 65 * a + 65],
                                               qs2, Caug[:, sl + 130 + 65 * a:sl + 130 + 65 * a + 65],
                                               start=True, stop=True)
                            p.inc("PE", ins)
                        block.tensor(_tmm)
                        pe_mm[(hp, tt)] = p.val("PE")

                        def _vda(v, tt=tt, abase=abase, g=pe_mm[(hp, tt)]):
                            p.wait(v, "PE", g)
                            den = PP[:, abase:abase + 130].rearrange("p (a e) -> p a e", a=2)[:, :, 64:65]
                            p.inc("VE", v.tensor_scalar(stats[:, 16:18],
                                                        den.rearrange("p a e -> p (a e)"),
                                                        1e-6, None, op0=ALU.max))
                            p.wait(v, "VE")
                            p.inc("VE", v.reciprocal(stats[:, 18:20], stats[:, 16:18]))
                            p.wait(v, "VE")
                            att = PP[:, abase:abase + 130].rearrange("p (a e) -> p a e", a=2)[:, :, 0:64]
                            db = stats[:, 18:20].unsqueeze(2).broadcast_to([128, 2, 64])
                            p.inc("VE", v.tensor_tensor(
                                xot[:, 128 * (tt % 2):128 * (tt % 2) + 128].rearrange(
                                    "p (a e) -> p a e", a=2),
                                att, db, op=ALU.mult))
                        block.vector(_vda)
                        vda = p.val("VE")

                        def _txo(t, tt=tt, xbase=xbase, g=vda):
                            p.wait(t, "VE", g)
                            p.inc("PE", t.matmul(PP[:, xbase:xbase + 128],
                                                 xot[:, 128 * (tt % 2):128 * (tt % 2) + 128],
                                                 c_ident[:], start=True, stop=True))
                        block.tensor(_txo)
                        ptx = p.val("PE")

                        def _vxo(v, hp=hp, tt=tt, xbase=xbase, g=ptx):
                            p.wait(v, "PE", g)
                            p.inc("VE", v.tensor_copy(
                                xlf[:, 2048 * hp + 128 * tt:2048 * hp + 128 * tt + 128],
                                PP[:, xbase:xbase + 128]))
                        block.vector(_vxo)
                        xo_ve[(hp, tt)] = p.val("VE")
                ve_b3 = p.val("VE")
                if b_stop == "B3":
                    return

                ve_wo = wq_group(0, wqg, wout_b)

                res_b = {}
                st_b = {}
                for tt in range(NT):
                    buf = tt % 2
                    obase = 0 if buf == 0 else 1024

                    def _ldx(sy, tt=tt, buf=buf, g=res_b.get(tt - 2, 0), g2=max(ln_b.values())):
                        p.wait(sy, "VE", g)
                        p.wait(sy, "SE", g2)
                        p.inc("LS", sy.dma_start(out=xt[:, 1024 * buf:1024 * buf + 1024],
                                                 in_=src_t[tt]), 16)
                    block.sync(_ldx)
                    ld_x = p.val("LS")

                    def _t15(t, tt=tt, obase=obase, g1=max(ve_wo, ve_b3), g2=res_b.get(tt - 2, 0)):
                        p.wait(t, "VE", max(g1, g2))
                        for c in range(8):
                            for nn in range(2):
                                ins = t.matmul(PP[:, obase + 512 * nn:obase + 512 * nn + 512],
                                               xlf[:, 2048 * c + 128 * tt:2048 * c + 128 * tt + 128],
                                               wqg[:, 1024 * c + 512 * nn:1024 * c + 512 * nn + 512],
                                               start=(c == 0), stop=(c == 7))
                        p.inc("PE", ins)
                    block.tensor(_t15)
                    pe15 = p.val("PE")

                    def _v21(v, buf=buf, obase=obase, g1=pe15, g2=ld_x,
                             g3=st_b.get(tt - 2, 0)):
                        p.wait(v, "PE", g1)
                        p.wait(v, "LS", g2)
                        p.wait(v, "ST", g3)
                        p.inc("VE", v.tensor_tensor(rt[:, 1024 * buf:1024 * buf + 1024],
                                                    PP[:, obase:obase + 1024],
                                                    xt[:, 1024 * buf:1024 * buf + 1024],
                                                    op=ALU.add))
                    block.vector(_v21)
                    res_b[tt] = p.val("VE")

                    def _st(g, tt=tt, buf=buf, ve_r=res_b[tt]):
                        p.wait(g, "VE", ve_r)
                        dst = out_t[tt] if write_out else dst_t[tt]
                        p.inc("ST", g.dma_start(out=dst, in_=rt[:, 1024 * buf:1024 * buf + 1024]), 16)
                    block.gpsimd(_st)
                    st_b[tt] = p.val("ST")

        if n_sub >= 2:
            emit_ffn(xres_t, xres_t, first=True, write_out=(n_sub == 2))
        if n_sub >= 3:
            emit_attn_b(xres_t, xres_t, write_out=(n_sub == 3))
        if n_sub >= 4:
            emit_ffn(xres_t, xres_t, first=True, write_out=True, ccs="C3")

        def _fin(g):
            p.wait(g, "ST")
        block.gpsimd(_fin)

    return nc


_NC_CACHE = {}


def kernel(**inputs):
    x = np.ascontiguousarray(inputs["L0_attn_a_x"], dtype=np.float32)
    n_sub = 4
    nc = build_nc(n_sub=n_sub)
    in_maps = []
    for c in range(8):
        b, qt = c // 4, c % 4
        s0 = qt * TC
        xhh = np.zeros((FR, D), np.float32)
        lo = max(0, s0 - HALO)
        xhh[HALO - (s0 - lo):] = x[b, lo:s0 + TC]
        m = dict(xh=xhh,
                 w_qkv=np.ascontiguousarray(inputs["L0_attn_a_w_qkv"], dtype=np.float32),
                 w_out=np.ascontiguousarray(inputs["L0_attn_a_w_out"], dtype=np.float32))
        m["wqkv_b"] = np.ascontiguousarray(inputs["L0_attn_b_W_qkv"], dtype=np.float32)
        m["wout_b"] = np.ascontiguousarray(inputs["L0_attn_b_W_out"], dtype=np.float32)
        for nm in ["slot_init", "w1", "b1", "w2", "b2", "tau", "Wih", "Whh", "bih",
                   "bhh", "wq", "bq", "wk", "bk", "wv", "bv", "ln_g", "ln_b",
                   "wvp", "bvp", "wo", "bo"]:
            a = np.ascontiguousarray(inputs["L0_ffn_" + nm], dtype=np.float32)
            if a.ndim == 1:
                a = a.reshape(1, -1)
            m["f_" + nm] = a
        m.update(host_consts(qt))
        in_maps.append(m)
    res = run_bass_kernel_spmd(nc, in_maps, core_ids=list(range(8)))
    outs = [res.results[c]["out"] for c in range(8)]
    y = np.stack([np.concatenate(outs[0:4], axis=0),
                  np.concatenate(outs[4:8], axis=0)], axis=0)
    return y


# version 13 (final)
# speedup vs baseline: 1.0243x; 1.0031x over previous
import numpy as np
import ml_dtypes
from contextlib import ExitStack
import concourse.bass as bass
import concourse.mybir as mybir
from concourse.bass_utils import run_bass_kernel_spmd

F32 = mybir.dt.float32
BF16 = mybir.dt.bfloat16
AF = mybir.ActivationFunctionType
ALU = mybir.AluOpType

D = 1024
H = 16
HD = 64
WIN = 512
TC = 2048
NT = TC // 128
HALO = 512
FR = HALO + TC
ROPE_BASE = 10000.0
BF = ml_dtypes.bfloat16


def host_consts(qt):
    c = {}
    c["ident"] = np.eye(128, dtype=BF)
    R = np.zeros((128, 128), np.float32)
    for a in range(2):
        for f in range(0, HD, 2):
            R[a * HD + f, a * HD + f + 1] = -1.0
            R[a * HD + f + 1, a * HD + f] = 1.0
    c["r2ht"] = np.ascontiguousarray(R.T).astype(BF)
    pos0 = qt * TC - HALO
    pos = np.arange(pos0, pos0 + FR, dtype=np.float64)
    pos = np.maximum(pos, 0.0)
    inv = 1.0 / ROPE_BASE ** (np.arange(0, HD, 2, dtype=np.float64) / HD)
    freqs = pos[None, :] * inv[:, None]
    cos = np.repeat(np.cos(freqs), 2, axis=0)
    sin = np.repeat(np.sin(freqs), 2, axis=0)
    c["cosfm"] = np.concatenate([cos, cos], axis=0).astype(BF)
    c["sinfm"] = np.concatenate([sin, sin], axis=0).astype(BF)

    def band(w):
        ccv = np.arange(128)[:, None]
        rv = np.arange(128)[None, :]
        dj = 128 * w + ccv - rv
        return ((dj > 0) & (dj <= WIN)).astype(np.float32)
    c["maskB"] = np.concatenate([band(0), band(4)], axis=1).astype(BF)
    mA = np.zeros((128, 4, 5, 128), np.float32)
    for t in range(4):
        for w in range(5):
            m = band(w)
            if qt == 0 and (t + w) <= 3:
                m = np.zeros_like(m)
            mA[:, t, w, :] = m
    c["maskA"] = np.ascontiguousarray(mA.reshape(128, 4 * 5 * 128)).astype(BF)
    bo = np.zeros((128, 16, 32), np.float32)
    for t in range(16):
        bo[0:64, t, 2 * t] = 1.0
        bo[64:128, t, 2 * t + 1] = 1.0
    c["bucket_ones"] = np.ascontiguousarray(bo.reshape(128, 512)).astype(BF)
    c["ltriT"] = np.triu(np.ones((32, 32), np.float32), 1).astype(BF)
    pm = (np.arange(4) < qt).astype(np.float32)
    c["pmT"] = pm.reshape(4, 1).astype(BF)
    c["pmR"] = pm.reshape(1, 4).astype(BF)
    return c


class P:
    def __init__(self):
        self.sem = {}
        self.cnt = {}

    def add(self, name, handle):
        self.sem[name] = handle
        self.cnt[name] = 0

    def inc(self, name, instr, amt=1):
        instr.then_inc(self.sem[name], amt)
        self.cnt[name] += amt
        return self.cnt[name]

    def val(self, name):
        return self.cnt[name]

    def wait(self, eng, name, v=None):
        eng.wait_ge(self.sem[name], self.cnt[name] if v is None else v)


def build_nc(n_sub=4, dbg_dump=False, dbg_half0=False, dbg_ffn=False, b_stop=None):
    nc = bass.Bass("TRN2", target_bir_lowering=False, num_devices=8)

    xh = nc.dram_tensor("xh", [FR, D], F32, kind="ExternalInput").ap()
    w_qkv = nc.dram_tensor("w_qkv", [D, 3 * D], F32, kind="ExternalInput").ap()
    w_out = nc.dram_tensor("w_out", [D, D], F32, kind="ExternalInput").ap()
    cons = {}
    for nm, shape in [("ident", [128, 128]), ("r2ht", [128, 128]),
                      ("cosfm", [128, FR]), ("sinfm", [128, FR]),
                      ("maskB", [128, 256]), ("maskA", [128, 4 * 5 * 128]),
                      ("bucket_ones", [128, 512]), ("ltriT", [32, 32]), ("pmT", [4, 1]),
                      ("pmR", [1, 4])]:
        cons[nm] = nc.dram_tensor(nm, shape, BF16, kind="ExternalInput").ap()
    wqkv_b = nc.dram_tensor("wqkv_b", [3 * D, D], F32, kind="ExternalInput").ap()
    wout_b = nc.dram_tensor("wout_b", [D, D], F32, kind="ExternalInput").ap()
    ffn_in = {}
    for nm, shape in [("slot_init", [64, D]), ("w1", [2 * D, 512]), ("b1", [1, 512]),
                      ("w2", [512, 64]), ("b2", [1, 64]), ("tau", [1, 1]),
                      ("Wih", [3 * D, D]), ("Whh", [3 * D, D]), ("bih", [1, 3 * D]),
                      ("bhh", [1, 3 * D]), ("wq", [D, 256]), ("bq", [1, 256]),
                      ("wk", [D, 256]), ("bk", [1, 256]), ("wv", [D, D]), ("bv", [1, D]),
                      ("ln_g", [1, D]), ("ln_b", [1, D]), ("wvp", [D, D]), ("bvp", [1, D]),
                      ("wo", [D, D]), ("bo", [1, D])]:
        ffn_in[nm] = nc.dram_tensor("f_" + nm, shape, F32, kind="ExternalInput").ap()
    out = nc.dram_tensor("out", [TC, D], F32, kind="ExternalOutput").ap()
    xres = nc.dram_tensor("xres", [TC, D], F32).ap()
    xlnsp = nc.dram_tensor("xlnsp", [TC, D], BF16).ap()
    cc_in = nc.dram_tensor("cc_in", [64, 1028], F32)
    cc_out = nc.dram_tensor("cc_out", [64, 1028], F32)
    qsp = nc.dram_tensor("qsp", [8, 128, 2048], BF16)
    ag2_in = nc.dram_tensor("ag2_in", [132, 512], F32)
    ag2_out = nc.dram_tensor("ag2_out", [528, 512], F32)
    dbg = {}
    if dbg_ffn:
        for nm, shape in [("e_alpha", [128, 16 * 64]), ("e_sitot", [64, 1028]),
                          ("e_gh0", [64, 3072]), ("e_snew", [64, 1024]),
                          ("e_sv", [64, 1024]), ("e_c1", [128, 4]),
                          ("e_hfm", [128, 4 * 2048]), ("e_xlf", [128, 8 * 2048]),
                          ("e_xres", [256, 1024]),
                          ("e_y", [64, 1024]), ("e_alphaT", [64, 2048])]:
            dbg[nm] = nc.dram_tensor(nm, shape, F32, kind="ExternalOutput").ap()
    if dbg_dump:
        for nm, shape in [("d_xlnf", [128, 8 * 1536]), ("d_qfm", [128, 8 * 1024]),
                          ("d_kfm", [128, 8 * 1536]), ("d_vaug", [128, 12 * H * 65]),
                          ("d_probs", [128, 8 * 640]), ("d_aofm", [128, 8 * 1024]),
                          ("d_aotm", [128, 8 * 64])]:
            dbg[nm] = nc.dram_tensor(nm, shape, F32, kind="ExternalOutput").ap()

    xh_t = xh.rearrange("(t p) d -> t p d", p=128)
    xlnsp_t = xlnsp.rearrange("(t p) d -> t p d", p=128)
    xres_t = xres.rearrange("(t p) d -> t p d", p=128)
    out_t = out.rearrange("(t p) d -> t p d", p=128)

    p = P()

    with ExitStack() as _es:
        def SB(name, shape, dt):
            return _es.enter_context(nc.sbuf_tensor(name, shape, dt))
        c_ident = SB("c_ident", [128, 128], BF16)
        c_r2ht = SB("c_r2ht", [128, 128], BF16)
        c_cos = SB("c_cos", [128, FR], BF16)
        c_sin = SB("c_sin", [128, FR], BF16)
        c_maskB = SB("c_maskB", [128, 256], BF16)
        c_maskA = SB("c_maskA", [128, 4 * 5 * 128], BF16)
        cst = SB("cst", [128, 8], F32)
        stats = SB("stats", [128, 96], F32)
        xt = SB("xt", [128, 2 * 1024], F32)
        xlt = SB("xlt", [128, 2 * 1024], BF16)
        rt = SB("rt", [128, 2 * 1024], F32)
        onesr = SB("onesr", [1, 128], BF16)
        c_bones = SB("c_bones", [128, 512], BF16)
        c_ltriT = SB("c_ltriT", [32, 32], BF16)
        c_pmT = SB("c_pmT", [4, 1], BF16)
        c_pmR = SB("c_pmR", [1, 4], BF16)
        es_a = ExitStack()
        def SBA(name, shape, dt):
            return es_a.enter_context(nc.sbuf_tensor(name, shape, dt))
        vw = SBA("vw", [128, 8 * D], BF16)
        wqk = SBA("wqk", [128, 2 * 2048], BF16)
        wo_b = SBA("wo_b", [128, 8 * D], BF16)
        xlnf = SBA("xlnf", [128, 8 * 1536], BF16)
        qfm = SBA("qfm", [128, 8 * 1024], BF16)
        kfm = SBA("kfm", [128, 8 * 1536], BF16)
        vaug = SBA("vaug", [128, 12 * H * 65], BF16)
        probs = SBA("probs", [128, 8 * 640], BF16)
        aofm = SBA("aofm", [128, 8 * 1024], BF16)
        aotm = SBA("aotm", [128, 8 * 64], BF16)
        qkbf = SBA("qkbf", [128, 2560], BF16)
        ubuf = SBA("ubuf", [128, 2560], BF16)
        vbuf = SBA("vbuf", [128, 1536], BF16)
        PP = _es.enter_context(nc.psum_tensor("PP", [128, 4096], F32))
        hLD = _es.enter_context(nc.semaphore("sLD"))
        hW = _es.enter_context(nc.semaphore("sW"))
        hPE = _es.enter_context(nc.semaphore("sPE"))
        hVE = _es.enter_context(nc.semaphore("sVE"))
        hSE = _es.enter_context(nc.semaphore("sSE"))
        hPO = _es.enter_context(nc.semaphore("sPO"))
        hST = _es.enter_context(nc.semaphore("sST"))
        hC1 = _es.enter_context(nc.semaphore("sC1"))
        hC2 = _es.enter_context(nc.semaphore("sC2"))
        hC3 = _es.enter_context(nc.semaphore("sC3"))
        hLS = _es.enter_context(nc.semaphore("sLS"))
        block = _es.enter_context(nc.Block())
        for nm, h in [("LD", hLD), ("W", hW), ("PE", hPE), ("VE", hVE),
                      ("SE", hSE), ("PO", hPO), ("ST", hST),
                      ("C1", hC1), ("C2", hC2), ("C3", hC3), ("LS", hLS)]:
            p.add(nm, h)

        def _pre_sync(s):
            for nm, dst in [("ident", c_ident), ("r2ht", c_r2ht), ("cosfm", c_cos),
                            ("sinfm", c_sin), ("maskB", c_maskB), ("maskA", c_maskA),
                            ("bucket_ones", c_bones), ("ltriT", c_ltriT), ("pmT", c_pmT),
                            ("pmR", c_pmR)]:
                p.inc("LS", s.dma_start(out=dst[:], in_=cons[nm]), 16)
        block.sync(_pre_sync)

        def _pre_g(g):
            g.memset(cst[:, 0:1], 1e-5)
            g.memset(cst[:, 1:2], -20.0)
            g.memset(cst[:, 2:3], 0.0)
            p.inc("PO", g.memset(onesr[:], 1.0))
            p.inc("LD", g.dma_start(
                out=vw.ap().rearrange("p (c j) -> p c j", c=8),
                in_=w_qkv.rearrange("(c p) j -> p c j", p=128)[:, :, 2 * D:3 * D]), 16)
            p.inc("LD", g.dma_start(
                out=wo_b.ap().rearrange("p (c j) -> p c j", c=8),
                in_=w_out.rearrange("(c p) j -> p c j", p=128)), 16)
        block.gpsimd(_pre_g)
        ld_pre = p.val("LD")
        ls_pre = p.val("LS")
        po_pre = p.val("PO")

        for bb in (block.tensor, block.vector, block.scalar, block.gpsimd, block.sync):
            def _init(e, lp=ld_pre, ls=ls_pre, pp_=po_pre):
                p.wait(e, "LD", lp)
                p.wait(e, "LS", ls)
                p.wait(e, "PO", pp_)
            bb(_init)

        def emit_ln(src_ap, dst_ap, sslot, extra_se_wait=None, ld_guard=None,
                    rows=128, extra_ve_wait=None):
            so = 8 * sslot
            bo = 32 + 12 * sslot
            R = rows

            def _v1(v):
                if ld_guard is not None:
                    p.wait(v, "LS", ld_guard)
                if extra_ve_wait is not None:
                    p.wait(v, "VE", extra_ve_wait)
                v.bn_stats(stats[0:R, bo:bo + 6], src_ap[:, 0:512])
                p.inc("VE", v.bn_stats(stats[0:R, bo + 6:bo + 12], src_ap[:, 512:1024]))
                p.wait(v, "VE")
                p.inc("VE", v.bn_aggr(stats[0:R, so:so + 2], stats[0:R, bo:bo + 12]))
            block.vector(_v1)
            ve_mv = p.val("VE")

            def _s1(sc):
                p.wait(sc, "VE", ve_mv)
                p.inc("SE", sc.activation(stats[0:R, so + 4:so + 5], stats[0:R, so + 1:so + 2],
                                          AF.Ln, bias=cst[0:R, 0:1], scale=1.0))
                p.wait(sc, "SE")
                p.inc("SE", sc.activation(stats[0:R, so + 2:so + 3], stats[0:R, so + 4:so + 5],
                                          AF.Exp, bias=cst[0:R, 2:3], scale=-0.5))
            block.scalar(_s1)
            se_rstd = p.val("SE")

            def _v2(v):
                p.wait(v, "SE", se_rstd)
                p.inc("VE", v.scalar_tensor_tensor(
                    out=stats[0:R, so + 3:so + 4], in0=stats[0:R, so:so + 1], scalar=-1.0,
                    in1=stats[0:R, so + 2:so + 3], op0=ALU.mult, op1=ALU.mult))
            block.vector(_v2)
            ve_nm = p.val("VE")

            def _s2(sc):
                p.wait(sc, "VE", ve_nm)
                if extra_se_wait is not None:
                    nm, vv = extra_se_wait
                    p.wait(sc, nm, vv)
                p.inc("SE", sc.activation(dst_ap, src_ap, AF.Identity,
                                          bias=stats[0:R, so + 3:so + 4],
                                          scale=stats[0:R, so + 2:so + 3]))
            block.scalar(_s2)
            return p.val("SE")

        res_ve = {}
        st_cnt = {}
        ln_se_g = {}

        def emit_attn_a():
            for half in range([2, 1][bool(dbg_half0)]):
                base_fr = 8 * half
                ve_half = p.val("VE")

                def _g_vone(g, w=ve_half):
                    p.wait(g, "VE", w)
                    p.inc("PO", g.memset(vaug[:], 1.0))
                block.gpsimd(_g_vone)
                po_vone = p.val("PO")

                evac_a = {}
                evac_b = {}
                t1_pe = {}
                t1b_pe = {}
                for fi in range(12):
                    fr = base_fr + fi
                    buf = fi % 2
                    tb = 0 if buf == 0 else 2048
                    xsrc = xt[:, 1024 * buf:1024 * buf + 1024]
                    xdst = xlt[:, 1024 * buf:1024 * buf + 1024]
                    g_se = ln_se_g.get((half, fi - 2), 0)
                    g_ve = ve_half if fi < 2 else res_ve.get(8 * half + fi - 10, 0)

                    def _ld(s, fr=fr, xsrc=xsrc, g_se=g_se, g_ve=g_ve):
                        p.wait(s, "SE", g_se)
                        p.wait(s, "VE", g_ve)
                        p.inc("LS", s.dma_start(out=xsrc, in_=xh_t[fr]), 16)
                    block.sync(_ld)
                    ln_se_g[(half, fi)] = emit_ln(
                        xsrc, xdst, sslot=buf,
                        extra_se_wait=("PE", t1_pe.get(fi - 2, 0)),
                        ld_guard=p.val("LS"))

                    def _t1(t, xdst=xdst, tb=tb, se=ln_se_g[(half, fi)],
                            g=evac_a.get(fi - 2, 0)):
                        p.wait(t, "SE", se)
                        p.wait(t, "VE", g)
                        for c in range(8):
                            ins = t.matmul(PP[:, tb + 128 * c: tb + 128 * c + 128],
                                           xdst[:, 128 * c:128 * c + 128], c_ident[:],
                                           start=True, stop=True)
                        p.inc("PE", ins)
                    block.tensor(_t1)
                    t1_pe[fi] = p.val("PE")

                    def _v3a(v, tb=tb, fi=fi, pe=t1_pe[fi]):
                        p.wait(v, "PE", pe)
                        dst = xlnf.ap().rearrange("p (c j) -> p c j", c=8)[:, :, 128 * fi:128 * fi + 128]
                        p.inc("VE", v.tensor_copy(
                            dst, PP[:, tb:tb + 1024].rearrange("p (c j) -> p c j", c=8)))
                    block.vector(_v3a)
                    evac_a[fi] = p.val("VE")

                    def _t1b(t, fi=fi, tb=tb, g1=evac_a[fi], g2=evac_b.get(fi - 2, 0)):
                        p.wait(t, "VE", max(g1, g2))
                        for c in range(8):
                            for nn in range(2):
                                ins = t.matmul(
                                    PP[:, tb + 1024 + 512 * nn: tb + 1536 + 512 * nn],
                                    xlnf[:, 1536 * c + 128 * fi: 1536 * c + 128 * fi + 128],
                                    vw[:, 1024 * c + 512 * nn: 1024 * c + 512 * nn + 512],
                                    start=(c == 0), stop=(c == 7))
                        p.inc("PE", ins)
                    block.tensor(_t1b)
                    t1b_pe[fi] = p.val("PE")

                    def _v3b(v, tb=tb, fi=fi, pe=t1b_pe[fi], pv=po_vone):
                        p.wait(v, "PE", pe)
                        p.wait(v, "PO", pv)
                        vdst = vaug.ap().rearrange("p (f h e) -> p f h e", f=12, h=H)[:, fi, :, 0:64]
                        p.inc("VE", v.tensor_copy(
                            vdst, PP[:, tb + 1024:tb + 2048].rearrange("p (h e) -> p h e", h=H)))
                    block.vector(_v3b)
                    evac_b[fi] = p.val("VE")

                ve_a1 = p.val("VE")

                evq_se = {}
                evk_se = {}
                addq_ve = {}
                rot_free_ve = {}
                for pair in range(8):
                    buf = pair % 2
                    wbuf = wqk[:, 2048 * buf:2048 * buf + 2048]

                    def _gw(g, pair=pair, wbuf=wbuf, se_g=evk_se.get(pair - 2, 0)):
                        p.wait(g, "SE", se_g)
                        src_q = w_qkv.rearrange("(c p) j -> p c j", p=128)[:, :, 128 * pair:128 * pair + 128]
                        src_k = w_qkv.rearrange("(c p) j -> p c j", p=128)[:, :, D + 128 * pair:D + 128 * pair + 128]
                        p.inc("W", g.dma_start(out=wbuf.rearrange("p (c j) -> p c j", c=8)[:, :, 0:128], in_=src_q), 16)
                        p.inc("W", g.dma_start(out=wbuf.rearrange("p (c j) -> p c j", c=8)[:, :, 128:256], in_=src_k), 16)
                    block.gpsimd(_gw)
                    w_ld = p.val("W")

                    def _tq(t, wbuf=wbuf, w_ld=w_ld, g1=ve_a1, g2=evq_se.get(pair - 1, 0)):
                        p.wait(t, "W", w_ld)
                        p.wait(t, "VE", g1)
                        p.wait(t, "SE", g2)
                        for c in range(8):
                            for nn in range(2):
                                ins = t.matmul(PP[:, 512 * nn: 512 * nn + 512],
                                               wbuf[:, 256 * c:256 * c + 128],
                                               xlnf[:, 1536 * c + 512 + 512 * nn: 1536 * c + 1024 + 512 * nn],
                                               start=(c == 0), stop=(c == 7))
                        p.inc("PE", ins)
                    block.tensor(_tq)
                    pe_q = p.val("PE")

                    def _sq(sc, pe_q=pe_q):
                        p.wait(sc, "PE", pe_q)
                        p.inc("SE", sc.activation(qkbf[:, 0:1024], PP[:, 0:1024],
                                                  AF.Identity, bias=cst[:, 2:3], scale=1.0))
                    block.scalar(_sq)
                    evq_se[pair] = p.val("SE")

                    def _tk(t, wbuf=wbuf, g=evk_se.get(pair - 1, 0)):
                        p.wait(t, "SE", g)
                        for c in range(8):
                            for nn in range(3):
                                ins = t.matmul(PP[:, 1024 + 512 * nn: 1536 + 512 * nn],
                                               wbuf[:, 256 * c + 128:256 * c + 256],
                                               xlnf[:, 1536 * c + 512 * nn: 1536 * c + 512 * nn + 512],
                                               start=(c == 0), stop=(c == 7))
                        p.inc("PE", ins)
                    block.tensor(_tk)
                    pe_k = p.val("PE")

                    def _sk(sc, pe_k=pe_k):
                        p.wait(sc, "PE", pe_k)
                        p.inc("SE", sc.activation(qkbf[:, 1024:2560], PP[:, 1024:2560],
                                                  AF.Identity, bias=cst[:, 2:3], scale=1.0))
                    block.scalar(_sk)
                    evk_se[pair] = p.val("SE")

                    def _trq(t, g1=evq_se[pair], g2=rot_free_ve.get(pair - 1, 0)):
                        p.wait(t, "SE", g1)
                        p.wait(t, "VE", g2)
                        for nn in range(2):
                            ins = t.matmul(PP[:, 2560 + 512 * nn:3072 + 512 * nn],
                                           c_r2ht[:], qkbf[:, 512 * nn:512 * nn + 512],
                                           start=True, stop=True)
                        p.inc("PE", ins)
                    block.tensor(_trq)
                    pe_rq = p.val("PE")

                    qc0 = HALO + 1024 * half
                    kc0 = 1024 * half

                    def _vq(v, pair=pair, qc0=qc0, g1=evq_se[pair], g2=pe_rq):
                        p.wait(v, "SE", g1)
                        p.inc("VE", v.tensor_tensor(ubuf[:, 0:1024], qkbf[:, 0:1024],
                                                    c_cos[:, qc0:qc0 + 1024], op=ALU.mult))
                        p.wait(v, "PE", g2)
                        p.inc("VE", v.tensor_tensor(vbuf[:, 0:1024], PP[:, 2560:3584],
                                                    c_sin[:, qc0:qc0 + 1024], op=ALU.mult))
                        p.wait(v, "VE")
                        p.inc("VE", v.tensor_tensor(qfm[:, 1024 * pair:1024 * pair + 1024],
                                                    ubuf[:, 0:1024], vbuf[:, 0:1024], op=ALU.add))
                    block.vector(_vq)
                    addq_ve[pair] = p.val("VE")

                    def _trk(t, g1=evk_se[pair], g2=addq_ve[pair]):
                        p.wait(t, "SE", g1)
                        p.wait(t, "VE", g2)
                        for nn in range(3):
                            ins = t.matmul(PP[:, 2560 + 512 * nn:3072 + 512 * nn],
                                           c_r2ht[:], qkbf[:, 1024 + 512 * nn:1536 + 512 * nn],
                                           start=True, stop=True)
                        p.inc("PE", ins)
                    block.tensor(_trk)
                    pe_rk = p.val("PE")

                    def _vk(v, pair=pair, kc0=kc0, g1=pe_rk):
                        p.inc("VE", v.tensor_tensor(ubuf[:, 1024:2560], qkbf[:, 1024:2560],
                                                    c_cos[:, kc0:kc0 + 1536], op=ALU.mult))
                        p.wait(v, "PE", g1)
                        p.inc("VE", v.tensor_tensor(vbuf[:, 0:1536], PP[:, 2560:4096],
                                                    c_sin[:, kc0:kc0 + 1536], op=ALU.mult))
                        p.wait(v, "VE")
                        p.inc("VE", v.tensor_tensor(kfm[:, 1536 * pair:1536 * pair + 1536],
                                                    ubuf[:, 1024:2560], vbuf[:, 0:1536], op=ALU.add))
                    block.vector(_vk)
                    rot_free_ve[pair] = p.val("VE")

                ve_a2 = p.val("VE")

                pending_tail = [None]

                def emit_tail(hh):
                    pair, a = hh // 2, hh % 2
                    rb = 64 * a
                    pe_av = p.val("PE")

                    def _v6(v, pe_av=pe_av):
                        p.wait(v, "PE", pe_av)
                        zv = PP[:, 3072:4096].rearrange("p (t e) -> p t e", e=128)[:, :, 64:65]
                        p.inc("VE", v.reciprocal(stats[:, 16:24], zv.rearrange("p t e -> p (t e)")))
                        p.wait(v, "VE")
                        att = PP[:, 3072:4096].rearrange("p (t e) -> p t e", e=128)[:, :, 0:64]
                        rz = stats[:, 16:24].unsqueeze(2).broadcast_to([128, 8, 64])
                        p.inc("VE", v.tensor_tensor(
                            aotm.ap().rearrange("p (t e) -> p t e", e=64), att, rz, op=ALU.mult))
                    block.vector(_v6)
                    ve_norm = p.val("VE")

                    def _t6(t, rb=rb, g=ve_norm):
                        p.wait(t, "VE", g)
                        for tt in range(8):
                            ins = t.matmul(PP[rb:rb + 64, 3072 + 128 * tt:3072 + 128 * tt + 128],
                                           aotm[:, 64 * tt:64 * tt + 64], c_ident[:],
                                           start=True, stop=True)
                        p.inc("PE", ins)
                    block.tensor(_t6)
                    pe_tr = p.val("PE")

                    def _v7(v, pair=pair, rb=rb, g=pe_tr):
                        p.wait(v, "PE", g)
                        p.inc("VE", v.tensor_copy(aofm[rb:rb + 64, 1024 * pair:1024 * pair + 1024],
                                                  PP[rb:rb + 64, 3072:4096]))
                    block.vector(_v7)
                    return p.val("VE")

                tail_ve = 0
                for hh in range(H):
                    pair, a = hh // 2, hh % 2
                    rb = 64 * a
                    first_grp = True
                    for tlist in ([0, 1, 2], [3, 4, 5], [6, 7]):
                        pe_before = p.val("PE")
                        se_guard = p.val("SE")

                        def _t4(t, pair=pair, rb=rb, tlist=tlist,
                                g1=ve_a2, g2=se_guard):
                            p.wait(t, "VE", g1)
                            p.wait(t, "SE", g2)
                            for kc in range(tlist[0], tlist[-1] + 5):
                                for tt in tlist:
                                    w = kc - tt
                                    if w < 0 or w > 4:
                                        continue
                                    slot = tt % 3
                                    ins = t.matmul(
                                        PP[:, 1024 * slot + 128 * w: 1024 * slot + 128 * w + 128],
                                        kfm[rb:rb + 64, 1536 * pair + 128 * kc: 1536 * pair + 128 * kc + 128],
                                        qfm[rb:rb + 64, 1024 * pair + 128 * tt: 1024 * pair + 128 * tt + 128],
                                        start=True, stop=True)
                                    if w == 4:
                                        p.inc("PE", ins)
                        block.tensor(_t4)

                        if first_grp and pending_tail[0] is not None:
                            tail_ve = emit_tail(pending_tail[0])
                            pending_tail[0] = None
                        first_grp = False

                        def _s4(sc, tlist=tlist, pe_base=pe_before):
                            for i, tt in enumerate(tlist):
                                p.wait(sc, "PE", pe_base + i + 1)
                                slot = tt % 3
                                p.inc("SE", sc.activation(
                                    probs[:, 640 * tt:640 * tt + 640],
                                    PP[:, 1024 * slot:1024 * slot + 640],
                                    AF.Exp, bias=cst[:, 1:2], scale=0.125))
                        block.scalar(_s4)
                        se_base = p.val("SE") - len(tlist)

                        def _v5(v, tlist=tlist, se_base=se_base, half=half):
                            for i, tt in enumerate(tlist):
                                p.wait(v, "SE", se_base + i + 1)
                                if half == 0 and tt < 4:
                                    p.inc("VE", v.tensor_tensor(
                                        probs[:, 640 * tt:640 * tt + 640],
                                        probs[:, 640 * tt:640 * tt + 640],
                                        c_maskA[:, 640 * tt:640 * tt + 640], op=ALU.mult))
                                else:
                                    v0 = probs[:, 640 * tt:640 * tt + 128]
                                    v4 = probs[:, 640 * tt + 512:640 * tt + 640]
                                    v.tensor_tensor(v0, v0, c_maskB[:, 0:128], op=ALU.mult)
                                    p.inc("VE", v.tensor_tensor(v4, v4, c_maskB[:, 128:256], op=ALU.mult))
                        block.vector(_v5)
                        ve_base = p.val("VE")

                        def _t5(t, tlist=tlist, hh=hh, g1=ve_base, g2=tail_ve):
                            p.wait(t, "VE", max(g1, g2))
                            for tt in tlist:
                                for w in range(5):
                                    kc = tt + w
                                    ins = t.matmul(
                                        PP[:, 3072 + 128 * tt:3072 + 128 * tt + 65],
                                        probs[:, 640 * tt + 128 * w:640 * tt + 128 * w + 128],
                                        vaug[:, 1040 * kc + 65 * hh:1040 * kc + 65 * hh + 65],
                                        start=(w == 0), stop=(w == 4))
                            p.inc("PE", ins)
                        block.tensor(_t5)
                    pending_tail[0] = hh
                tail_ve = emit_tail(pending_tail[0])
                pending_tail[0] = None

                ve_a3 = p.val("VE")
                se_a3 = p.val("SE")

                if dbg_dump and half == 0:
                    def _dump(g, g1=ve_a3):
                        p.wait(g, "VE", g1)
                        for nm, src in [("d_xlnf", xlnf), ("d_qfm", qfm), ("d_kfm", kfm),
                                        ("d_vaug", vaug), ("d_probs", probs),
                                        ("d_aofm", aofm), ("d_aotm", aotm)]:
                            p.inc("ST", g.dma_start(out=dbg[nm], in_=src[:]), 16)
                        p.wait(g, "ST")
                    block.gpsimd(_dump)

                for tt in range(8):
                    buf = tt % 2
                    obase = 0 if buf == 0 else 1024
                    gt = 8 * half + tt
                    g_se = max(se_a3, ln_se_g.get((half, 10 + buf), 0))
                    g_ve = res_ve.get(gt - 2, 0)

                    def _ldx(s, gt=gt, buf=buf, g_se=g_se, g_ve=g_ve):
                        p.wait(s, "SE", g_se)
                        p.wait(s, "VE", g_ve)
                        p.inc("LS", s.dma_start(out=xt[:, 1024 * buf:1024 * buf + 1024],
                                                in_=xh_t[4 + gt]), 16)
                    block.sync(_ldx)
                    ld_x = p.val("LS")

                    def _t7(t, tt=tt, obase=obase, g1=ve_a3, g2=res_ve.get(gt - 2, 0)):
                        p.wait(t, "VE", max(g1, g2))
                        for c in range(8):
                            for nn in range(2):
                                ins = t.matmul(PP[:, obase + 512 * nn:obase + 512 * nn + 512],
                                               aofm[:, 1024 * c + 128 * tt:1024 * c + 128 * tt + 128],
                                               wo_b[:, 1024 * c + 512 * nn:1024 * c + 512 * nn + 512],
                                               start=(c == 0), stop=(c == 7))
                        p.inc("PE", ins)
                    block.tensor(_t7)
                    pe_o = p.val("PE")

                    def _v8(v, buf=buf, obase=obase, pe_o=pe_o, ld_x=ld_x,
                            g_st=st_cnt.get(gt - 2, 0)):
                        p.wait(v, "PE", pe_o)
                        p.wait(v, "LS", ld_x)
                        p.wait(v, "ST", g_st)
                        p.inc("VE", v.tensor_tensor(rt[:, 1024 * buf:1024 * buf + 1024],
                                                    PP[:, obase:obase + 1024],
                                                    xt[:, 1024 * buf:1024 * buf + 1024], op=ALU.add))
                    block.vector(_v8)
                    res_ve[gt] = p.val("VE")

                    def _st(g, gt=gt, buf=buf, ve_r=res_ve[gt]):
                        p.wait(g, "VE", ve_r)
                        dst = out_t[gt] if n_sub == 1 else xres_t[gt]
                        p.inc("ST", g.dma_start(out=dst, in_=rt[:, 1024 * buf:1024 * buf + 1024]), 16)
                    block.gpsimd(_st)
                    st_cnt[gt] = p.val("ST")

        emit_attn_a()
        es_a.close()

        def emit_join():
            vals = {nm: p.val(nm) for nm in ["LD", "LS", "W", "PE", "VE", "SE", "PO", "ST"]}
            for bb in (block.tensor, block.vector, block.scalar, block.gpsimd, block.sync):
                def _j(e, vals=vals):
                    for nm, v in vals.items():
                        p.wait(e, nm, v)
                bb(_j)

        def emit_ffn(src_t, dst_t, first, write_out, ccs="C1"):
            emit_join()
            with ExitStack() as ef:
                def SBF(name, shape, dt):
                    return ef.enter_context(nc.sbuf_tensor(name + "_" + ccs, shape, dt))
                xlf = SBF("xlf", [128, 8 * 2048], BF16)
                hfm = SBF("hfm", [128, 4 * 2048], BF16)
                w1jc = SBF("w1jc", [128, 2 * 1024], BF16)
                w2b = SBF("w2b", [128, 4 * 64], BF16)
                alpha = SBF("alpha", [128, 16 * 64], BF16)
                alphaT = SBF("alphaT", [64, 2048], BF16)
                wihT = SBF("wihT", [128, 3072], BF16)
                wraw = SBF("wraw", [128, 2 * 3072], BF16)
                sa = SBF("sa", [64, 6 * 1024], F32)
                sab = SBF("sab", [64, 3712], BF16)
                sfmb = SBF("sfmb", [128, 1536], BF16)
                wsl = SBF("wsl", [128, 8 * 1024], BF16)
                xlnt = SBF("xlnt", [128, 2 * 1024], BF16)
                ffsc = SBF("ffsc", [128, 256], F32)
                sipart = SBF("sipart", [64, 1028], F32)
                sitot = SBF("sitot", [64, 1028], F32)
                brow = SBF("brow", [1, 4096], BF16)
                gh0_sb = SBF("gh0_sb", [64, 3 * D], F32)

                s0f = SBF("s0f", [64, D], F32)
                s0b = SBF("s0b", [64, D], BF16)
                c1sb = SBF("c1sb", [128, 4], F32)
                Gt = SBF("Gt", [64, D], BF16)
                Bt = SBF("Bt", [64, D], BF16)

                SA = dict(t1=0, t2=1024, r=2048, z=3072, n=4096, snew=5120, y=2048)

                def sac(nm):
                    return sa[:, SA[nm]:SA[nm] + 1024]

                if True:
                    def _g0(g):
                        p.inc("LD", g.dma_start(out=s0f[:], in_=ffn_in["slot_init"]), 16)
                        p.inc("LD", g.dma_start(out=s0b[:], in_=ffn_in["slot_init"]), 16)
                        p.inc("LD", g.dma_start(out=brow[0:1, 0:1024], in_=ffn_in["ln_g"]), 16)
                        p.inc("LD", g.dma_start(out=brow[0:1, 1024:2048], in_=ffn_in["ln_b"]), 16)
                        p.inc("LD", g.dma_start(out=ffsc[0:1, 0:1], in_=ffn_in["tau"]), 16)
                        p.inc("LD", g.dma_start(
                            out=wsl.ap().rearrange("p (c j) -> p c j", c=8)[:, :, 0:512],
                            in_=ffn_in["w1"].rearrange("(c p) j -> p c j", p=128)[:, 8:16, :]), 16)
                        p.inc("LD", g.dma_start(out=brow[0:1, 2048:2560], in_=ffn_in["b1"]), 16)
                    block.gpsimd(_g0)
                    ld0 = p.val("LD")

                    def _s0(sc):
                        p.wait(sc, "LD", ld0)
                        p.inc("SE", sc.activation(ffsc[0:1, 1:2], ffsc[0:1, 0:1], AF.Abs,
                                                  bias=cst[0:1, 2:3], scale=1.0))
                    block.scalar(_s0)
                    se0 = p.val("SE")

                    def _v0(v):
                        p.wait(v, "SE", se0)
                        p.inc("VE", v.tensor_scalar(ffsc[0:1, 2:3], ffsc[0:1, 1:2], 0.1, None, op0=ALU.add))
                        p.wait(v, "VE")
                        p.inc("VE", v.reciprocal(ffsc[0:1, 3:4], ffsc[0:1, 2:3]))
                        p.wait(v, "VE")
                        p.inc("VE", v.tensor_copy(brow[0:1, 2560:2561], ffsc[0:1, 3:4]))
                    block.vector(_v0)
                    ve0 = p.val("VE")

                    def _t0(t):
                        p.wait(t, "VE", ve0)
                        t.matmul(PP[:, 0:1], onesr[:], brow[0:1, 2560:2561], start=True, stop=True)
                        for c in range(8):
                            ins = t.matmul(PP[:, 128 + 64 * c:128 + 64 * c + 64],
                                           s0b[:, 128 * c:128 * c + 128],
                                           c_ident[0:64, 0:64], start=True, stop=True)
                        p.inc("PE", ins)
                    block.tensor(_t0)
                    pe0 = p.val("PE")

                    def _v01(v):
                        p.wait(v, "PE", pe0)
                        v.tensor_copy(cst[:, 3:4], PP[:, 0:1])
                        v.tensor_copy(sfmb[:, 512:1024], PP[:, 128:640])
                        p.inc("VE", v.tensor_reduce(
                            stats[:, 24:32], PP[:, 128:640].rearrange("p (c s) -> p c s", s=64),
                            axis=mybir.AxisListType.X, op=ALU.add))
                        p.wait(v, "VE")
                        p.inc("VE", v.tensor_scalar(sfmb[:, 1024:1032], stats[:, 24:32],
                                                    1.0 / 64, None, op0=ALU.mult))
                    block.vector(_v01)
                    ve01 = p.val("VE")

                    def _t01(t):
                        p.wait(t, "VE", ve01)
                        for piece in range(4):
                            for c in range(8):
                                t.matmul(PP[:, 1024 + piece:1024 + piece + 1],
                                         wsl[:, 1024 * c + 128 * piece:1024 * c + 128 * piece + 128],
                                         sfmb[:, 1024 + c:1024 + c + 1],
                                         start=(c == 0), stop=False)
                            t.matmul(PP[:, 1024 + piece:1024 + piece + 1],
                                     brow[0:1, 2048 + 128 * piece:2048 + 128 * piece + 128],
                                     onesr[0:1, 0:1], start=False, stop=True)
                        for piece in range(2):
                            t.matmul(PP[0:64, 2048 + 512 * piece:2560 + 512 * piece],
                                     onesr[0:1, 0:64],
                                     brow[0:1, 512 * piece:512 * piece + 512],
                                     start=True, stop=True)
                            ins = t.matmul(PP[0:64, 3072 + 512 * piece:3584 + 512 * piece],
                                           onesr[0:1, 0:64],
                                           brow[0:1, 1024 + 512 * piece:1536 + 512 * piece],
                                           start=True, stop=True)
                        p.inc("PE", ins)
                    block.tensor(_t01)
                    pe01 = p.val("PE")

                    def _v02(v):
                        p.wait(v, "PE", pe01)
                        v.tensor_copy(c1sb[:], PP[:, 1024:1028])
                        v.tensor_copy(Gt[:], PP[0:64, 2048:3072])
                        p.inc("VE", v.tensor_copy(Bt[:], PP[0:64, 3072:4096]))
                    block.vector(_v02)
                ve_f0 = p.val("VE")

                ln_f = {}
                ev_f = {}
                t1p = {}
                sp_st = {}
                for tt in range(NT):
                    buf = tt % 2
                    tb = 0 if buf == 0 else 1024
                    xsrc = xt[:, 1024 * buf:1024 * buf + 1024]
                    xdst = xlt[:, 1024 * buf:1024 * buf + 1024]

                    def _ld(sy, tt=tt, xsrc=xsrc, g=ln_f.get(tt - 2, 0)):
                        p.wait(sy, "SE", g)
                        p.inc("LS", sy.dma_start(out=xsrc, in_=src_t[tt]), 16)
                    block.sync(_ld)
                    ln_f[tt] = emit_ln(xsrc, xdst, sslot=buf,
                                       extra_se_wait=("PE", t1p.get(tt - 2, 0)),
                                       ld_guard=p.val("LS"))

                    def _sp(g, tt=tt, xdst=xdst, se=ln_f[tt]):
                        p.wait(g, "SE", se)
                        p.inc("ST", g.dma_start(out=xlnsp_t[tt], in_=xdst), 16)
                    block.gpsimd(_sp)
                    sp_st[tt] = p.val("ST")

                    def _t1(t, xdst=xdst, tb=tb, se=ln_f[tt], g=ev_f.get(tt - 2, 0),
                            g2=sp_st[tt]):
                        p.wait(t, "SE", se)
                        p.wait(t, "VE", g)
                        p.wait(t, "ST", g2)
                        for c in range(8):
                            ins = t.matmul(PP[:, tb + 128 * c:tb + 128 * c + 128],
                                           xdst[:, 128 * c:128 * c + 128], c_ident[:],
                                           start=True, stop=True)
                        p.inc("PE", ins)
                    block.tensor(_t1)
                    t1p[tt] = p.val("PE")

                    def _v3(v, tb=tb, tt=tt, pe=t1p[tt]):
                        p.wait(v, "PE", pe)
                        dst = xlf.ap().rearrange("p (c j) -> p c j", c=8)[:, :, 128 * tt:128 * tt + 128]
                        p.inc("VE", v.tensor_copy(
                            dst, PP[:, tb:tb + 1024].rearrange("p (c j) -> p c j", c=8)))
                    block.vector(_v3)
                    ev_f[tt] = p.val("VE")
                ve_f1 = p.val("VE")

                if dbg_ffn:
                    def _dxl(g, g1=ve_f1):
                        p.wait(g, "VE", g1)
                        p.inc("ST", g.dma_start(out=dbg["e_xlf"], in_=xlf[:]), 16)
                        p.inc("ST", g.dma_start(out=dbg["e_xres"].rearrange("(a p) d -> p a d", p=128),
                                                in_=xt.ap().rearrange("p (a d) -> p a d", a=2)), 16)
                        p.wait(g, "ST")
                    block.gpsimd(_dxl)

                def _gw2(g):
                    p.inc("W", g.dma_start(
                        out=w2b.ap().rearrange("p (c j) -> p c j", c=4),
                        in_=ffn_in["w2"].rearrange("(c p) j -> p c j", p=128)), 16)
                    p.inc("W", g.dma_start(out=brow[0:1, 3072:3136], in_=ffn_in["b2"]), 16)
                block.gpsimd(_gw2)
                w2_ld = p.val("W")

                h_se = {}
                for jc in range(4):
                    buf = jc % 2
                    pb = 2048 * buf
                    wb = w1jc[:, 1024 * buf:1024 * buf + 1024]

                    def _gw1(g, jc=jc, wb=wb, guard=h_se.get(jc - 2, 0)):
                        p.wait(g, "SE", guard)
                        p.inc("W", g.dma_start(
                            out=wb.rearrange("p (c j) -> p c j", c=8),
                            in_=ffn_in["w1"].rearrange("(c p) j -> p c j", p=128)[:, 0:8, 128 * jc:128 * jc + 128]), 16)
                    block.gpsimd(_gw1)
                    wld = p.val("W")

                    def _t2(t, wb=wb, wld=wld, pb=pb, g1=ve_f1, g2=h_se.get(jc - 2, 0)):
                        p.wait(t, "W", wld)
                        p.wait(t, "VE", g1)
                        p.wait(t, "SE", g2)
                        for piece in range(4):
                            for c in range(8):
                                ins = t.matmul(
                                    PP[:, pb + 512 * piece:pb + 512 * piece + 512],
                                    wb[:, 128 * c:128 * c + 128],
                                    xlf[:, 2048 * c + 512 * piece:2048 * c + 512 * piece + 512],
                                    start=(c == 0), stop=(c == 7))
                        p.inc("PE", ins)
                    block.tensor(_t2)
                    pe2 = p.val("PE")

                    def _s2(sc, jc=jc, pe2=pe2, pb=pb, g0=ve_f0):
                        p.wait(sc, "PE", pe2)
                        p.wait(sc, "VE", g0)
                        p.inc("SE", sc.activation(hfm[:, 2048 * jc:2048 * jc + 2048],
                                                  PP[:, pb:pb + 2048], AF.Gelu,
                                                  bias=c1sb[:, jc:jc + 1], scale=1.0))
                    block.scalar(_s2)
                    h_se[jc] = p.val("SE")
                se_f2 = p.val("SE")

                al_ve = {}
                sel_ve = {}
                for tt in range(NT):
                    buf = tt % 2
                    rb4 = 2048 if buf == 0 else 2560
                    so2 = 16 + 32 * buf
                    fo = 128 * buf
                    xb = xlnt[:, 1024 * buf:1024 * buf + 1024]

                    def _lx(g, xb=xb, guard=al_ve.get(tt - 2, 0), g2=sp_st[tt]):
                        p.wait(g, "VE", guard)
                        p.wait(g, "ST", g2)
                        p.inc("LD", g.dma_start(out=xb, in_=xlnsp_t[tt]), 16)
                    block.gpsimd(_lx)
                    lx = p.val("LD")

                    def _t3(t, tt=tt, rb4=rb4, g1=se_f2, g2=w2_ld, g3=sel_ve.get(tt - 2, 0)):
                        p.wait(t, "SE", g1)
                        p.wait(t, "W", g2)
                        p.wait(t, "VE", g3)
                        for c in range(4):
                            t.matmul(PP[:, rb4:rb4 + 64],
                                     hfm[:, 2048 * c + 128 * tt:2048 * c + 128 * tt + 128],
                                     w2b[:, 64 * c:64 * c + 64], start=(c == 0), stop=False)
                        ins = t.matmul(PP[:, rb4:rb4 + 64], onesr[:], brow[0:1, 3072:3136],
                                       start=False, stop=True)
                        p.inc("PE", ins)
                    block.tensor(_t3)
                    pe3 = p.val("PE")

                    def _v4(v, rb4=rb4, so2=so2, fo=fo, pe3=pe3):
                        p.wait(v, "PE", pe3)
                        p.inc("VE", v.max(stats[:, so2:so2 + 8], PP[:, rb4:rb4 + 64]))
                        p.wait(v, "VE")
                        v.scalar_tensor_tensor(out=stats[:, so2 + 8:so2 + 9],
                                               in0=stats[:, so2:so2 + 1],
                                               scalar=-1.0, in1=cst[:, 3:4],
                                               op0=ALU.mult, op1=ALU.mult)
                        p.inc("VE", v.tensor_scalar(ffsc[:, fo + 64:fo + 128], PP[:, rb4:rb4 + 64],
                                                    stats[:, so2 + 7:so2 + 8], None, op0=ALU.is_ge))
                    block.vector(_v4)
                    sel_ve[tt] = p.val("VE")
                    ve4 = p.val("VE")

                    def _s3(sc, rb4=rb4, so2=so2, fo=fo, ve4=ve4):
                        p.wait(sc, "VE", ve4)
                        p.inc("SE", sc.activation(ffsc[:, fo:fo + 64], PP[:, rb4:rb4 + 64], AF.Exp,
                                                  bias=stats[:, so2 + 8:so2 + 9], scale=cst[:, 3:4]))
                    block.scalar(_s3)
                    se3 = p.val("SE")

                    def _v5(v, tt=tt, so2=so2, fo=fo, se3=se3):
                        p.wait(v, "SE", se3)
                        p.inc("VE", v.scalar_tensor_tensor(
                            out=ffsc[:, fo:fo + 64], in0=ffsc[:, fo:fo + 64], scalar=1.0,
                            in1=ffsc[:, fo + 64:fo + 128], op0=ALU.mult, op1=ALU.mult,
                            accum_out=stats[:, so2 + 9:so2 + 10]))
                        p.wait(v, "VE")
                        p.inc("VE", v.reciprocal(stats[:, so2 + 10:so2 + 11], stats[:, so2 + 9:so2 + 10]))
                        p.wait(v, "VE")
                        p.inc("VE", v.tensor_scalar(alpha[:, 64 * tt:64 * tt + 64],
                                                    ffsc[:, fo:fo + 64], stats[:, so2 + 10:so2 + 11],
                                                    None, op0=ALU.mult))
                    block.vector(_v5)
                    ve5 = p.val("VE")

                    def _t4(t, tt=tt, ve5=ve5, lx=lx, g3=al_ve.get(tt - 1, 0)):
                        p.wait(t, "VE", max(ve5, g3))
                        p.wait(t, "LD", lx)
                        t.matmul(PP[0:64, 0:128], alpha[:, 64 * tt:64 * tt + 64],
                                 c_ident[:], start=True, stop=True)
                        xb2 = xlnt[:, 1024 * (tt % 2):1024 * (tt % 2) + 1024]
                        for piece in range(2):
                            ins = t.matmul(PP[0:64, 3072 + 512 * piece:3584 + 512 * piece],
                                           alpha[:, 64 * tt:64 * tt + 64],
                                           xb2[:, 512 * piece:512 * piece + 512],
                                           start=(tt == 0), stop=(tt == NT - 1),
                                           skip_group_check=True)
                        p.inc("PE", ins)
                    block.tensor(_t4)
                    pe4 = p.val("PE")

                    def _v6(v, tt=tt, pe4=pe4):
                        p.wait(v, "PE", pe4)
                        p.inc("VE", v.tensor_copy(alphaT[:, 128 * tt:128 * tt + 128],
                                                  PP[0:64, 0:128]))
                    block.vector(_v6)
                    al_ve[tt] = p.val("VE")
                pe_si = p.val("PE")

                def _v7(v, g=pe_si):
                    p.wait(v, "PE", g)
                    v.memset(sipart[:, 1025:1028], 0.0)
                    v.tensor_copy(sipart[:, 0:1024], PP[0:64, 3072:4096])
                    p.inc("VE", v.tensor_reduce(sipart[:, 1024:1025], alphaT[:],
                                                axis=mybir.AxisListType.X, op=ALU.add))
                block.vector(_v7)
                ve7 = p.val("VE")

                def _gcc(g, ve7=ve7):
                    p.wait(g, "VE", ve7)
                    p.inc("ST", g.dma_start(out=cc_in[:, :], in_=sipart[:]), 16)
                    p.wait(g, "ST")
                    g.collective_compute(
                        "AllReduce", ALU.add, replica_groups=[[0, 1, 2, 3], [4, 5, 6, 7]],
                        ins=[cc_in.ap().opt()], outs=[cc_out.ap().opt()]).then_inc(p.sem[ccs], 1)
                    g.wait_ge(p.sem[ccs], 1)
                    p.inc("LD", g.dma_start(out=sitot[:], in_=cc_out[:, :]), 16)
                block.gpsimd(_gcc)
                ld_cc = p.val("LD")

                def _v8(v, g=ld_cc):
                    p.wait(v, "LD", g)
                    p.inc("VE", v.tensor_scalar(stats[0:64, 25:26], sitot[:, 1024:1025],
                                                1e-8, None, op0=ALU.add))
                    p.wait(v, "VE")
                    p.inc("VE", v.reciprocal(stats[0:64, 26:27], stats[0:64, 25:26]))
                    p.wait(v, "VE")
                    p.inc("VE", v.tensor_scalar(sab[:, 0:1024], sitot[:, 0:1024],
                                                stats[0:64, 26:27], None, op0=ALU.mult))
                block.vector(_v8)
                ve8 = p.val("VE")

                def _t5(t, g=ve8):
                    p.wait(t, "VE", g)
                    for c in range(8):
                        ins = t.matmul(PP[:, 64 * c:64 * c + 64], sab[:, 128 * c:128 * c + 128],
                                       c_ident[0:64, 0:64], start=True, stop=True)
                    p.inc("PE", ins)
                block.tensor(_t5)
                pe5 = p.val("PE")

                def _v9(v, g=pe5):
                    p.wait(v, "PE", g)
                    p.inc("VE", v.tensor_copy(sfmb[:, 0:512], PP[:, 0:512]))
                block.vector(_v9)
                ve9 = p.val("VE")

                def emit_gru_mm(weight_ap, bias_ap, lhs_col0, psum_free_guard):
                    tr_ve = {}
                    entry_ve = p.val("VE")
                    for c in range(8):
                        wbuf2 = c % 2
                        wrb = wraw[:, 3072 * wbuf2:3072 * wbuf2 + 3072]

                        def _gw(g, c=c, wrb=wrb, guard=max(tr_ve.get(c - 2, 0), entry_ve)):
                            p.wait(g, "VE", guard)
                            p.inc("W", g.dma_start(
                                out=wrb.rearrange("p (jc j) -> p jc j", jc=24),
                                in_=weight_ap.rearrange("(jc p) d -> p jc d", p=128)[:, :, 128 * c:128 * c + 128]), 16)
                        block.gpsimd(_gw)
                        wld = p.val("W")

                        prev_ev = p.val("VE")
                        for b3 in range(3):
                            def _tt(t, wrb=wrb, b3=b3, wld=wld, g=prev_ev):
                                p.wait(t, "W", wld)
                                p.wait(t, "VE", g)
                                for j in range(8):
                                    ins = t.matmul(PP[:, 3072 + 128 * j:3072 + 128 * j + 128],
                                                   wrb[:, 128 * (8 * b3 + j):128 * (8 * b3 + j) + 128],
                                                   c_ident[:], start=True, stop=True)
                                p.inc("PE", ins)
                            block.tensor(_tt)
                            peb = p.val("PE")

                            def _ve(v, b3=b3, peb=peb):
                                p.wait(v, "PE", peb)
                                p.inc("VE", v.tensor_copy(wihT[:, 1024 * b3:1024 * b3 + 1024],
                                                          PP[:, 3072:4096]))
                            block.vector(_ve)
                            prev_ev = p.val("VE")
                        tr_ve[c] = p.val("VE")

                        def _tg(t, c=c, g=tr_ve[c], g2=psum_free_guard):
                            p.wait(t, "VE", max(g, g2))
                            for piece in range(6):
                                ins = t.matmul(PP[0:64, 512 * piece:512 * piece + 512],
                                               sfmb[:, lhs_col0 + 64 * c:lhs_col0 + 64 * c + 64],
                                               wihT[:, 512 * piece:512 * piece + 512],
                                               start=(c == 0), stop=False,
                                               skip_group_check=True)
                            p.inc("PE", ins)
                        block.tensor(_tg)

                    def _gb(g):
                        p.inc("W", g.dma_start(out=brow[0:1, 0:3072], in_=bias_ap), 16)
                    block.gpsimd(_gb)
                    bld = p.val("W")

                    def _tb(t, g=bld):
                        p.wait(t, "W", g)
                        for piece in range(6):
                            ins = t.matmul(PP[0:64, 512 * piece:512 * piece + 512],
                                           onesr[0:1, 0:64],
                                           brow[0:1, 512 * piece:512 * piece + 512],
                                           start=False, stop=True, skip_group_check=True)
                        p.inc("PE", ins)
                    block.tensor(_tb)
                    return p.val("PE")

                if True:
                    pe_gh = emit_gru_mm(ffn_in["Whh"], ffn_in["bhh"], 512, ve9)

                    def _vgh(v, g=pe_gh):
                        p.wait(v, "PE", g)
                        p.inc("VE", v.tensor_copy(gh0_sb[:], PP[0:64, 0:3072]))
                    block.vector(_vgh)
                gh_done = p.val("VE")

                pe_gi = emit_gru_mm(ffn_in["Wih"], ffn_in["bih"], 0, gh_done)

                def _vg1(v, g=pe_gi):
                    p.wait(v, "PE", g)
                    v.tensor_tensor(sac("t1"), PP[0:64, 0:1024], gh0_sb[:, 0:1024], op=ALU.add)
                    p.inc("VE", v.tensor_tensor(sac("t2"), PP[0:64, 1024:2048],
                                                gh0_sb[:, 1024:2048], op=ALU.add))
                block.vector(_vg1)
                vg1 = p.val("VE")

                def _sg1(sc, g=vg1):
                    p.wait(sc, "VE", g)
                    sc.activation(sac("r"), sac("t1"), AF.Sigmoid, bias=cst[0:64, 2:3], scale=1.0)
                    p.inc("SE", sc.activation(sac("z"), sac("t2"), AF.Sigmoid,
                                              bias=cst[0:64, 2:3], scale=1.0))
                block.scalar(_sg1)
                sg1 = p.val("SE")

                def _vg2(v, g=sg1):
                    p.wait(v, "SE", g)
                    p.inc("VE", v.tensor_tensor(sac("t1"), sac("r"), gh0_sb[:, 2048:3072], op=ALU.mult))
                    p.wait(v, "VE")
                    p.inc("VE", v.tensor_tensor(sac("t2"), sac("t1"), PP[0:64, 2048:3072], op=ALU.add))
                block.vector(_vg2)
                vg2 = p.val("VE")

                def _sg2(sc, g=vg2):
                    p.wait(sc, "VE", g)
                    p.inc("SE", sc.activation(sac("n"), sac("t2"), AF.Tanh,
                                              bias=cst[0:64, 2:3], scale=1.0))
                block.scalar(_sg2)
                sg2 = p.val("SE")

                def _vg3(v, g=sg2):
                    p.wait(v, "SE", g)
                    p.inc("VE", v.tensor_tensor(sac("t1"), s0f[:], sac("n"), op=ALU.subtract))
                    p.wait(v, "VE")
                    p.inc("VE", v.tensor_tensor(sac("t2"), sac("z"), sac("t1"), op=ALU.mult))
                    p.wait(v, "VE")
                    p.inc("VE", v.tensor_tensor(sac("snew"), sac("n"), sac("t2"), op=ALU.add))
                    p.wait(v, "VE")
                    p.inc("VE", v.tensor_copy(sab[:, 1024:2048], sac("snew")))
                block.vector(_vg3)
                vg3 = p.val("VE")

                def _t6(t, g=vg3):
                    p.wait(t, "VE", g)
                    for c in range(8):
                        ins = t.matmul(PP[:, 3072 + 64 * c:3072 + 64 * c + 64],
                                       sab[:, 1024 + 128 * c:1024 + 128 * c + 128],
                                       c_ident[0:64, 0:64], start=True, stop=True)
                    p.inc("PE", ins)
                block.tensor(_t6)
                pe6 = p.val("PE")

                def _v10(v, g=pe6):
                    p.wait(v, "PE", g)
                    p.inc("VE", v.tensor_copy(sfmb[:, 0:512], PP[:, 3072:3584]))
                block.vector(_v10)
                ve10 = p.val("VE")

                def _gqk(g):
                    p.inc("W", g.dma_start(
                        out=wsl.ap().rearrange("p (c j) -> p c j", c=8)[:, :, 0:256],
                        in_=ffn_in["wq"].rearrange("(c p) j -> p c j", p=128)), 16)
                    p.inc("W", g.dma_start(
                        out=wsl.ap().rearrange("p (c j) -> p c j", c=8)[:, :, 256:512],
                        in_=ffn_in["wk"].rearrange("(c p) j -> p c j", p=128)), 16)
                    p.inc("W", g.dma_start(out=brow[0:1, 3136:3392], in_=ffn_in["bq"]), 16)
                    p.inc("W", g.dma_start(out=brow[0:1, 3392:3648], in_=ffn_in["bk"]), 16)
                block.gpsimd(_gqk)
                qk_ld = p.val("W")

                def _t7(t, g1=ve10, g2=qk_ld):
                    p.wait(t, "VE", g1)
                    p.wait(t, "W", g2)
                    for c in range(8):
                        t.matmul(PP[0:64, 0:256], sfmb[:, 64 * c:64 * c + 64],
                                 wsl[:, 1024 * c:1024 * c + 256], start=(c == 0), stop=False)
                    t.matmul(PP[0:64, 0:256], onesr[0:1, 0:64], brow[0:1, 3136:3392],
                             start=False, stop=True)
                    for c in range(8):
                        t.matmul(PP[0:64, 512:768], sfmb[:, 64 * c:64 * c + 64],
                                 wsl[:, 1024 * c + 256:1024 * c + 512], start=(c == 0), stop=False)
                    ins = t.matmul(PP[0:64, 512:768], onesr[0:1, 0:64], brow[0:1, 3392:3648],
                                   start=False, stop=True)
                    p.inc("PE", ins)
                block.tensor(_t7)
                pe7 = p.val("PE")

                def _v11(v, g=pe7):
                    p.wait(v, "PE", g)
                    v.tensor_copy(sab[:, 2048:2304], PP[0:64, 0:256])
                    p.inc("VE", v.tensor_copy(sab[:, 2304:2560], PP[0:64, 512:768]))
                block.vector(_v11)
                ve11 = p.val("VE")

                def _gv(g, g1=pe7):
                    p.wait(g, "PE", g1)
                    p.inc("W", g.dma_start(
                        out=wsl.ap().rearrange("p (c j) -> p c j", c=8)[:, :, 0:1024],
                        in_=ffn_in["wv"].rearrange("(c p) j -> p c j", p=128)), 16)
                    p.inc("W", g.dma_start(out=brow[0:1, 0:1024], in_=ffn_in["bv"]), 16)
                block.gpsimd(_gv)
                v_ld = p.val("W")

                def _t8(t, g1=ve11, g2=v_ld, g3=vg3):
                    p.wait(t, "W", g2)
                    p.wait(t, "VE", max(g1, g3))
                    for piece in range(2):
                        for c in range(8):
                            t.matmul(PP[0:64, 1024 + 512 * piece:1536 + 512 * piece],
                                     sfmb[:, 64 * c:64 * c + 64],
                                     wsl[:, 1024 * c + 512 * piece:1024 * c + 512 * piece + 512],
                                     start=(c == 0), stop=False)
                        t.matmul(PP[0:64, 1024 + 512 * piece:1536 + 512 * piece],
                                 onesr[0:1, 0:64], brow[0:1, 512 * piece:512 * piece + 512],
                                 start=False, stop=True)
                    for c in range(2):
                        t.matmul(PP[:, 3072 + 64 * c:3072 + 64 * c + 64],
                                 sab[:, 2048 + 128 * c:2048 + 128 * c + 128],
                                 c_ident[0:64, 0:64], start=True, stop=True)
                        ins = t.matmul(PP[:, 3200 + 64 * c:3200 + 64 * c + 64],
                                       sab[:, 2304 + 128 * c:2304 + 128 * c + 128],
                                       c_ident[0:64, 0:64], start=True, stop=True)
                    p.inc("PE", ins)
                block.tensor(_t8)
                pe8 = p.val("PE")

                def _v12(v, g=pe8):
                    p.wait(v, "PE", g)
                    v.tensor_copy(sfmb[:, 1040:1168], PP[:, 3072:3200])
                    v.tensor_copy(sfmb[:, 1168:1296], PP[:, 3200:3328])
                    p.inc("VE", v.tensor_copy(sab[:, 2560:3584], PP[0:64, 1024:2048]))
                block.vector(_v12)
                ve12 = p.val("VE")

                def _t9(t, g=ve12):
                    p.wait(t, "VE", g)
                    for c in range(2):
                        ins = t.matmul(PP[0:64, 2048:2112], sfmb[:, 1040 + 64 * c:1040 + 64 * c + 64],
                                       sfmb[:, 1168 + 64 * c:1168 + 64 * c + 64],
                                       start=(c == 0), stop=(c == 1))
                    p.inc("PE", ins)
                block.tensor(_t9)
                pe9 = p.val("PE")

                def _v13(v, g=pe9):
                    p.wait(v, "PE", g)
                    p.inc("VE", v.reduce_max(stats[0:64, 16:17], PP[0:64, 2048:2112],
                                             axis=mybir.AxisListType.X, negate=True))
                    p.wait(v, "VE")
                    p.inc("VE", v.tensor_scalar(stats[0:64, 17:18], stats[0:64, 16:17],
                                                0.0625, None, op0=ALU.mult))
                block.vector(_v13)
                ve13 = p.val("VE")

                def _s4(sc, g=ve13):
                    p.wait(sc, "VE", g)
                    p.inc("SE", sc.activation(ffsc[0:64, 0:64], PP[0:64, 2048:2112], AF.Exp,
                                              bias=stats[0:64, 17:18], scale=0.0625,
                                              accum_out=stats[0:64, 18:19]))
                block.scalar(_s4)
                se4 = p.val("SE")

                def _v14(v, g=se4):
                    p.wait(v, "SE", g)
                    p.inc("VE", v.reciprocal(stats[0:64, 19:20], stats[0:64, 18:19]))
                    p.wait(v, "VE")
                    p.inc("VE", v.tensor_scalar(sab[:, 3584:3648], ffsc[0:64, 0:64],
                                                stats[0:64, 19:20], None, op0=ALU.mult))
                block.vector(_v14)
                ve14 = p.val("VE")

                def _t10(t, g=ve14):
                    p.wait(t, "VE", g)
                    p.inc("PE", t.matmul(PP[0:64, 3072:3136], sab[:, 3584:3648],
                                         c_ident[0:64, 0:64], start=True, stop=True))
                block.tensor(_t10)
                pe10 = p.val("PE")

                def _v15(v, g=pe10):
                    p.wait(v, "PE", g)
                    p.inc("VE", v.tensor_copy(sab[:, 3648:3712], PP[0:64, 3072:3136]))
                block.vector(_v15)
                ve15 = p.val("VE")

                def _t11(t, g=ve15):
                    p.wait(t, "VE", g)
                    for piece in range(2):
                        ins = t.matmul(PP[0:64, 2048 + 512 * piece:2560 + 512 * piece],
                                       sab[:, 3648:3712],
                                       sab[:, 2560 + 512 * piece:3072 + 512 * piece],
                                       start=True, stop=True)
                    p.inc("PE", ins)
                block.tensor(_t11)
                pe11 = p.val("PE")

                def _v16(v, g=pe11):
                    p.wait(v, "PE", g)
                    p.inc("VE", v.tensor_tensor(sac("y"), sac("snew"), PP[0:64, 2048:3072], op=ALU.add))
                block.vector(_v16)
                ve16 = p.val("VE")
                ln_y = emit_ln(sac("y"), sac("t2"), sslot=2, rows=64,
                               extra_ve_wait=ve16)

                def _v17(v, g=ln_y):
                    p.wait(v, "SE", g)
                    p.inc("VE", v.tensor_tensor(sac("t1"), sac("t2"), Gt[:], op=ALU.mult))
                    p.wait(v, "VE")
                    p.inc("VE", v.tensor_tensor(sac("t2"), sac("t1"), Bt[:], op=ALU.add))
                    p.wait(v, "VE")
                    p.inc("VE", v.tensor_copy(sab[:, 2048:3072], sac("t2")))
                block.vector(_v17)
                ve17 = p.val("VE")

                def _t12(t, g=ve17):
                    p.wait(t, "VE", g)
                    for c in range(8):
                        ins = t.matmul(PP[:, 3072 + 64 * c:3072 + 64 * c + 64],
                                       sab[:, 2048 + 128 * c:2048 + 128 * c + 128],
                                       c_ident[0:64, 0:64], start=True, stop=True)
                    p.inc("PE", ins)
                block.tensor(_t12)
                pe12 = p.val("PE")

                def _v18(v, g=pe12):
                    p.wait(v, "PE", g)
                    p.inc("VE", v.tensor_copy(sfmb[:, 0:512], PP[:, 3072:3584]))
                block.vector(_v18)
                ve18 = p.val("VE")

                def _gvp(g, g1=pe8):
                    p.wait(g, "PE", g1)
                    p.inc("W", g.dma_start(
                        out=wsl.ap().rearrange("p (c j) -> p c j", c=8)[:, :, 0:1024],
                        in_=ffn_in["wvp"].rearrange("(c p) j -> p c j", p=128)), 16)
                    p.inc("W", g.dma_start(out=brow[0:1, 0:1024], in_=ffn_in["bvp"]), 16)
                block.gpsimd(_gvp)
                vp_ld = p.val("W")

                def _t13(t, g1=ve18, g2=vp_ld):
                    p.wait(t, "VE", g1)
                    p.wait(t, "W", g2)
                    for piece in range(2):
                        for c in range(8):
                            t.matmul(PP[0:64, 512 * piece:512 * piece + 512],
                                     sfmb[:, 64 * c:64 * c + 64],
                                     wsl[:, 1024 * c + 512 * piece:1024 * c + 512 * piece + 512],
                                     start=(c == 0), stop=False)
                        ins = t.matmul(PP[0:64, 512 * piece:512 * piece + 512],
                                       onesr[0:1, 0:64], brow[0:1, 512 * piece:512 * piece + 512],
                                       start=False, stop=True)
                    p.inc("PE", ins)
                block.tensor(_t13)
                pe13 = p.val("PE")

                def _v19(v, g=pe13):
                    p.wait(v, "PE", g)
                    p.inc("VE", v.tensor_copy(sab[:, 0:1024], PP[0:64, 0:1024]))
                block.vector(_v19)
                ve19 = p.val("VE")

                if dbg_ffn:
                    def _dmp(g, g1=ve19):
                        p.wait(g, "VE", g1)
                        for nm, src in [("e_alpha", alpha[:]), ("e_sitot", sitot[:]),
                                        ("e_gh0", gh0_sb[:]), ("e_snew", sab[:, 1024:2048]),
                                        ("e_sv", sab[:, 0:1024]), ("e_c1", c1sb[:]),
                                        ("e_hfm", hfm[:]),
                                        ("e_y", sac("y")), ("e_alphaT", alphaT[:])]:
                            p.inc("ST", g.dma_start(out=dbg[nm], in_=src), 16)
                        p.wait(g, "ST")
                    block.gpsimd(_dmp)

                of_ve = {}
                for c in range(8):
                    def _t14(t, c=c, g1=ve19, g2=of_ve.get(c - 1, 0)):
                        p.wait(t, "VE", max(g1, g2))
                        for piece in range(4):
                            ins = t.matmul(PP[:, 512 * piece:512 * piece + 512],
                                           sab[:, 128 * c:128 * c + 128],
                                           alphaT[:, 512 * piece:512 * piece + 512],
                                           start=True, stop=True)
                        p.inc("PE", ins)
                    block.tensor(_t14)
                    pe14 = p.val("PE")

                    def _v20(v, c=c, g=pe14):
                        p.wait(v, "PE", g)
                        p.inc("VE", v.tensor_copy(xlf[:, 2048 * c:2048 * c + 2048], PP[:, 0:2048]))
                    block.vector(_v20)
                    of_ve[c] = p.val("VE")
                ve_f6 = p.val("VE")

                def _gwo(g, g1=pe13):
                    p.wait(g, "PE", g1)
                    p.inc("W", g.dma_start(
                        out=wsl.ap().rearrange("p (c j) -> p c j", c=8)[:, :, 0:1024],
                        in_=ffn_in["wo"].rearrange("(c p) j -> p c j", p=128)), 16)
                    p.inc("W", g.dma_start(out=brow[0:1, 0:1024], in_=ffn_in["bo"]), 16)
                block.gpsimd(_gwo)
                wo_ld = p.val("W")

                res_f = {}
                st_f = {}
                se_f1_last = max(ln_f.values())
                for tt in range(NT):
                    buf = tt % 2
                    obase = 2048 if buf == 0 else 3072

                    def _ldx(sy, tt=tt, buf=buf, g=res_f.get(tt - 2, 0), g2=se_f1_last):
                        p.wait(sy, "VE", g)
                        p.wait(sy, "SE", g2)
                        p.inc("LS", sy.dma_start(out=xt[:, 1024 * buf:1024 * buf + 1024],
                                                 in_=src_t[tt]), 16)
                    block.sync(_ldx)
                    ld_x = p.val("LS")

                    def _t15(t, tt=tt, obase=obase, g1=ve_f6, g2=wo_ld, g3=res_f.get(tt - 2, 0)):
                        p.wait(t, "VE", max(g1, g3))
                        p.wait(t, "W", g2)
                        for piece in range(2):
                            for c in range(8):
                                t.matmul(PP[:, obase + 512 * piece:obase + 512 * piece + 512],
                                         xlf[:, 2048 * c + 128 * tt:2048 * c + 128 * tt + 128],
                                         wsl[:, 1024 * c + 512 * piece:1024 * c + 512 * piece + 512],
                                         start=(c == 0), stop=False)
                            ins = t.matmul(PP[:, obase + 512 * piece:obase + 512 * piece + 512],
                                           onesr[:], brow[0:1, 512 * piece:512 * piece + 512],
                                           start=False, stop=True)
                        p.inc("PE", ins)
                    block.tensor(_t15)
                    pe15 = p.val("PE")

                    def _v21(v, tt=tt, buf=buf, obase=obase, g1=pe15, g2=ld_x,
                             g3=st_f.get(tt - 2, 0)):
                        p.wait(v, "PE", g1)
                        p.wait(v, "LS", g2)
                        p.wait(v, "ST", g3)
                        p.inc("VE", v.tensor_tensor(rt[:, 1024 * buf:1024 * buf + 1024],
                                                    PP[:, obase:obase + 1024],
                                                    xt[:, 1024 * buf:1024 * buf + 1024],
                                                    op=ALU.add))
                    block.vector(_v21)
                    res_f[tt] = p.val("VE")

                    def _st(g, tt=tt, buf=buf, ve_r=res_f[tt]):
                        p.wait(g, "VE", ve_r)
                        dst = out_t[tt] if write_out else dst_t[tt]
                        p.inc("ST", g.dma_start(out=dst, in_=rt[:, 1024 * buf:1024 * buf + 1024]), 16)
                    block.gpsimd(_st)
                    st_f[tt] = p.val("ST")

        def emit_attn_b(src_t, dst_t, write_out):
            emit_join()
            with ExitStack() as eb:
                def SBB(name, shape, dt):
                    return eb.enter_context(nc.sbuf_tensor(name, shape, dt))
                xlf = SBB("bxlf", [128, 8 * 2048], BF16)
                wqg = SBB("bwqg", [128, 8 * 1024], BF16)
                wqg2 = SBB("bwqg2", [128, 8 * 1024], BF16)
                wraw2 = SBB("bwraw", [128, 8 * 1024], BF16)
                ctxs = SBB("bctx", [128, 8 * 2048], BF16)
                qbuf = SBB("bqbuf", [128, 2 * 2048], BF16)
                qtb = SBB("bqtb", [128, 2 * 1024], BF16)
                ktile = SBB("bkt", [128, 1024], BF16)
                vte = SBB("bvte", [128, 1024], BF16)
                vto = SBB("bvto", [128, 1024], BF16)
                ffq32 = SBB("bffq32", [128, 1024], F32)
                ffq2 = SBB("bffq2", [128, 1024], F32)
                ffqb = SBB("bffqb", [128, 1024], BF16)
                Ctot = SBB("bCtot", [128, 512], F32)
                cof4 = SBB("bcof4", [128, 4 * 512], F32)
                Coff = SBB("bCoff", [128, 512], F32)
                Crun = SBB("bCrun", [128, 512], F32)
                Caug = SBB("bCaug", [128, 2 * 260], BF16)
                ksb = SBB("bksb", [32, 1024], BF16)
                kone = SBB("bkone", [32, 1], BF16)
                ktall = SBB("bktall", [4, 1024], BF16)
                ktrow = SBB("bktrow", [1, 1024], F32)
                koffb = SBB("bkoff", [1, 1024], BF16)
                kcb = SBB("bkcb", [32, 1024], BF16)
                kcT = SBB("bkcT", [128, 256], F32)
                mrb = SBB("bmrb", [128, 4], F32)
                xot = SBB("bxot", [128, 2 * 128], BF16)

                def wq_group(g_idx, dstw, src_ap):
                    pe_g = p.val("PE")

                    def _gl(g, g_idx=g_idx, pg=pe_g):
                        p.wait(g, "PE", pg)
                        p.inc("W", g.dma_start(
                            out=wraw2.ap().rearrange("p (c j) -> p c j", c=8),
                            in_=src_ap.rearrange("(c p) j -> p c j", p=128)[:, 8 * g_idx:8 * g_idx + 8, :]), 16)
                    block.gpsimd(_gl)
                    wld = p.val("W")
                    prev = p.val("VE")
                    for b8 in range(8):
                        def _tt(t, b8=b8, wld=wld, g=prev):
                            p.wait(t, "W", wld)
                            p.wait(t, "VE", g)
                            for j in range(8):
                                ins = t.matmul(PP[:, 3072 + 128 * j:3072 + 128 * j + 128],
                                               wraw2[:, 1024 * b8 + 128 * j:1024 * b8 + 128 * j + 128],
                                               c_ident[:], start=True, stop=True)
                            p.inc("PE", ins)
                        block.tensor(_tt)
                        peb = p.val("PE")

                        def _ve(v, b8=b8, peb=peb, dstw=dstw):
                            p.wait(v, "PE", peb)
                            dst = dstw.ap().rearrange("p (c j) -> p c j", c=8)[:, :, 128 * b8:128 * b8 + 128]
                            p.inc("VE", v.tensor_copy(
                                dst, PP[:, 3072:4096].rearrange("p (c j) -> p c j", c=8)))
                        block.vector(_ve)
                        prev = p.val("VE")
                    return p.val("VE")

                def _gm1(g):
                    g.memset(vte[:], 0.0)
                    g.memset(vto[:], 0.0)
                    g.memset(Caug[:], 0.0)
                    p.inc("PO", g.memset(kone[:], 1.0))
                block.gpsimd(_gm1)
                po_m1 = p.val("PO")

                ln_b = {}
                ev_b = {}
                t1pb = {}
                for tt in range(NT):
                    buf = tt % 2
                    tb = 0 if buf == 0 else 1024
                    xsrc = xt[:, 1024 * buf:1024 * buf + 1024]
                    xdst = xlt[:, 1024 * buf:1024 * buf + 1024]

                    def _ld(sy, tt=tt, xsrc=xsrc, g=ln_b.get(tt - 2, 0)):
                        p.wait(sy, "SE", g)
                        p.inc("LS", sy.dma_start(out=xsrc, in_=src_t[tt]), 16)
                    block.sync(_ld)
                    ln_b[tt] = emit_ln(xsrc, xdst, sslot=buf,
                                       extra_se_wait=("PE", t1pb.get(tt - 2, 0)),
                                       ld_guard=p.val("LS"))

                    def _t1(t, xdst=xdst, tb=tb, se=ln_b[tt], g=ev_b.get(tt - 2, 0)):
                        p.wait(t, "SE", se)
                        p.wait(t, "VE", g)
                        for c in range(8):
                            ins = t.matmul(PP[:, tb + 128 * c:tb + 128 * c + 128],
                                           xdst[:, 128 * c:128 * c + 128], c_ident[:],
                                           start=True, stop=True)
                        p.inc("PE", ins)
                    block.tensor(_t1)
                    t1pb[tt] = p.val("PE")

                    def _v3(v, tb=tb, tt=tt, pe=t1pb[tt]):
                        p.wait(v, "PE", pe)
                        dst = xlf.ap().rearrange("p (c j) -> p c j", c=8)[:, :, 128 * tt:128 * tt + 128]
                        p.inc("VE", v.tensor_copy(
                            dst, PP[:, tb:tb + 1024].rearrange("p (c j) -> p c j", c=8)))
                    block.vector(_v3)
                    ev_b[tt] = p.val("VE")
                ve_b1 = p.val("VE")
                if b_stop == "B1":
                    return

                ve_wq = wq_group(0, wqg, wqkv_b)

                q_se = {}
                q_st = {}
                for tt in range(NT):
                    def _tq(t, tt=tt, g1=ve_b1, g2=ve_wq, g3=q_se.get(tt - 1, 0)):
                        p.wait(t, "VE", max(g1, g2))
                        p.wait(t, "SE", g3)
                        for c in range(8):
                            for nn in range(2):
                                ins = t.matmul(PP[:, 2048 + 512 * nn:2560 + 512 * nn],
                                               xlf[:, 2048 * c + 128 * tt:2048 * c + 128 * tt + 128],
                                               wqg[:, 1024 * c + 512 * nn:1024 * c + 512 * nn + 512],
                                               start=(c == 0), stop=(c == 7))
                        p.inc("PE", ins)
                    block.tensor(_tq)
                    peq = p.val("PE")

                    def _vq(v, peq=peq):
                        p.wait(v, "PE", peq)
                        qv = PP[:, 2048:3072].rearrange("p (h e) -> p h e", h=H)
                        p.inc("VE", v.tensor_reduce(stats[:, 16:32], qv,
                                                    axis=mybir.AxisListType.X, op=ALU.max))
                        p.wait(v, "VE")
                        mb = stats[:, 16:32].unsqueeze(2).broadcast_to([128, 16, 64])
                        p.inc("VE", v.tensor_tensor(
                            ffq32.ap().rearrange("p (h e) -> p h e", h=H), qv, mb, op=ALU.subtract))
                    block.vector(_vq)
                    veq = p.val("VE")

                    def _sq(sc, veq=veq):
                        p.wait(sc, "VE", veq)
                        p.inc("SE", sc.activation(ffq2[:], ffq32[:], AF.Exp,
                                                  bias=cst[:, 2:3], scale=1.0))
                    block.scalar(_sq)
                    q_se[tt] = p.val("SE")

                    def _vq2(v, g=q_se[tt]):
                        p.wait(v, "SE", g)
                        ev = ffq2.ap().rearrange("p (h e) -> p h e", h=H)
                        p.inc("VE", v.tensor_reduce(stats[:, 16:32], ev,
                                                    axis=mybir.AxisListType.X, op=ALU.add))
                        p.wait(v, "VE")
                        p.inc("VE", v.reciprocal(stats[:, 32:48], stats[:, 16:32]))
                        p.wait(v, "VE")
                        p.inc("VE", v.tensor_scalar(stats[:, 32:48], stats[:, 32:48],
                                                    0.125, None, op0=ALU.mult))
                        p.wait(v, "VE")
                        rb2 = stats[:, 32:48].unsqueeze(2).broadcast_to([128, 16, 64])
                        p.inc("VE", v.tensor_tensor(ffqb.ap().rearrange("p (h e) -> p h e", h=H),
                                                    ev, rb2, op=ALU.mult))
                    block.vector(_vq2)
                    veq2 = p.val("VE")

                    def _tq2(t, g=veq2):
                        p.wait(t, "VE", g)
                        for c in range(8):
                            ins = t.matmul(PP[:, 3072 + 128 * c:3072 + 128 * c + 128],
                                           ffqb[:, 128 * c:128 * c + 128], c_ident[:],
                                           start=True, stop=True)
                        p.inc("PE", ins)
                    block.tensor(_tq2)
                    peq2 = p.val("PE")

                    def _vq3(v, tt=tt, peq2=peq2, g=q_st.get(tt - 2, 0)):
                        p.wait(v, "PE", peq2)
                        p.wait(v, "ST", g)
                        p.inc("VE", v.tensor_copy(qtb[:, 1024 * (tt % 2):1024 * (tt % 2) + 1024],
                                                  PP[:, 3072:4096]))
                    block.vector(_vq3)
                    veq3 = p.val("VE")

                    def _gsp(g, tt=tt, veq3=veq3):
                        p.wait(g, "VE", veq3)
                        src = qtb[:, 1024 * (tt % 2):1024 * (tt % 2) + 1024].rearrange(
                            "p (c j) -> p c j", c=8)
                        dst = qsp.ap().rearrange("a p (t j) -> p a t j", t=16)[:, :, tt, :]
                        p.inc("ST", g.dma_start(out=dst, in_=src), 16)
                    block.gpsimd(_gsp)
                    q_st[tt] = p.val("ST")
                se_bq = p.val("SE")
                st_bq = p.val("ST")
                if b_stop == "B1b":
                    return

                ve_wk = wq_group(1, wqg, wqkv_b)
                ve_wv = wq_group(2, wqg2, wqkv_b)
                if b_stop == "B2w":
                    return
                kv_se = {}
                kv_ve = {}
                vto_ve = {}
                pe_k_cell = [0]
                for tt in range(NT):
                    def _tkv(t, tt=tt, g1=ve_wk, g2=ve_wv, g3=kv_se.get(tt - 1, 0)):
                        p.wait(t, "VE", max(g1, g2))
                        p.wait(t, "SE", g3)
                        for c in range(8):
                            for nn in range(2):
                                ins = t.matmul(PP[:, 2048 + 512 * nn:2560 + 512 * nn],
                                               xlf[:, 2048 * c + 128 * tt:2048 * c + 128 * tt + 128],
                                               wqg[:, 1024 * c + 512 * nn:1024 * c + 512 * nn + 512],
                                               start=(c == 0), stop=(c == 7))
                        p.inc("PE", ins)
                        pe_k_cell[0] = p.val("PE")
                        for c in range(8):
                            for nn in range(2):
                                ins = t.matmul(PP[:, 3072 + 512 * nn:3584 + 512 * nn],
                                               xlf[:, 2048 * c + 128 * tt:2048 * c + 128 * tt + 128],
                                               wqg2[:, 1024 * c + 512 * nn:1024 * c + 512 * nn + 512],
                                               start=(c == 0), stop=(c == 7))
                        p.inc("PE", ins)
                    block.tensor(_tkv)
                    pekv = p.val("PE")
                    if b_stop == "B2y":
                        kv_se[tt] = p.val("SE")
                        kv_ve[tt] = p.val("VE")
                        continue

                    def _skv(sc, pekv=pekv, pek=pe_k_cell[0], g=kv_ve.get(tt - 1, 0), g2=po_m1):
                        p.wait(sc, "PE", pek)
                        p.wait(sc, "VE", g)
                        p.wait(sc, "PO", g2)
                        sc.activation(ktile[:], PP[:, 2048:3072], AF.Exp,
                                      bias=cst[:, 2:3], scale=1.0)
                        p.wait(sc, "PE", pekv)
                        p.inc("SE", sc.activation(vte[0:64, :], PP[0:64, 3072:4096],
                                                  AF.Identity, bias=cst[0:64, 2:3], scale=1.0))
                    block.scalar(_skv)
                    kv_se[tt] = p.val("SE")

                    def _vkv(v, pekv=pekv):
                        p.wait(v, "PE", pekv)
                        p.inc("VE", v.tensor_copy(vto[64:128, :], PP[64:128, 3072:4096]))
                    block.vector(_vkv)
                    vto_ve[tt] = p.val("VE")
                    if b_stop == "B2x":
                        kv_ve[tt] = p.val("VE")
                        continue

                    def _tks(t, tt=tt, g=kv_se[tt], gv=vto_ve[tt]):
                        p.wait(t, "SE", g)
                        p.wait(t, "VE", gv)
                        if b_stop != "B2t2":
                            for nn in range(2):
                                t.matmul(PP[0:32, 512 * nn:512 * nn + 512],
                                         c_bones[:, 32 * tt:32 * tt + 32],
                                         ktile[:, 512 * nn:512 * nn + 512],
                                         start=(tt == 0), stop=(tt == NT - 1),
                                         skip_group_check=True)
                        if b_stop == "B2t1":
                            p.inc("PE", t.matmul(PP[0:64, 1024:1088], ktile[0:64, 0:64],
                                                 vtile[0:64, 0:64], start=True, stop=True))
                            return
                        for hh in range(H):
                            hp, a = hh // 2, hh % 2
                            for par in range(2):
                                vsrc = vte if par == 0 else vto
                                ins = t.matmul(
                                    PP[64 * a:64 * a + 64,
                                       1024 + 128 * hp + 64 * par:1024 + 128 * hp + 64 * par + 64],
                                    ktile[:, 64 * hh:64 * hh + 64],
                                    vsrc[:, 64 * hh:64 * hh + 64],
                                    start=True, stop=True)
                        p.inc("PE", ins)
                    block.tensor(_tks)
                    peks = p.val("PE")
                    if b_stop in ("B2t1", "B2t2"):
                        kv_ve[tt] = p.val("VE")
                        continue

                    def _vks(v, tt=tt, peks=peks):
                        p.wait(v, "PE", peks)
                        src = PP[:, 1024:2048].rearrange("p (hp x e) -> p hp x e", hp=8, x=2)
                        dst = ctxs.ap().rearrange("p (hp b e) -> p hp b e", hp=8, b=32)[:, :, 2 * tt:2 * tt + 2, :]
                        p.inc("VE", v.tensor_copy(dst, src))
                        for par in range(2):
                            p.wait(v, "VE")
                            if tt == 0 and par == 0:
                                p.inc("VE", v.tensor_copy(
                                    Ctot.ap().rearrange("p (hp e) -> p hp e", hp=8),
                                    src[:, :, 0, :]))
                            else:
                                p.inc("VE", v.scalar_tensor_tensor(
                                    out=Ctot.ap().rearrange("p (hp e) -> p hp e", hp=8),
                                    in0=src[:, :, par, :],
                                    scalar=1.0,
                                    in1=Ctot.ap().rearrange("p (hp e) -> p hp e", hp=8),
                                    op0=ALU.mult, op1=ALU.add))
                    block.vector(_vks)
                    kv_ve[tt] = p.val("VE")
                pe_b2 = p.val("PE")
                ve_b2 = p.val("VE")
                if b_stop in ("B2", "B2x", "B2y", "B2t1", "B2t2"):
                    return

                def _vkt(v, g=pe_b2):
                    p.wait(v, "PE", g)
                    p.inc("VE", v.tensor_copy(ksb[:], PP[0:32, 0:1024]))
                block.vector(_vkt)
                ve_ksb = p.val("VE")

                def _tkt(t, g1=ve_ksb, g2=po_m1):
                    p.wait(t, "VE", g1)
                    p.wait(t, "PO", g2)
                    for nn in range(2):
                        ins = t.matmul(PP[0:1, 2048 + 512 * nn:2560 + 512 * nn],
                                       kone[:], ksb[:, 512 * nn:512 * nn + 512],
                                       start=True, stop=True)
                    p.inc("PE", ins)
                block.tensor(_tkt)
                pe_kt = p.val("PE")

                def _vkt2(v, g1=pe_kt, g2=ve_b2):
                    p.wait(v, "PE", g1)
                    p.wait(v, "VE", g2)
                    p.inc("VE", v.tensor_copy(ktrow[:], PP[0:1, 2048:3072]))
                block.vector(_vkt2)
                ve_ag = p.val("VE")

                def _gag(g, g1=ve_ag):
                    p.wait(g, "VE", g1)
                    p.inc("ST", g.dma_start(out=ag2_in[0:128, :], in_=Ctot[:]), 16)
                    p.inc("ST", g.dma_start(
                        out=ag2_in.rearrange("r c -> (r c)").unsqueeze(0)[:, 128 * 512:130 * 512],
                        in_=ktrow[:]), 16)
                    p.wait(g, "ST")
                    g.collective_compute(
                        "AllGather", ALU.bypass, replica_groups=[[0, 1, 2, 3], [4, 5, 6, 7]],
                        ins=[ag2_in.ap().opt()], outs=[ag2_out.ap().opt()]).then_inc(p.sem["C2"], 1)
                    g.wait_ge(p.sem["C2"], 1)
                    for r in range(4):
                        p.inc("LD", g.dma_start(
                            out=cof4[:, 512 * r:512 * r + 512],
                            in_=ag2_out.rearrange("(r x) c -> r x c", r=4)[r, 0:128, :]), 16)
                    p.inc("LD", g.dma_start(
                        out=ktall[:],
                        in_=ag2_out.rearrange("(r x) c -> r (x c)", r=4)[:, 128 * 512:130 * 512]), 16)
                block.gpsimd(_gag)
                ld_ag = p.val("LD")

                def _tmr(t, g=ld_ag):
                    p.wait(t, "LD", g)
                    p.inc("PE", t.matmul(PP[:, 2048:2052], onesr[:], c_pmR[:],
                                         start=True, stop=True))
                block.tensor(_tmr)
                pe_mr = p.val("PE")

                def _vmr(v, g=pe_mr):
                    p.wait(v, "PE", g)
                    v.memset(Coff[:], 0.0)
                    p.inc("VE", v.tensor_copy(mrb[:], PP[:, 2048:2052]))
                    for r in range(4):
                        p.wait(v, "VE")
                        p.inc("VE", v.scalar_tensor_tensor(
                            out=Coff[:], in0=cof4[:, 512 * r:512 * r + 512],
                            scalar=mrb[:, r:r + 1], in1=Coff[:],
                            op0=ALU.mult, op1=ALU.add))
                    p.wait(v, "VE")
                    p.inc("VE", v.tensor_copy(Crun[:], Coff[:]))
                block.vector(_vmr)
                ve_co = p.val("VE")

                def _tko(t, g=ve_co):
                    p.wait(t, "VE", g)
                    for nn in range(2):
                        ins = t.matmul(PP[0:1, 2048 + 512 * nn:2560 + 512 * nn],
                                       c_pmT[:], ktall[:, 512 * nn:512 * nn + 512],
                                       start=True, stop=True)
                    p.inc("PE", ins)
                block.tensor(_tko)
                pe_ko = p.val("PE")

                def _vko(v, g=pe_ko):
                    p.wait(v, "PE", g)
                    p.inc("VE", v.tensor_copy(koffb[:], PP[0:1, 2048:3072]))
                block.vector(_vko)
                ve_ko = p.val("VE")

                def _tkc(t, g=ve_ko):
                    p.wait(t, "VE", g)
                    for nn in range(2):
                        t.matmul(PP[0:32, 512 * nn:512 * nn + 512], c_ltriT[:],
                                 ksb[:, 512 * nn:512 * nn + 512], start=True, stop=False)
                        ins = t.matmul(PP[0:32, 512 * nn:512 * nn + 512], onesr[0:1, 0:32],
                                       koffb[0:1, 512 * nn:512 * nn + 512],
                                       start=False, stop=True)
                    p.inc("PE", ins)
                block.tensor(_tkc)
                pe_kc = p.val("PE")

                def _vkc(v, g=pe_kc):
                    p.wait(v, "PE", g)
                    p.inc("VE", v.tensor_copy(kcb[:], PP[0:32, 0:1024]))
                block.vector(_vkc)
                ve_kcb = p.val("VE")

                def _tkc2(t, g=ve_kcb):
                    p.wait(t, "VE", g)
                    for c in range(8):
                        ins = t.matmul(PP[:, 3072 + 32 * c:3072 + 32 * c + 32],
                                       kcb[:, 128 * c:128 * c + 128], c_ident[0:32, 0:32],
                                       start=True, stop=True)
                    p.inc("PE", ins)
                block.tensor(_tkc2)
                pe_kc2 = p.val("PE")

                def _vkc2(v, g=pe_kc2):
                    p.wait(v, "PE", g)
                    p.inc("VE", v.tensor_copy(kcT[:], PP[:, 3072:3328]))
                block.vector(_vkc2)
                ve_kcT = p.val("VE")
                if b_stop == "B2b":
                    return

                qld = {}
                xo_ve = {}
                pe_mm = {}
                for hp in range(8):
                    bufh = hp % 2

                    def _gq(g, hp=hp, bufh=bufh, g1=xo_ve.get((hp - 2, NT - 1), 0),
                            g2=st_bq):
                        p.wait(g, "VE", g1)
                        p.wait(g, "ST", g2)
                        p.inc("LD", g.dma_start(out=qbuf[:, 2048 * bufh:2048 * bufh + 2048],
                                                in_=qsp.ap()[hp]), 16)
                    block.gpsimd(_gq)
                    qld[hp] = p.val("LD")

                    for tt in range(NT):
                        sl = 260 * (tt % 2)
                        abase = 2048 if tt % 2 == 0 else 2560
                        xbase = 3072 if tt % 2 == 0 else 3584

                        def _vca(v, hp=hp, tt=tt, sl=sl, g1=max(ve_kcT, ve_co),
                                 gpe=pe_mm.get((hp, tt - 2), pe_mm.get((hp - 1, NT + tt - 2), 0)),
                                 gpo=po_m1):
                            p.wait(v, "VE", g1)
                            p.wait(v, "PE", gpe)
                            p.wait(v, "PO", gpo)
                            hc = slice(64 * hp, 64 * hp + 64)
                            v.tensor_copy(Caug[0:64, sl + 0:sl + 64], Crun[0:64, hc])
                            v.tensor_copy(Caug[64:128, sl + 65:sl + 129], Crun[64:128, hc])
                            v.tensor_copy(Caug[0:64, sl + 64:sl + 65],
                                          kcT[0:64, 32 * hp + 2 * tt:32 * hp + 2 * tt + 1])
                            p.inc("PO", v.tensor_copy(
                                Caug[64:128, sl + 129:sl + 130],
                                kcT[64:128, 32 * hp + 2 * tt:32 * hp + 2 * tt + 1]))
                            p.wait(v, "PO")
                            p.inc("PO", v.tensor_tensor(
                                Crun[:, hc],
                                ctxs[:, 2048 * hp + 128 * tt:2048 * hp + 128 * tt + 64],
                                Crun[:, hc], op=ALU.add))
                            p.wait(v, "PO")
                            v.tensor_copy(Caug[0:64, sl + 130:sl + 194], Crun[0:64, hc])
                            v.tensor_copy(Caug[64:128, sl + 195:sl + 259], Crun[64:128, hc])
                            v.tensor_copy(Caug[0:64, sl + 194:sl + 195],
                                          kcT[0:64, 32 * hp + 2 * tt + 1:32 * hp + 2 * tt + 2])
                            p.inc("PO", v.tensor_copy(
                                Caug[64:128, sl + 259:sl + 260],
                                kcT[64:128, 32 * hp + 2 * tt + 1:32 * hp + 2 * tt + 2]))
                            p.wait(v, "PO")
                            p.inc("PO", v.tensor_tensor(
                                Crun[:, hc],
                                ctxs[:, 2048 * hp + 128 * tt + 64:2048 * hp + 128 * tt + 128],
                                Crun[:, hc], op=ALU.add))
                        block.gpsimd(_vca)
                        vca = p.val("PO")

                        def _tmm(t, hp=hp, tt=tt, sl=sl, abase=abase, g1=vca,
                                 g2=qld[hp], bufh=bufh, g3=xo_ve.get((hp, tt - 2), 0)):
                            p.wait(t, "PO", g1)
                            p.wait(t, "VE", g3)
                            p.wait(t, "LD", g2)
                            qs = qbuf[:, 2048 * bufh + 128 * tt:2048 * bufh + 128 * tt + 64]
                            qs2 = qbuf[:, 2048 * bufh + 128 * tt + 64:2048 * bufh + 128 * tt + 128]
                            for a in range(2):
                                t.matmul(PP[0:64, abase + 65 * a:abase + 65 * a + 65],
                                         qs, Caug[:, sl + 65 * a:sl + 65 * a + 65],
                                         start=True, stop=True)
                            for a in range(2):
                                ins = t.matmul(PP[64:128, abase + 65 * a:abase + 65 * a + 65],
                                               qs2, Caug[:, sl + 130 + 65 * a:sl + 130 + 65 * a + 65],
                                               start=True, stop=True)
                            p.inc("PE", ins)
                        block.tensor(_tmm)
                        pe_mm[(hp, tt)] = p.val("PE")

                        s0 = 16 + 4 * (tt % 2)

                        def _vda(v, tt=tt, abase=abase, s0=s0, g=pe_mm[(hp, tt)]):
                            p.wait(v, "PE", g)
                            den = PP[:, abase:abase + 130].rearrange("p (a e) -> p a e", a=2)[:, :, 64:65]
                            p.inc("VE", v.tensor_scalar(stats[:, s0:s0 + 2],
                                                        den.rearrange("p a e -> p (a e)"),
                                                        1e-6, None, op0=ALU.max))
                            p.wait(v, "VE")
                            p.inc("VE", v.reciprocal(stats[:, s0 + 2:s0 + 4], stats[:, s0:s0 + 2]))
                        block.vector(_vda)
                        vda = p.val("VE")

                        def _sda(sc, tt=tt, abase=abase, s0=s0, g=vda):
                            p.wait(sc, "VE", g)
                            for a in range(2):
                                ins = sc.activation(
                                    xot[:, 128 * (tt % 2) + 64 * a:128 * (tt % 2) + 64 * a + 64],
                                    PP[:, abase + 65 * a:abase + 65 * a + 64], AF.Identity,
                                    bias=cst[:, 2:3], scale=stats[:, s0 + 2 + a:s0 + 3 + a])
                            p.inc("SE", ins)
                        block.scalar(_sda)
                        sda = p.val("SE")

                        def _txo(t, tt=tt, xbase=xbase, g=sda):
                            p.wait(t, "SE", g)
                            p.inc("PE", t.matmul(PP[:, xbase:xbase + 128],
                                                 xot[:, 128 * (tt % 2):128 * (tt % 2) + 128],
                                                 c_ident[:], start=True, stop=True))
                        block.tensor(_txo)
                        ptx = p.val("PE")

                        def _vxo(v, hp=hp, tt=tt, xbase=xbase, g=ptx):
                            p.wait(v, "PE", g)
                            p.inc("VE", v.tensor_copy(
                                xlf[:, 2048 * hp + 128 * tt:2048 * hp + 128 * tt + 128],
                                PP[:, xbase:xbase + 128]))
                        block.vector(_vxo)
                        xo_ve[(hp, tt)] = p.val("VE")
                ve_b3 = p.val("VE")
                if b_stop == "B3":
                    return

                ve_wo = wq_group(0, wqg, wout_b)

                res_b = {}
                st_b = {}
                for tt in range(NT):
                    buf = tt % 2
                    obase = 0 if buf == 0 else 1024

                    def _ldx(sy, tt=tt, buf=buf, g=res_b.get(tt - 2, 0), g2=max(ln_b.values())):
                        p.wait(sy, "VE", g)
                        p.wait(sy, "SE", g2)
                        p.inc("LS", sy.dma_start(out=xt[:, 1024 * buf:1024 * buf + 1024],
                                                 in_=src_t[tt]), 16)
                    block.sync(_ldx)
                    ld_x = p.val("LS")

                    def _t15(t, tt=tt, obase=obase, g1=max(ve_wo, ve_b3), g2=res_b.get(tt - 2, 0)):
                        p.wait(t, "VE", max(g1, g2))
                        for c in range(8):
                            for nn in range(2):
                                ins = t.matmul(PP[:, obase + 512 * nn:obase + 512 * nn + 512],
                                               xlf[:, 2048 * c + 128 * tt:2048 * c + 128 * tt + 128],
                                               wqg[:, 1024 * c + 512 * nn:1024 * c + 512 * nn + 512],
                                               start=(c == 0), stop=(c == 7))
                        p.inc("PE", ins)
                    block.tensor(_t15)
                    pe15 = p.val("PE")

                    def _v21(v, buf=buf, obase=obase, g1=pe15, g2=ld_x,
                             g3=st_b.get(tt - 2, 0)):
                        p.wait(v, "PE", g1)
                        p.wait(v, "LS", g2)
                        p.wait(v, "ST", g3)
                        p.inc("VE", v.tensor_tensor(rt[:, 1024 * buf:1024 * buf + 1024],
                                                    PP[:, obase:obase + 1024],
                                                    xt[:, 1024 * buf:1024 * buf + 1024],
                                                    op=ALU.add))
                    block.vector(_v21)
                    res_b[tt] = p.val("VE")

                    def _st(g, tt=tt, buf=buf, ve_r=res_b[tt]):
                        p.wait(g, "VE", ve_r)
                        dst = out_t[tt] if write_out else dst_t[tt]
                        p.inc("ST", g.dma_start(out=dst, in_=rt[:, 1024 * buf:1024 * buf + 1024]), 16)
                    block.gpsimd(_st)
                    st_b[tt] = p.val("ST")

        if n_sub >= 2:
            emit_ffn(xres_t, xres_t, first=True, write_out=(n_sub == 2))
        if n_sub >= 3:
            emit_attn_b(xres_t, xres_t, write_out=(n_sub == 3))
        if n_sub >= 4:
            emit_ffn(xres_t, xres_t, first=True, write_out=True, ccs="C3")

        def _fin(g):
            p.wait(g, "ST")
        block.gpsimd(_fin)

    return nc


_NC_CACHE = {}


def kernel(**inputs):
    x = np.ascontiguousarray(inputs["L0_attn_a_x"], dtype=np.float32)
    n_sub = 4
    nc = build_nc(n_sub=n_sub)
    in_maps = []
    for c in range(8):
        b, qt = c // 4, c % 4
        s0 = qt * TC
        xhh = np.zeros((FR, D), np.float32)
        lo = max(0, s0 - HALO)
        xhh[HALO - (s0 - lo):] = x[b, lo:s0 + TC]
        m = dict(xh=xhh,
                 w_qkv=np.ascontiguousarray(inputs["L0_attn_a_w_qkv"], dtype=np.float32),
                 w_out=np.ascontiguousarray(inputs["L0_attn_a_w_out"], dtype=np.float32))
        m["wqkv_b"] = np.ascontiguousarray(inputs["L0_attn_b_W_qkv"], dtype=np.float32)
        m["wout_b"] = np.ascontiguousarray(inputs["L0_attn_b_W_out"], dtype=np.float32)
        for nm in ["slot_init", "w1", "b1", "w2", "b2", "tau", "Wih", "Whh", "bih",
                   "bhh", "wq", "bq", "wk", "bk", "wv", "bv", "ln_g", "ln_b",
                   "wvp", "bvp", "wo", "bo"]:
            a = np.ascontiguousarray(inputs["L0_ffn_" + nm], dtype=np.float32)
            if a.ndim == 1:
                a = a.reshape(1, -1)
            m["f_" + nm] = a
        m.update(host_consts(qt))
        in_maps.append(m)
    res = run_bass_kernel_spmd(nc, in_maps, core_ids=list(range(8)))
    outs = [res.results[c]["out"] for c in range(8)]
    y = np.stack([np.concatenate(outs[0:4], axis=0),
                  np.concatenate(outs[4:8], axis=0)], axis=0)
    return y
